# version 12
# baseline (speedup 1.0000x reference)
import os
import numpy as np
import ml_dtypes
from contextlib import ExitStack

import concourse.bass as bass
import concourse.mybir as mybir
from concourse.bass_utils import run_bass_kernel_spmd

F32 = mybir.dt.float32
BF16 = mybir.dt.bfloat16
AF = mybir.ActivationFunctionType
ALU = mybir.AluOpType
AX = mybir.AxisListType

NCORES = 8
B = 2
N = 8192
D = 1024
T = 2048
WCOL = T + 1
NH = 8
DH = 64
INNER = 512
WINDOW = 512
HID = 2730
NCA = 22
HIDP = NCA * 128
NEGBIG = -1.0e30
BF16NP = ml_dtypes.bfloat16

ENGINES = ("T", "V", "S", "G", "Q")



class Emitter:

    def __init__(self, nc):
        self.nc = nc
        self.ops = {e: [] for e in ENGINES}
        self.last_w = {}
        self.readers = {}
        self.pending_bar = {e: None for e in ENGINES}
        self.prolog = {}

    def _is_async(self, eng, idx):
        return self.ops[eng][idx]["kind"] in ("d", "cc")

    def _dep(self, eng, reads, writes):
        waits = {}

        def add(se, si):
            if se == eng and not self._is_async(se, si):
                self.ops[se][si]["same_dep"] = True
            if si > waits.get(se, -1):
                waits[se] = si

        for k in reads:
            lw = self.last_w.get(k)
            if lw is not None:
                add(*lw)
        for k in writes:
            lw = self.last_w.get(k)
            if lw is not None:
                add(*lw)
            for se, si in self.readers.get(k, ()):
                add(se, si)
        if self.pending_bar[eng] is not None:
            for se, si in self.pending_bar[eng].items():
                add(se, si)
            self.pending_bar[eng] = None
        return waits

    def _record(self, eng, fn, reads, writes, kind, tag):
        waits = self._dep(eng, reads, writes)
        idx = len(self.ops[eng])
        self.ops[eng].append(dict(fn=fn, waits=waits, inc=False, kind=kind,
                                  tag=tag, own_sem=None, same_dep=False))
        for k in writes:
            self.last_w[k] = (eng, idx)
            self.readers[k] = []
        for k in reads:
            self.readers.setdefault(k, []).append((eng, idx))
        return (eng, idx)

    def op(self, eng, fn, reads=(), writes=(), tag=""):
        return self._record(eng, fn, reads, writes, "c", tag)

    def dma(self, fn, reads=(), writes=(), tag=""):
        return self._record("Q", fn, reads, writes, "d", tag)

    def coll(self, fn, reads=(), writes=(), tag=""):
        return self._record("G", fn, reads, writes, "cc", tag)

    def barrier(self):
        snap = {e: len(self.ops[e]) - 1 for e in ENGINES if self.ops[e]}
        for e in ENGINES:
            pb = {se: si for se, si in snap.items() if se != e}
            self.pending_bar[e] = pb

    def finalize(self, es, final_out_keys=()):
        nc = self.nc
        for e in ENGINES:
            for o in self.ops[e]:
                if o["same_dep"] and o["kind"] == "c":
                    o["inc"] = True
                for se, si in o["waits"].items():
                    self.ops[se][si]["inc"] = True
        for k in final_out_keys:
            lw = self.last_w.get(k)
            if lw is not None:
                self.ops[lw[0]][lw[1]]["inc"] = True
        for o in self.ops["Q"]:
            o["inc"] = True
        ncc = 0
        for e in ENGINES:
            c = 0
            for o in self.ops[e]:
                if o["kind"] == "cc":
                    o["cum"] = None
                    if o["inc"]:
                        o["own_sem"] = es.enter_context(nc.semaphore(f"cc{ncc}"))
                        ncc += 1
                else:
                    if o["inc"]:
                        c += 16 if o["kind"] == "d" else 1
                    o["cum"] = c
        sems = {e: es.enter_context(nc.semaphore(f"sem_{e}")) for e in ENGINES}
        self.sems = sems

        out_fence = []
        for k in final_out_keys:
            lw = self.last_w.get(k)
            if lw is not None:
                se, si = lw
                out_fence.append((se, self.ops[se][si]["cum"]))

        def make_body(e):
            def body(eng):
                if e in self.prolog:
                    self.prolog[e](eng)
                hwm = {}
                ccseen = set()
                for o in self.ops[e]:
                    for se in sorted(o["waits"]):
                        si = o["waits"][se]
                        src = self.ops[se][si]
                        if src["kind"] == "cc":
                            if (se, si) not in ccseen:
                                eng.wait_ge(src["own_sem"], 1)
                                ccseen.add((se, si))
                        else:
                            tgt = src["cum"]
                            if tgt > hwm.get(se, 0):
                                eng.wait_ge(sems[se], tgt)
                                hwm[se] = tgt
                    inst = o["fn"](eng)
                    if o["inc"]:
                        if o["kind"] == "cc":
                            inst.then_inc(o["own_sem"], 1)
                        else:
                            inst.then_inc(sems[e], 16 if o["kind"] == "d" else 1)
                if e == "Q":
                    for se, tgt in out_fence:
                        eng.wait_ge(sems[se], tgt)
            return body

        blk = es.enter_context(nc.Block())
        blk.sync(make_body("Q"))
        blk.tensor(make_body("T"))
        blk.vector(make_body("V"))
        blk.scalar(make_body("S"))
        blk.gpsimd(make_body("G"))

    def n_instructions(self):
        return sum(len(v) for v in self.ops.values())


def bc_free(tile, reps, offset=0, pcnt=128):
    a = tile.ap()
    return bass.AP(a.tensor, offset, [[a.ap[0][0], pcnt], [0, reps]])


def bc_free3(tile, offset, reps, inner, pcnt=128):
    a = tile.ap()
    return bass.AP(a.tensor, offset, [[a.ap[0][0], pcnt], [0, reps], [1, inner]])


def view3(ap, offset, step, count, inner, pcnt=128):
    return bass.AP(ap.tensor, offset, [[ap.ap[0][0], pcnt], [step, count], [1, inner]])



class Prog:
    def __init__(self, nc, es, taps, stop):
        self.nc, self.es = nc, es
        self.em = Emitter(nc)
        self.taps = set(taps)
        self.stop = stop
        self.tap_names = []
        self._par = 0

    def par(self):
        self._par += 1
        return self._par % 2

    def declare(self):
        nc, es = self.nc, self.es
        f = lambda name, shape: nc.dram_tensor(name, shape, F32, kind="ExternalInput").ap()
        self.in_x = f("x", [T, D])
        self.in_xh = f("xh", [1, D])
        self.in_rotc = f("rotc", [T, 128])
        self.in_gate = f("gate", [128, 1])
        self.in_maskp = nc.dram_tensor("maskp", [128, 512], BF16, kind="ExternalInput").ap()
        self.in_W = {w: f(w, [D, D]) for w in ("Wq", "Wk", "Wv", "Wo")}
        self.in_b = {w: f(w, [8, 128]) for w in ("bq", "bk", "bv", "bo")}
        self.in_wqkv = [f(f"wqkv{l}", [D, 3 * INNER]) for l in range(2)]
        self.in_qs = [f(f"qs{l}", [1, DH]) for l in range(2)]
        self.in_ks = [f(f"ks{l}", [1, DH]) for l in range(2)]
        self.in_wof = [f(f"wof{l}", [INNER, D]) for l in range(2)]
        self.in_lg = [f(f"lg{l}", [8, 128]) for l in range(2)]
        self.in_lb = [f(f"lb{l}", [8, 128]) for l in range(2)]
        self.in_w1a = [f(f"w1a{l}", [D, HIDP]) for l in range(2)]
        self.in_w1g = [f(f"w1g{l}", [D, HIDP]) for l in range(2)]
        self.in_b1a = [f(f"b1a{l}", [NCA, 128]) for l in range(2)]
        self.in_b1g = [f(f"b1g{l}", [NCA, 128]) for l in range(2)]
        self.in_w2 = [f(f"w2{l}", [HIDP, D]) for l in range(2)]
        self.in_b2 = [f(f"b2{l}", [8, 128]) for l in range(2)]
        self.out = nc.dram_tensor("out", [T, D], F32, kind="ExternalOutput").ap()

        self.d_xres1 = nc.dram_tensor("d_xres1", [8, 128, T], F32).ap()
        self.d_ags_in = nc.dram_tensor("d_ags_in", [2, D], F32).ap()
        self.d_ags_out = nc.dram_tensor("d_ags_out", [8, D], F32).ap()
        self.d_agkv_in = [nc.dram_tensor(f"d_agkv_in{l}", [1024, 512], BF16).ap() for l in range(2)]
        self.d_agkv_out = [nc.dram_tensor(f"d_agkv_out{l}", [4096, 512], BF16).ap() for l in range(2)]
        self.d_agx2_in = nc.dram_tensor("d_agx2_in", [D, 1], F32).ap()
        self.d_agx2_out = nc.dram_tensor("d_agx2_out", [4 * D, 1], F32).ap()
        self.d_agxb_in = [nc.dram_tensor(f"d_agxb_in{l}", [D, 1], F32).ap() for l in range(2)]
        self.d_agxb_out = [nc.dram_tensor(f"d_agxb_out{l}", [4 * D, 1], F32).ap() for l in range(2)]
        self.d_lh = nc.dram_tensor("d_lh", [1, D], BF16).ap()

        identf = nc.inline_tensor(np.eye(128, dtype=np.float32), "c_identf")
        identb = nc.inline_tensor(np.eye(128, dtype=BF16NP).view(np.uint16), "c_identb")
        ii, jj = np.arange(128)[:, None], np.arange(128)[None, :]
        msku = np.where(ii < jj, np.float32(NEGBIG), np.float32(0)).astype(np.float32)
        mskl = np.where(ii > jj, np.float32(NEGBIG), np.float32(0)).astype(np.float32)
        masku = nc.inline_tensor(msku, "c_masku")
        maskl = nc.inline_tensor(mskl, "c_maskl")
        maskub = nc.inline_tensor(msku.astype(BF16NP).view(np.uint16), "c_maskub")
        masklb = nc.inline_tensor(mskl.astype(BF16NP).view(np.uint16), "c_masklb")
        onesb = nc.inline_tensor(np.ones((128, 1), dtype=BF16NP).view(np.uint16), "c_onesb")
        onesrb = nc.inline_tensor(np.ones((1, 128), dtype=BF16NP).view(np.uint16), "c_onesrb")
        onesf = nc.inline_tensor(np.ones((1, 128), np.float32), "c_onesf")
        self.ACT_CONSTS = [1e-5, 1e-6, -1.0, -0.5, 8.0, 1.0 / D, 1.0, 0.0, 2.0, 1e-24]
        cval = nc.inline_tensor(np.broadcast_to(
            np.array(self.ACT_CONSTS, np.float32), (128, len(self.ACT_CONSTS))).copy(), "c_cval")
        self.c = dict(identf=identf.ap(), identb=identb.ap().bitcast(BF16),
                      masku=masku.ap(), maskl=maskl.ap(),
                      maskub=maskub.ap().bitcast(BF16), masklb=masklb.ap().bitcast(BF16),
                      onesb=onesb.ap().bitcast(BF16), onesrb=onesrb.ap().bitcast(BF16),
                      onesf=onesf.ap(), cval=cval.ap())

        sb = lambda name, shape, dt: es.enter_context(nc.sbuf_tensor(name, shape, dt))
        self.x = sb("x_fm", [128, 8 * WCOL], F32)
        self.hbuf = sb("hbuf", [128, 8 * WCOL], BF16)
        self.identf = sb("identf", [128, 128], F32)
        self.identb = sb("identb", [128, 128], BF16)
        self.masku = sb("masku", [128, 128], BF16)
        self.maskl = sb("maskl", [128, 128], BF16)
        self.maskpb = sb("maskpb", [128, 512], BF16)
        self.gate = sb("gate_s", [128, 1], F32)
        self.onesb = sb("onesb", [128, 1], BF16)
        self.onesrb = sb("onesrb", [1, 128], BF16)
        self.rowhi = sb("rowhi", [1, 512], BF16)
        self.rowlo = sb("rowlo", [1, 512], BF16)
        self.rowtmp = sb("rowtmp", [1, 512], F32)
        if self.taps:
            self._lndbg = sb("lndbg", [128, 8], F32)
            self._pbdbg = sb("pbdbg", [128, 144], BF16)
        self.cval = sb("cval_s", [128, len(self.ACT_CONSTS)], F32)
        for i, v in enumerate(self.ACT_CONSTS):
            try:
                nc.const_aps.register(F32, v, self.cval[:, i:i + 1])
            except Exception:
                pass
        self.bqt = sb("bqt", [128, 8], F32)
        self.nbqt = sb("nbqt", [128, 8], F32)
        self.bkt = sb("bkt", [128, 8], F32)
        self.nbkt = sb("nbkt", [128, 8], F32)
        self.bvt = sb("bvt", [128, 8], F32)
        self.bot = sb("bot", [128, 8], F32)
        self.lgt = [sb(f"lgt{l}", [128, 8], F32) for l in range(2)]
        self.lbt = [sb(f"lbt{l}", [128, 8], F32) for l in range(2)]
        self.b1at = [sb(f"b1at{l}", [128, NCA], F32) for l in range(2)]
        self.b1gt = [sb(f"b1gt{l}", [128, NCA], F32) for l in range(2)]
        self.b2t = [sb(f"b2t{l}", [128, 8], F32) for l in range(2)]
        self.kvtot = sb("kvtot", [128, 8], F32)
        self.kstot = sb("kstot", [128, 8], F32)
        self.qsrep = sb("qsrep", [128, DH], F32)
        self.ksrep = sb("ksrep", [128, DH], F32)
        self.strow = [sb(f"strow{i}", [1, 512], F32) for i in range(4)]
        self.pp = [sb(f"pp{i}", [128, 8], F32) for i in range(4)]
        self.lncast = sb("lncast", [128, 512], BF16)
        self.lnsq = sb("lnsq", [128, 512], BF16)
        self.lnt1 = sb("lnt1", [128, 512], F32)

        ps = lambda name, shape, dt: es.enter_context(nc.psum_tensor(name, shape, dt))
        self.pAB = ps("pAB", [128, 1024], F32)
        self.pCD = ps("pCD", [128, 1024], F32)
        self.pA = self.pAB.ap()[:, 0:512]
        self.pB = self.pAB.ap()[:, 512:1024]
        self.pC = self.pCD.ap()[:, 0:512]
        self.pD = self.pCD.ap()[:, 512:1024]
        self.pE = ps("pE", [128, 512], F32).ap()
        self.pF = ps("pF", [128, 512], F32).ap()
        self.pT1 = ps("pT1", [128, 1024], BF16)
        self.pT2 = ps("pT2", [128, 1024], BF16)
        self.PS = {"pA": self.pA, "pB": self.pB, "pC": self.pC, "pD": self.pD,
                   "pE": self.pE, "pF": self.pF, "pT1": self.pT1, "pT2": self.pT2,
                   "pABfull": self.pAB.ap(), "pCDfull": self.pCD.ap()}

        self.rt = {}

        def qprolog(eng):
            pid = eng.partition_id()
            self.rt["rr"] = (pid + 3) % 4
        self.em.prolog["Q"] = qprolog

    def xc(self, c, col0, w):
        return self.x[:, c * WCOL + col0: c * WCOL + col0 + w]

    def hc(self, c, col0, w):
        return self.hbuf[:, c * WCOL + col0: c * WCOL + col0 + w]

    def xkeys(self, c, col0, w=None):
        ks = [f"x{c}"]
        if col0 == 0:
            ks.append(f"xh{c}")
        return ks

    def hkeys(self, c, col0, w=None):
        ks = [f"h{c}"]
        if col0 == 0:
            ks.append(f"hh{c}")
        return ks

    def tap_sbuf(self, name, sbt, dt, keys):
        if name not in self.taps:
            return
        nc, em = self.nc, self.em
        out = nc.dram_tensor("tap_" + name, [sbt.shape[0], sbt.shape[1]], dt,
                             kind="ExternalOutput").ap()
        em.dma(lambda q: q.dma_start(out=out, in_=sbt[:]), reads=list(keys),
               writes=["tap_" + name], tag=f"tap_{name}")
        self.tap_names.append("tap_" + name)

    def load_consts(self):
        em = self.em
        for nm in ("identf", "identb", "onesb", "onesrb"):
            dst = getattr(self, nm)
            em.dma(lambda q, dst=dst, nm=nm: q.dma_start(out=dst[:], in_=self.c[nm]),
                   writes=[nm], tag=f"c_{nm}")
        em.dma(lambda q: q.dma_start(out=self.masku[:], in_=self.c["maskub"]), writes=["masku"])
        em.dma(lambda q: q.dma_start(out=self.maskl[:], in_=self.c["masklb"]), writes=["maskl"])
        em.dma(lambda q: q.dma_start(out=self.maskpb[:], in_=self.in_maskp), writes=["maskpb"])
        em.dma(lambda q: q.dma_start(out=self.cval[:], in_=self.c["cval"]), writes=["cval"])
        em.dma(lambda q: q.dma_start(out=self.gate[:], in_=self.in_gate), writes=["gate"])

        self.em.barrier()

    def load_biases(self):
        em = self.em

        def colload(dst, src, ncol, key):
            for c in range(ncol):
                self.em.dma(lambda q, c=c, dst=dst, src=src: q.dma_start(
                    out=dst[:, c:c + 1], in_=src[c:c + 1, :].rearrange("a p -> p a")),
                    writes=[key], tag=f"{key}{c}")

        colload(self.bqt, self.in_b["bq"], 8, "bqt")
        colload(self.bkt, self.in_b["bk"], 8, "bkt")
        colload(self.bvt, self.in_b["bv"], 8, "bvt")
        colload(self.bot, self.in_b["bo"], 8, "bot")
        for l in range(2):
            colload(self.lgt[l], self.in_lg[l], 8, f"lgt{l}")
            colload(self.lbt[l], self.in_lb[l], 8, f"lbt{l}")
            colload(self.b2t[l], self.in_b2[l], 8, f"b2t{l}")
            colload(self.b1at[l], self.in_b1a[l], NCA, f"b1at{l}")
            colload(self.b1gt[l], self.in_b1g[l], NCA, f"b1gt{l}")
        em.op("V", lambda v: v.tensor_single_scalar(self.nbqt[:], self.bqt[:], -1.0, ALU.mult),
              reads=["bqt"], writes=["nbqt"])
        em.op("V", lambda v: v.tensor_single_scalar(self.nbkt[:], self.bkt[:], -1.0, ALU.mult),
              reads=["bkt"], writes=["nbkt"])
        em.dma(lambda q: q.dma_start(out=self.cval[:], in_=self.c["cval"]), writes=["cval"])

    def emit_replicate(self, row, rowk, ps, psk, wdt, tag, scale=None):
        em = self.em
        src = row
        if scale is not None:
            em.op("S", lambda s: s.activation(self.rowtmp[0:1, :wdt], row[0:1, :wdt],
                                              AF.Copy, scale=scale),
                  reads=[rowk], writes=["rowtmp"], tag=tag + "sc")
            src = self.rowtmp
            rowk = "rowtmp"
        em.op("S", lambda s: s.copy(self.rowhi[0:1, :wdt], src[0:1, :wdt]),
              reads=[rowk], writes=["rowhi"], tag=tag + "hi")
        em.op("V", lambda v: v.tensor_tensor(self.rowtmp[0:1, :wdt], src[0:1, :wdt],
                                             self.rowhi[0:1, :wdt], ALU.subtract),
              reads=[rowk, "rowhi"], writes=["rowtmp"], tag=tag + "sub")
        em.op("S", lambda s: s.copy(self.rowlo[0:1, :wdt], self.rowtmp[0:1, :wdt]),
              reads=["rowtmp"], writes=["rowlo"], tag=tag + "lo")
        em.op("T", lambda t: t.matmul(ps[:, :wdt], self.onesrb[:], self.rowhi[0:1, :wdt],
                                      start=True, stop=False, skip_group_check=True),
              reads=["rowhi", "onesrb"], writes=[psk], tag=tag + "mm1")
        em.op("T", lambda t: t.matmul(ps[:, :wdt], self.onesrb[:], self.rowlo[0:1, :wdt],
                                      start=False, stop=True, skip_group_check=True),
              reads=["rowlo", "onesrb"], writes=[psk], tag=tag + "mm2")

    def _rowstats(self, em, pSum, kSum, pSq, kSq, wdt, eps, rstd_row, nmr_row, krow, tag):
        em.op("S", lambda s: s.activation(rstd_row[0:1, :wdt], pSum[0:1, :wdt], AF.Copy,
                                          scale=1.0 / D),
              reads=[kSum], writes=[krow], tag=tag + "_mu")
        em.op("S", lambda s: s.activation(nmr_row[0:1, :wdt], rstd_row[0:1, :wdt], AF.Square),
              reads=[krow], writes=[krow])
        em.op("V", lambda v: v.scalar_tensor_tensor(nmr_row[0:1, :wdt], pSq[0:1, :wdt],
                                                    1.0 / D, nmr_row[0:1, :wdt],
                                                    ALU.mult, ALU.subtract),
              reads=[kSq, krow], writes=[krow])
        em.op("S", lambda s: s.activation(nmr_row[0:1, :wdt], nmr_row[0:1, :wdt], AF.Ln,
                                          bias=eps),
              reads=[krow], writes=[krow])
        em.op("S", lambda s: s.activation(nmr_row[0:1, :wdt], nmr_row[0:1, :wdt], AF.Exp,
                                          scale=-0.5),
              reads=[krow], writes=[krow])
        em.op("V", lambda v: v.scalar_tensor_tensor(rstd_row[0:1, :wdt], rstd_row[0:1, :wdt],
                                                    -1.0, nmr_row[0:1, :wdt],
                                                    ALU.mult, ALU.mult),
              reads=[krow], writes=[krow])

    def _ln_stats_blocks(self, src_is_x, blocks, layer_shift, tag, apply_fn=None):
        em = self.em
        pend = [None]

        def flush():
            if pend[0] is not None:
                pend[0]()
                pend[0] = None

        out = []
        for bi, (col0, wdt) in enumerate(blocks):
            par = bi % 2
            kSa, kSb = [("pC", "pD"), ("pA", "pB")][par]
            pSa, pSb = self.PS[kSa], self.PS[kSb]
            r1, r2 = self.strow[2 * par], self.strow[2 * par + 1]
            krow = f"strow{par}{tag}"
            for c in range(8):
                if src_is_x:
                    sc0 = col0 - 1 if (layer_shift and c >= 4) else col0
                    srck = self.xkeys(c, sc0)
                    src_ap = self.xc(c, sc0, wdt)
                    em.op("G", lambda g, src_ap=src_ap, wdt=wdt: g.tensor_copy(
                        self.lncast[:, :wdt], src_ap), reads=srck, writes=["lncast"],
                        tag=tag + "cast")
                    data = self.lncast[:, :wdt]
                    datak = ["lncast"]
                    em.op("S", lambda s, src_ap=src_ap, wdt=wdt: s.activation(
                        self.lnsq[:, :wdt], src_ap, AF.Square), reads=srck,
                        writes=["lnsq"], tag=tag + "sq")
                else:
                    srck = self.hkeys(c, col0)
                    data = self.hc(c, col0, wdt)
                    datak = srck
                    em.op("S", lambda s, c=c, col0=col0, wdt=wdt: s.activation(
                        self.lnsq[:, :wdt], self.hc(c, col0, wdt), AF.Square),
                        reads=srck, writes=["lnsq"], tag=tag + "sq")
                em.op("T", lambda t, data=data, wdt=wdt, c=c, pSa=pSa: t.matmul(
                    pSa[0:1, :wdt], self.onesb[:], data,
                    start=(c == 0), stop=(c == 7)),
                    reads=datak + ["onesb"], writes=[kSa], tag=tag + "mmS")
                em.op("T", lambda t, wdt=wdt, c=c, pSb=pSb: t.matmul(
                    pSb[0:1, :wdt], self.onesb[:], self.lnsq[:, :wdt],
                    start=(c == 0), stop=(c == 7)),
                    reads=["lnsq", "onesb"], writes=[kSb], tag=tag + "mmQ")
            self._rowstats(em, pSa, kSa, pSb, kSb, wdt, 1e-5, r1, r2, krow, tag)
            flush()

            def tail(bi=bi, col0=col0, wdt=wdt, r1=r1, r2=r2, krow=krow):
                self.emit_replicate(r2, krow, self.pE, "pE", wdt, tag + "rep1")
                self.emit_replicate(r1, krow, self.pF, "pF", wdt, tag + "rep2")
                if apply_fn is not None:
                    apply_fn(bi, col0, wdt)
            if apply_fn is None:
                tail()
                out.append((bi, col0, wdt))
            else:
                pend[0] = tail
        flush()
        if apply_fn is None:
            return out
        return None

    def step0(self):
        nc, em, es = self.nc, self.em, self.es
        sc = ExitStack()
        xt = [sc.enter_context(nc.sbuf_tensor(f"xt{i}", [128, 1024], F32)) for i in range(2)]
        lt = [sc.enter_context(nc.sbuf_tensor(f"lt{i}", [128, 1024], BF16)) for i in range(2)]
        sq0 = [sc.enter_context(nc.sbuf_tensor(f"sq0{i}", [128, 1024], F32)) for i in range(2)]
        st0 = [sc.enter_context(nc.sbuf_tensor(f"st0{i}", [128, 6], F32)) for i in range(2)]
        sth = sc.enter_context(nc.sbuf_tensor("sth", [1, 6], F32))
        self.xrow = sc.enter_context(nc.sbuf_tensor("xrow", [1, 1024], F32))
        sqh = sc.enter_context(nc.sbuf_tensor("sqh", [1, 1024], F32))
        lh = sc.enter_context(nc.sbuf_tensor("lh", [1, 1024], BF16))

        def tok_stats(src_ap, sq_t, st_t, pcnt, kin, kst):
            em.op("S", lambda s: s.activation(sq_t[:pcnt, :], src_ap, AF.Square),
                  reads=[kin], writes=[kst + "sq"])
            em.op("V", lambda v: v.tensor_reduce(st_t[:pcnt, 0:1], src_ap, AX.X, ALU.add),
                  reads=[kin], writes=[kst])
            em.op("V", lambda v: v.tensor_reduce(st_t[:pcnt, 1:2], sq_t[:pcnt, :], AX.X, ALU.add),
                  reads=[kst + "sq"], writes=[kst])
            em.op("V", lambda v: v.tensor_single_scalar(st_t[:pcnt, 2:3], st_t[:pcnt, 0:1],
                                                        1.0 / D, ALU.mult),
                  reads=[kst], writes=[kst])
            em.op("S", lambda s: s.activation(st_t[:pcnt, 3:4], st_t[:pcnt, 2:3], AF.Square),
                  reads=[kst], writes=[kst])
            em.op("V", lambda v: v.scalar_tensor_tensor(st_t[:pcnt, 3:4], st_t[:pcnt, 1:2],
                                                        1.0 / D, st_t[:pcnt, 3:4],
                                                        ALU.mult, ALU.subtract),
                  reads=[kst], writes=[kst])
            em.op("S", lambda s: s.activation(st_t[:pcnt, 3:4], st_t[:pcnt, 3:4], AF.Ln,
                                              bias=1e-5),
                  reads=[kst], writes=[kst])
            em.op("S", lambda s: s.activation(st_t[:pcnt, 4:5], st_t[:pcnt, 3:4], AF.Exp,
                                              scale=-0.5),
                  reads=[kst], writes=[kst])
            em.op("V", lambda v: v.scalar_tensor_tensor(st_t[:pcnt, 5:6], st_t[:pcnt, 2:3],
                                                        -1.0, st_t[:pcnt, 4:5],
                                                        ALU.mult, ALU.mult),
                  reads=[kst], writes=[kst])

        em.dma(lambda q: q.dma_start(out=self.xrow[:], in_=self.in_xh), writes=["xrow"])
        tok_stats(self.xrow[0:1, :], sqh, sth, 1, "xrow", "sth")
        em.op("S", lambda s: s.activation(lh[:], self.xrow[0:1, :], AF.Identity,
                                          scale=sth[0:1, 4:5], bias=sth[0:1, 5:6]),
              reads=["xrow", "sth"], writes=["lhrow"])
        em.dma(lambda q: q.dma_start(out=self.d_lh, in_=lh[:]), reads=["lhrow"],
               writes=["d_lh"])
        for c in range(8):
            em.dma(lambda q, c=c: q.dma_start(
                out=self.xc(c, 0, 1),
                in_=self.in_xh[0:1, c * 128:(c + 1) * 128].rearrange("a p -> p a")),
                writes=[f"xh{c}"], tag=f"xh{c}")
            em.dma(lambda q, c=c: q.dma_start(
                out=self.hc(c, 0, 1),
                in_=self.d_lh[0:1, c * 128:(c + 1) * 128].rearrange("a p -> p a")),
                reads=["d_lh"], writes=[f"hh{c}"], tag=f"lh{c}")

        for i in range(16):
            p = i % 2
            kx, kl, kst = f"xt{p}", f"lt{p}", f"st{p}"
            em.dma(lambda q, i=i, p=p: q.dma_start(
                out=xt[p][:], in_=self.in_x[i * 128:(i + 1) * 128, :]),
                writes=[kx], tag=f"ldx{i}")
            tok_stats(xt[p][:, :], sq0[p], st0[p], 128, kx, kst)
            em.op("S", lambda s, p=p: s.activation(lt[p][:], xt[p][:], AF.Identity,
                                                   scale=st0[p][:, 4:5], bias=st0[p][:, 5:6]),
                  reads=[kx, kst], writes=[kl])
            for half in range(2):
                pk = "pA" if half == 0 else "pB"
                pst = self.PS[pk]
                for j in range(4):
                    c = half * 4 + j
                    em.op("T", lambda t, p=p, c=c, j=j, pst=pst: t.matmul(
                        pst[:, j * 128:(j + 1) * 128], xt[p][:, c * 128:(c + 1) * 128],
                        self.identf[:], start=(j == 0), stop=(j == 3),
                        is_transpose=True, skip_group_check=True),
                        reads=[kx, "identf"], writes=[pk])
                dst = view3(self.x[:], half * 4 * WCOL + 1 + i * 128, WCOL, 4, 128)
                em.op("V", lambda v, pst=pst, dst=dst: v.tensor_copy(dst, pst[:]),
                      reads=[pk], writes=[f"x{half * 4 + j2}" for j2 in range(4)])
            ptk = "pT1" if p == 0 else "pT2"
            pt = self.PS[ptk]
            for c in range(8):
                em.op("T", lambda t, p=p, c=c, pt=pt: t.matmul(
                    pt[:, c * 128:(c + 1) * 128], lt[p][:, c * 128:(c + 1) * 128],
                    self.identb[:], start=(c == 0), stop=(c == 7),
                    is_transpose=True, skip_group_check=True),
                    reads=[kl, "identb"], writes=[ptk])
            dsth = view3(self.hbuf[:], 1 + i * 128, WCOL, 8, 128)
            em.op("S", lambda s, pt=pt, dsth=dsth: s.copy(dsth, pt[:]),
                  reads=[ptk], writes=[f"h{c2}" for c2 in range(8)])
        self.tap_sbuf("x_fm", self.x, F32,
                      [f"x{c}" for c in range(8)] + [f"xh{c}" for c in range(8)])
        self.tap_sbuf("ln1", self.hbuf, BF16,
                      [f"h{c}" for c in range(8)] + [f"hh{c}" for c in range(8)])
        em.barrier()
        sc.close()

    def attn_block(self):
        nc, em, es = self.nc, self.em, self.es
        sc = ExitStack()
        wA = sc.enter_context(nc.sbuf_tensor("wA", [128, 8 * 1024], BF16))
        wB = sc.enter_context(nc.sbuf_tensor("wB", [128, 8 * 1024], BF16))
        stg = [sc.enter_context(nc.sbuf_tensor(f"stg{i}", [128, 1024], F32)) for i in range(2)]
        attnv = sc.enter_context(nc.sbuf_tensor("attnv", [128, 8 * WCOL], BF16))
        scr = [sc.enter_context(nc.sbuf_tensor(f"scrA{i}", [128, 512], F32)) for i in range(6)]
        kvacc = sc.enter_context(nc.sbuf_tensor("kvacc", [128, 32], F32))
        ksacc = sc.enter_context(nc.sbuf_tensor("ksacc", [128, 32], F32))

        def load_w(src, dst, dkey, tag):
            for kc in range(8):
                p = kc % 2
                em.dma(lambda q, kc=kc, p=p, src=src: q.dma_start(
                    out=stg[p][:], in_=src[kc * 128:(kc + 1) * 128, :]),
                    writes=[f"stg{p}"], tag=f"{tag}{kc}")
                em.op("G", lambda g, kc=kc, p=p, dst=dst: g.tensor_copy(
                    dst[:, kc * 1024:(kc + 1) * 1024], stg[p][:]),
                    reads=[f"stg{p}"], writes=[dkey], tag=f"{tag}c{kc}")

        def fmap(ps, pk, bias, nbias, biask, s1, k1, s2, k2, outt, outk, wdt):
            em.op("S", lambda s: s.activation(s1[:, :wdt], ps[:, :wdt], AF.Relu, bias=bias),
                  reads=[pk, biask], writes=[k1])
            em.op("S", lambda s: s.activation(s2[:, :wdt], ps[:, :wdt], AF.Relu,
                                              bias=nbias, scale=-1.0),
                  reads=[pk, "n" + biask], writes=[k2])
            em.op("S", lambda s: s.activation(s2[:, :wdt], s2[:, :wdt], AF.Exp, scale=-1.0),
                  reads=[k2], writes=[k2])
            em.op("G", lambda g: g.tensor_add(outt[:, :wdt], s1[:, :wdt], s2[:, :wdt]),
                  reads=[k1, k2], writes=[outk])

        load_w(self.in_W["Wk"], wA, "wA", "ldWk")
        load_w(self.in_W["Wv"], wB, "wB", "ldWv")

        for m in range(8):
            for cb in range(4):
                col0 = 1 + cb * 512
                pkk = "pA" if cb % 2 == 0 else "pB"
                pvk = "pC" if cb % 2 == 0 else "pD"
                pk, pv = self.PS[pkk], self.PS[pvk]
                for kc in range(8):
                    em.op("T", lambda t, m=m, kc=kc, col0=col0, pk=pk: t.matmul(
                        pk[:], wA[:, kc * 1024 + m * 128: kc * 1024 + (m + 1) * 128],
                        self.hc(kc, col0, 512), start=(kc == 0), stop=(kc == 7)),
                        reads=["wA", f"h{kc}"], writes=[pkk])
                for kc in range(8):
                    em.op("T", lambda t, m=m, kc=kc, col0=col0, pv=pv: t.matmul(
                        pv[:], wB[:, kc * 1024 + m * 128: kc * 1024 + (m + 1) * 128],
                        self.hc(kc, col0, 512), start=(kc == 0), stop=(kc == 7)),
                        reads=["wB", f"h{kc}"], writes=[pvk])
                fmap(pk, pkk, self.bkt[:, m:m + 1], self.nbkt[:, m:m + 1], "bkt",
                     scr[1], "scr1", scr[2], "scr2", scr[0], "scr0", 512)
                em.op("S", lambda s, pv=pv, m=m: s.activation(
                    scr[3][:], pv[:], AF.Identity, bias=self.bvt[:, m:m + 1]),
                    reads=[pvk, "bvt"], writes=["scr3"])
                em.op("G", lambda g: g.tensor_mul(scr[4][:], scr[0][:], scr[3][:]),
                      reads=["scr0", "scr3"], writes=["scr4"])
                em.op("V", lambda v, m=m, cb=cb: v.tensor_reduce(
                    kvacc[:, m * 4 + cb: m * 4 + cb + 1], scr[4][:], AX.X, ALU.add),
                    reads=["scr4"], writes=["kvacc"])
                em.op("V", lambda v, m=m, cb=cb: v.tensor_reduce(
                    ksacc[:, m * 4 + cb: m * 4 + cb + 1], scr[0][:], AX.X, ALU.add),
                    reads=["scr0"], writes=["ksacc"])
        em.op("V", lambda v: v.tensor_reduce(self.pp[0][:, 0:8],
                                             view3(kvacc[:], 0, 4, 8, 4), AX.X, ALU.add),
              reads=["kvacc"], writes=["pp0"])
        em.op("V", lambda v: v.tensor_reduce(self.pp[1][:, 0:8],
                                             view3(ksacc[:], 0, 4, 8, 4), AX.X, ALU.add),
              reads=["ksacc"], writes=["pp1"])
        for m in range(8):
            em.dma(lambda q, m=m: q.dma_start(
                out=self.d_ags_in[0:1, m * 128:(m + 1) * 128].rearrange("a p -> p a"),
                in_=self.pp[0][:, m:m + 1]),
                reads=["pp0"], writes=["d_ags_in"])
            em.dma(lambda q, m=m: q.dma_start(
                out=self.d_ags_in[1:2, m * 128:(m + 1) * 128].rearrange("a p -> p a"),
                in_=self.pp[1][:, m:m + 1]),
                reads=["pp1"], writes=["d_ags_in"])
        em.coll(lambda g: g.collective_compute(
            "AllGather", ALU.bypass, replica_groups=[[0, 1, 2, 3], [4, 5, 6, 7]],
            ins=[self.d_ags_in], outs=[self.d_ags_out]),
            reads=["d_ags_in"], writes=["d_ags_out"], tag="AGsums")
        load_w(self.in_W["Wq"], wA, "wA", "ldWq")
        for m in range(8):
            for r in range(8):
                em.dma(lambda q, m=m, r=r: q.dma_start(
                    out=self.pp[2][:, r:r + 1],
                    in_=self.d_ags_out[r:r + 1, m * 128:(m + 1) * 128].rearrange("a p -> p a")),
                    reads=["d_ags_out"], writes=["pp2"], tag=f"agr{m}_{r}")
            pstep = self.pp[2].ap().ap[0][0]
            kvv = bass.AP(self.pp[2].ap().tensor, 0, [[pstep, 128], [2, 4]])
            ksv = bass.AP(self.pp[2].ap().tensor, 1, [[pstep, 128], [2, 4]])
            em.op("V", lambda v, m=m, kvv=kvv: v.tensor_reduce(
                self.kvtot[:, m:m + 1], kvv, AX.X, ALU.add),
                reads=["pp2"], writes=["kvtot"])
            em.op("V", lambda v, m=m, ksv=ksv: v.tensor_reduce(
                self.kstot[:, m:m + 1], ksv, AX.X, ALU.add),
                reads=["pp2"], writes=["kstot"])
        blocks5 = [(0, 512), (512, 512), (1024, 512), (1536, 512), (2048, 1)]
        for m in range(8):
            for bi, (col0, wdt) in enumerate(blocks5):
                pqk = "pA" if bi % 2 == 0 else "pB"
                pq = self.PS[pqk]
                for kc in range(8):
                    em.op("T", lambda t, m=m, kc=kc, col0=col0, wdt=wdt, pq=pq: t.matmul(
                        pq[:, :wdt], wA[:, kc * 1024 + m * 128: kc * 1024 + (m + 1) * 128],
                        self.hc(kc, col0, wdt), start=(kc == 0), stop=(kc == 7)),
                        reads=["wA"] + self.hkeys(kc, col0), writes=[pqk])
                fmap(pq, pqk, self.bqt[:, m:m + 1], self.nbqt[:, m:m + 1], "bqt",
                     scr[1], "scr1", scr[2], "scr2", scr[5], "scr5", wdt)
                em.op("G", lambda g, m=m, wdt=wdt: g.tensor_tensor(
                    scr[3][:, :wdt], scr[5][:, :wdt], bc_free(self.kstot, wdt, m), ALU.mult),
                    reads=["scr5", "kstot"], writes=["scr3"])
                em.op("S", lambda s, wdt=wdt: s.activation(scr[3][:, :wdt], scr[3][:, :wdt],
                                                           AF.Ln, bias=1e-6),
                      reads=["scr3"], writes=["scr3"])
                em.op("S", lambda s, wdt=wdt: s.activation(scr[3][:, :wdt], scr[3][:, :wdt],
                                                           AF.Exp, scale=-1.0),
                      reads=["scr3"], writes=["scr3"])
                em.op("G", lambda g, m=m, wdt=wdt: g.tensor_tensor(
                    scr[4][:, :wdt], scr[5][:, :wdt], bc_free(self.kvtot, wdt, m), ALU.mult),
                    reads=["scr5", "kvtot"], writes=["scr4"])
                em.op("V", lambda v, m=m, col0=col0, wdt=wdt: v.tensor_tensor(
                    attnv[:, m * WCOL + col0: m * WCOL + col0 + wdt],
                    scr[4][:, :wdt], scr[3][:, :wdt], ALU.mult),
                    reads=["scr3", "scr4"], writes=[f"av{m}"])
        self.tap_sbuf("attn_v", attnv, BF16, [f"av{m}" for m in range(8)])

        load_w(self.in_W["Wo"], wB, "wB", "ldWo")
        for bi, (col0, wdt) in enumerate(blocks5):
            for m in range(8):
                pwk = "pA" if (bi * 8 + m) % 2 == 0 else "pB"
                pw = self.PS[pwk]
                for kc in range(8):
                    em.op("T", lambda t, m=m, kc=kc, col0=col0, wdt=wdt, pw=pw: t.matmul(
                        pw[:, :wdt], wB[:, kc * 1024 + m * 128: kc * 1024 + (m + 1) * 128],
                        attnv[:, kc * WCOL + col0: kc * WCOL + col0 + wdt],
                        start=(kc == 0), stop=(kc == 7)),
                        reads=["wB", f"av{kc}"], writes=[pwk])
                wk = [f"h{m}"] + ([f"hh{m}"] if col0 == 0 else [])
                em.op("S", lambda s, m=m, col0=col0, wdt=wdt, pw=pw: s.activation(
                    self.hc(m, col0, wdt), pw[:, :wdt], AF.Gelu_apprx_tanh,
                    bias=self.bot[:, m:m + 1]),
                    reads=[pwk, "bot"], writes=wk)
        self.tap_sbuf("res", self.hbuf, BF16,
                      [f"h{c}" for c in range(8)] + [f"hh{c}" for c in range(8)])

        blocks5o = [(0, 512), (512, 512), (1024, 512), (1536, 512), (1537, 512)]

        def apply_lnres(bi, col0, wdt):
            ov = (bi == 4)
            for c in range(8):
                em.op("V", lambda v, c=c, col0=col0, wdt=wdt: v.tensor_tensor(
                    self.lnt1[:, :wdt], self.hc(c, col0, wdt), self.pE[:, :wdt], ALU.mult),
                    reads=self.hkeys(c, col0) + ["pE"], writes=["lnt1"])
                em.op("V", lambda v, wdt=wdt: v.tensor_tensor(
                    self.lnt1[:, :wdt], self.lnt1[:, :wdt], self.pF[:, :wdt], ALU.add),
                    reads=["lnt1", "pF"], writes=["lnt1"])
                wk = self.xkeys(c, col0)
                if not ov:
                    em.op("V", lambda v, c=c, col0=col0, wdt=wdt: v.tensor_tensor(
                        self.xc(c, col0, wdt), self.xc(c, col0, wdt), self.lnt1[:, :wdt],
                        ALU.add),
                        reads=["lnt1"] + wk, writes=wk)
                else:
                    em.op("S", lambda s_, c=c: s_.activation(
                        self.xc(c, 2048, 1), self.lnt1[:, 511:512], AF.Identity,
                        bias=self.xc(c, 2048, 1)),
                        reads=["lnt1", f"x{c}"], writes=[f"x{c}"])

        self._ln_stats_blocks(False, blocks5o, None, "lnres", apply_fn=apply_lnres)
        self.tap_sbuf("x_res1", self.x, F32,
                      [f"x{c}" for c in range(8)] + [f"xh{c}" for c in range(8)])
        for c in range(8):
            em.dma(lambda q, c=c: q.dma_start(out=self.d_xres1[c], in_=self.xc(c, 1, T)),
                   reads=[f"x{c}"], writes=["d_xres1"], tag=f"spill{c}")
        blocks5x = [(1537, 512), (0, 512), (512, 512), (1024, 512), (1536, 512)]

        def apply_lnx1(bi, col0, wdt):
            ov = (bi == 0)
            for c in range(8):
                rk = self.xkeys(c, col0)
                em.op("V", lambda v, c=c, col0=col0, wdt=wdt: v.tensor_tensor(
                    self.lnt1[:, :wdt], self.xc(c, col0, wdt), self.pE[:, :wdt], ALU.mult),
                    reads=rk + ["pE"], writes=["lnt1"])
                if not ov:
                    em.op("V", lambda v, c=c, col0=col0, wdt=wdt: v.tensor_tensor(
                        self.xc(c, col0, wdt), self.lnt1[:, :wdt], self.pF[:, :wdt], ALU.add),
                        reads=["lnt1", "pF"], writes=rk)
                else:
                    em.op("V", lambda v, wdt=wdt: v.tensor_tensor(
                        self.lnt1[:, :wdt], self.lnt1[:, :wdt], self.pF[:, :wdt], ALU.add),
                        reads=["lnt1", "pF"], writes=["lnt1"])
                    em.op("S", lambda s_, c=c: s_.copy(
                        self.xc(c, 2048, 1), self.lnt1[:, 511:512]),
                        reads=["lnt1"], writes=[f"x{c}"])

        self._ln_stats_blocks(True, blocks5x, None, "lnx1", apply_fn=apply_lnx1)
        for c in range(8):
            em.op("S", lambda s_, c=c: s_.activation(
                self.xc(c, 0, 1), self.xc(c, 0, 1), AF.Identity, scale=self.gate[:]),
                reads=[f"xh{c}", "gate"], writes=[f"xh{c}"])
        self.tap_sbuf("x1", self.x, F32,
                      [f"x{c}" for c in range(8)] + [f"xh{c}" for c in range(8)])
        em.barrier()
        sc.close()

    def layer(self, l):
        nc, em, es = self.nc, self.em, self.es
        sc_outer = ExitStack()
        q_fm = sc_outer.enter_context(nc.sbuf_tensor(f"q_fm{l}", [128, 4 * 2048], BF16))
        k_fm = sc_outer.enter_context(nc.sbuf_tensor(f"k_fm{l}", [128, 4 * 2560], BF16))
        v_tm = sc_outer.enter_context(nc.sbuf_tensor(f"v_tm{l}", [128, 20 * 512], BF16))
        qlo_fm = sc_outer.enter_context(nc.sbuf_tensor(f"qlo_fm{l}", [128, 512], BF16))
        klo_fm = sc_outer.enter_context(nc.sbuf_tensor(f"klo_fm{l}", [128, 512], BF16))
        AG = [[0, 1, 2, 3], [4, 5, 6, 7]]

        scA = ExitStack()
        self.rotc = scA.enter_context(nc.sbuf_tensor(f"rotcL{l}", [128, 16 * 128], F32))
        tabq = scA.enter_context(nc.sbuf_tensor(f"tabqL{l}", [128, 16 * 128], BF16))
        tabk = scA.enter_context(nc.sbuf_tensor(f"tabkL{l}", [128, 16 * 128], BF16))
        wpart = scA.enter_context(nc.sbuf_tensor(f"wpart{l}", [128, 8 * 512], BF16))
        stg = [scA.enter_context(nc.sbuf_tensor(f"stgL{l}_{i}", [128, 512], F32)) for i in range(1)]
        sqs = scA.enter_context(nc.sbuf_tensor(f"sqs{l}", [128, 512], F32))
        ss8 = scA.enter_context(nc.sbuf_tensor(f"ss8{l}", [128, 8], F32))
        rs8 = scA.enter_context(nc.sbuf_tensor(f"rs8{l}", [128, 8], F32))
        qn = scA.enter_context(nc.sbuf_tensor(f"qn{l}", [128, 512], F32))
        u = [self.rotc[:, 1024 + i * 256: 1024 + (i + 1) * 256] for i in range(4)]
        qrot = scA.enter_context(nc.sbuf_tensor(f"qrot{l}", [128, 512], BF16))
        qsrow = scA.enter_context(nc.sbuf_tensor(f"qsrow{l}", [1, DH], F32))
        ksrow = scA.enter_context(nc.sbuf_tensor(f"ksrow{l}", [1, DH], F32))

        if l == 1:
            for c in range(8):
                em.dma(lambda q, c=c: q.dma_start(
                    out=self.d_agx2_in[c * 128:(c + 1) * 128, 0:1],
                    in_=self.xc(c, 2048, 1)),
                    reads=[f"x{c}"], writes=["d_agx2_in"], tag=f"x2o{c}")
            em.coll(lambda g: g.collective_compute(
                "AllGather", ALU.bypass, replica_groups=AG,
                ins=[self.d_agx2_in], outs=[self.d_agx2_out]),
                reads=["d_agx2_in"], writes=["d_agx2_out"], tag="AGx2")
            for c in range(8):
                em.dma(lambda q, c=c: q.dma_start(
                    out=self.xc(c, 0, 1),
                    in_=self.d_agx2_out[bass.ts(self.rt["rr"] * 8 + c, 128), 0:1]),
                    reads=["d_agx2_out"], writes=[f"xh{c}"], tag=f"x2i{c}")
                em.op("S", lambda s_, c=c: s_.activation(
                    self.xc(c, 0, 1), self.xc(c, 0, 1), AF.Identity,
                    scale=self.gate[:]),
                    reads=[f"xh{c}", "gate"], writes=[f"xh{c}"])

        em.dma(lambda q: q.dma_start(
            out=self.rotc[:].rearrange("p (i c) -> p i c", c=128),
            in_=self.in_rotc.rearrange("(i p) c -> p i c", p=128)),
            writes=["rotc"])
        em.dma(lambda q: q.dma_start(out=qsrow[:], in_=self.in_qs[l]), writes=["qsrow"])
        em.dma(lambda q: q.dma_start(out=ksrow[:], in_=self.in_ks[l]), writes=["ksrow"])
        self.emit_replicate(qsrow, "qsrow", self.pE, "pE", DH, "repqs", scale=8.0)
        em.op("S", lambda s: s.copy(self.qsrep[:], self.pE[:, 0:DH]),
              reads=["pE"], writes=["qsrep"])
        self.emit_replicate(ksrow, "ksrow", self.pF, "pF", DH, "repks")
        em.op("S", lambda s: s.copy(self.ksrep[:], self.pF[:, 0:DH]),
              reads=["pF"], writes=["ksrep"])
        for i in range(16):
            base = i * 128
            for (tab, tkey, rep, repk, c0) in ((tabq, "tabq", self.qsrep, "qsrep", 0),
                                               (tabk, "tabk", self.ksrep, "ksrep", 64)):
                em.op("G", lambda g, base=base, tab=tab, rep=rep, c0=c0: g.tensor_tensor(
                    view3(tab[:], base, 32, 2, 32),
                    bc_free3(self.rotc, base + c0, 2, 32),
                    view3(rep[:], 0, 32, 2, 32), ALU.mult),
                    reads=["rotc", repk], writes=[tkey])
                em.op("G", lambda g, base=base, tab=tab, rep=rep, c0=c0: g.tensor_tensor(
                    tab[:, base + 64: base + 96],
                    self.rotc[:, base + c0 + 32: base + c0 + 64],
                    rep[:, 32:64], ALU.mult),
                    reads=["rotc", repk], writes=[tkey])
                em.op("G", lambda g, base=base, tab=tab, rep=rep, c0=c0: g.tensor_tensor(
                    tab[:, base + 96: base + 128],
                    self.rotc[:, base + c0 + 32: base + c0 + 64],
                    rep[:, 0:32], ALU.mult),
                    reads=["rotc", repk], writes=[tkey])

        order = [12, 13, 14, 15] + (list(range(0, 12)) if l == 0 else list(range(1, 12)) + [0])

        def load_part(off, tag):
            flushA()
            for kc in range(8):
                p = 0
                em.dma(lambda q, kc=kc, p=p, off=off: q.dma_start(
                    out=stg[p][:],
                    in_=self.in_wqkv[l][kc * 128:(kc + 1) * 128, off:off + 512]),
                    writes=[f"stgL{p}"], tag=f"{tag}{kc}")
                em.op("G", lambda g, kc=kc, p=p: g.tensor_copy(
                    wpart[:, kc * 512:(kc + 1) * 512], stg[p][:]),
                    reads=[f"stgL{p}"], writes=["wpart"])

        def qk_post(i, which, PP, ppk, tab, tkey, dst, dkey, dst_off, dst_stride, tpk, TPK, toff):
            em.op("S", lambda s, PP=PP: s.activation(sqs[:], PP[:], AF.Square),
                  reads=[ppk], writes=["sqs"])
            em.op("V", lambda v: v.tensor_reduce(ss8[:, 0:8], view3(sqs[:], 0, 64, 8, 64),
                                                 AX.X, ALU.add),
                  reads=["sqs"], writes=["ss8"])
            em.op("S", lambda s: s.activation(ss8[:], ss8[:], AF.Sqrt, bias=1e-24),
                  reads=["ss8"], writes=["ss8"])
            em.op("V", lambda v: v.reciprocal(rs8[:], ss8[:]),
                  reads=["ss8"], writes=["rs8"])
            rinv_bc = bass.AP(rs8.ap().tensor, 0,
                              [[rs8.ap().ap[0][0], 128], [1, 8], [0, 64]])
            em.op("V", lambda v, PP=PP, rinv_bc=rinv_bc: v.tensor_tensor(
                qn[:].rearrange("p (h d) -> p h d", h=8),
                PP[:].rearrange("p (h d) -> p h d", h=8), rinv_bc, ALU.mult),
                reads=[ppk, "rs8"], writes=["qn"])
            base = i * 128
            qlo = view3(qn[:], 0, 64, 8, 32)
            qhi = view3(qn[:], 32, 64, 8, 32)
            em.op("G", lambda g, tab=tab, base=base, qlo=qlo: g.tensor_tensor(
                u[0].rearrange("p (h d) -> p h d", h=8), qlo,
                bc_free3(tab, base, 8, 32), ALU.mult),
                reads=["qn", tkey], writes=["u0", "rotc"])
            em.op("G", lambda g, tab=tab, base=base, qhi=qhi: g.tensor_tensor(
                u[1].rearrange("p (h d) -> p h d", h=8), qhi,
                bc_free3(tab, base + 64, 8, 32), ALU.mult),
                reads=["qn", tkey], writes=["u1", "rotc"])
            em.op("V", lambda v: v.tensor_tensor(
                view3(qrot[:], 0, 64, 8, 32), u[0].rearrange("p (h d) -> p h d", h=8),
                u[1].rearrange("p (h d) -> p h d", h=8), ALU.subtract),
                reads=["u0", "u1"], writes=["qrot"])
            em.op("G", lambda g, tab=tab, base=base, qhi=qhi: g.tensor_tensor(
                u[2].rearrange("p (h d) -> p h d", h=8), qhi,
                bc_free3(tab, base + 32, 8, 32), ALU.mult),
                reads=["qn", tkey], writes=["u2", "rotc"])
            em.op("G", lambda g, tab=tab, base=base, qlo=qlo: g.tensor_tensor(
                u[3].rearrange("p (h d) -> p h d", h=8), qlo,
                bc_free3(tab, base + 96, 8, 32), ALU.mult),
                reads=["qn", tkey], writes=["u3", "rotc"])
            em.op("V", lambda v: v.tensor_tensor(
                view3(qrot[:], 32, 64, 8, 32), u[2].rearrange("p (h d) -> p h d", h=8),
                u[3].rearrange("p (h d) -> p h d", h=8), ALU.add),
                reads=["u2", "u3"], writes=["qrot"])
            for j in range(4):
                em.op("T", lambda t, j=j, toff=toff, TPK=TPK: t.matmul(
                    TPK[:, toff + j * 128: toff + (j + 1) * 128],
                    qrot[:, j * 128:(j + 1) * 128], self.identb[:],
                    start=(j == 0), stop=(j == 3),
                    is_transpose=True, skip_group_check=True),
                    reads=["qrot", "identb"], writes=[tpk])
            ddst = view3(dst[:], dst_off, dst_stride, 4, 128)
            em.op("S", lambda s, ddst=ddst, toff=toff, TPK=TPK: s.copy(
                ddst, TPK[:, toff:toff + 512]),
                reads=[tpk], writes=[dkey])
            if i == 0:
                lo_sb = qn.ap().bitcast(BF16)[:, 0:512]
                em.op("V", lambda v: v.tensor_tensor(
                    view3(sqs[:], 0, 64, 8, 32), u[0].rearrange("p (h d) -> p h d", h=8),
                    u[1].rearrange("p (h d) -> p h d", h=8), ALU.subtract),
                    reads=["u0", "u1"], writes=["sqs"])
                em.op("V", lambda v: v.tensor_tensor(
                    view3(sqs[:], 32, 64, 8, 32), u[2].rearrange("p (h d) -> p h d", h=8),
                    u[3].rearrange("p (h d) -> p h d", h=8), ALU.add),
                    reads=["u2", "u3"], writes=["sqs"])
                em.op("V", lambda v: v.tensor_tensor(
                    sqs[:], sqs[:], qrot[:], ALU.subtract),
                    reads=["sqs", "qrot"], writes=["sqs"])
                em.op("S", lambda s, lo_sb=lo_sb: s.copy(lo_sb, sqs[:]),
                      reads=["sqs"], writes=["qn"])
                lodst = qlo_fm if which == "q" else klo_fm
                lokey = "qlofm" if which == "q" else "klofm"
                for j in range(4):
                    em.op("T", lambda t, j=j, toff=toff, TPK=TPK, lo_sb=lo_sb: t.matmul(
                        TPK[:, toff + j * 128: toff + (j + 1) * 128],
                        lo_sb[:, j * 128:(j + 1) * 128], self.identb[:],
                        start=(j == 0), stop=(j == 3),
                        is_transpose=True, skip_group_check=True),
                        reads=["qn", "identb"], writes=[tpk])
                lddst = view3(lodst[:], 0, 128, 4, 128)
                em.op("S", lambda s, lddst=lddst, toff=toff, TPK=TPK: s.copy(
                    lddst, TPK[:, toff:toff + 512]),
                    reads=[tpk], writes=[lokey])

        prevA = [None]

        def flushA():
            if prevA[0] is not None:
                prevA[0]()
                prevA[0] = None

        for pi, part in enumerate(("k", "v", "q")):
            off = {"q": 0, "k": 512, "v": 1024}[part]
            load_part(off, f"ldw{part}")
            for i in order:
                if pi == 0:
                    for c in range(8):
                        sc0 = (1 + i * 128) if c < 4 else (0 + i * 128)
                        em.op("G", lambda g, c=c, i=i, sc0=sc0: g.tensor_copy(
                            self.hc(c, 1 + i * 128, 128), self.xc(c, sc0, 128)),
                            reads=self.xkeys(c, sc0 if c >= 4 else 1), writes=[f"h{c}"],
                            tag=f"hcast{i}_{c}")
                ppk = [["pA", "pB"], ["pC", "pD"], ["pE", "pF"]][pi][i % 2]
                PP = self.PS[ppk]
                for kc in range(8):
                    em.op("T", lambda t, kc=kc, i=i, PP=PP: t.matmul(
                        PP[:], self.hc(kc, 1 + i * 128, 128),
                        wpart[:, kc * 512:(kc + 1) * 512],
                        start=(kc == 0), stop=(kc == 7)),
                        reads=[f"h{kc}", "wpart"], writes=[ppk])
                flushA()
                if part == "v":
                    em.op("S", lambda s, i=i, PP=PP: s.copy(
                        v_tm[:, (4 + i) * 512:(5 + i) * 512], PP[:]),
                        reads=[ppk], writes=[f"vt{4 + i}"])
                elif part == "k":
                    tpk = ["pT1", "pT2"][i % 2]
                    prevA[0] = (lambda i=i, PP=PP, ppk=ppk, tpk=tpk: qk_post(
                        i, "k", PP, ppk, tabk, "tabk", k_fm, f"kw{i // 4}",
                        512 + i * 128, 2560, tpk, self.PS[tpk], 512))
                else:
                    tpk = ["pT1", "pT2"][i % 2]
                    prevA[0] = (lambda i=i, PP=PP, ppk=ppk, tpk=tpk: qk_post(
                        i, "q", PP, ppk, tabq, "tabq", q_fm, f"qw{i // 4}",
                        i * 128, 2048, tpk, self.PS[tpk], 0))
                if part == "v" and i == 15:
                    for c in range(4):
                        em.dma(lambda q, c=c: q.dma_start(
                            out=self.d_agkv_in[l][c * 128:(c + 1) * 128, :],
                            in_=k_fm[:, c * 2560 + 512 + 1536: c * 2560 + 512 + 2048]),
                            reads=["kw3"], writes=["d_agkv_in"], tag=f"kvout{c}")
                    for j in range(4):
                        em.dma(lambda q, j=j: q.dma_start(
                            out=self.d_agkv_in[l][512 + j * 128: 512 + (j + 1) * 128, :],
                            in_=v_tm[:, (16 + j) * 512:(17 + j) * 512]),
                            reads=[f"vt{16 + j}"], writes=["d_agkv_in"], tag=f"vvout{j}")
                    em.coll(lambda g, l=l: g.collective_compute(
                        "AllGather", ALU.bypass, replica_groups=AG,
                        ins=[self.d_agkv_in[l]], outs=[self.d_agkv_out[l]]),
                        reads=["d_agkv_in"], writes=["d_agkv_out"], tag=f"AGkv{l}")
                    for c in range(4):
                        em.dma(lambda q, c=c, l=l: q.dma_start(
                            out=k_fm[:, c * 2560: c * 2560 + 512],
                            in_=self.d_agkv_out[l][bass.ts(self.rt["rr"] * 8 + c, 128), :]),
                            reads=["d_agkv_out"], writes=["kwh"], tag=f"kvin{c}")
                    for j in range(4):
                        em.dma(lambda q, j=j, l=l: q.dma_start(
                            out=v_tm[:, j * 512:(j + 1) * 512],
                            in_=self.d_agkv_out[l][bass.ts(self.rt["rr"] * 8 + 4 + j, 128), :]),
                            reads=["d_agkv_out"], writes=[f"vt{j}"], tag=f"vvin{j}")

        flushA()
        if f"l{l}_qfm" in self.taps:
            self.tap_sbuf(f"l{l}_qfm", q_fm, BF16, [f"qw{w}" for w in range(4)])
            self.tap_sbuf(f"l{l}_kfm", k_fm, BF16, [f"kw{w}" for w in range(4)] + ["kwh"])
            self.tap_sbuf(f"l{l}_vtm", v_tm, BF16, [f"vt{j}" for j in range(20)])
        em.barrier()
        scA.close()

        scB = ExitStack()
        o_fm = scB.enter_context(nc.sbuf_tensor(f"o_fm{l}", [128, 4 * 2048], BF16))
        wo = scB.enter_context(nc.sbuf_tensor(f"wo{l}", [128, 4 * 1024], BF16))
        stgW = scB.enter_context(nc.sbuf_tensor(f"stgW{l}", [128, 1024], F32))
        p_sb = [scB.enter_context(nc.sbuf_tensor(f"p_sb{l}_{i}", [128, 640], BF16)) for i in range(2)]
        pt_sb = [scB.enter_context(nc.sbuf_tensor(f"pt_sb{l}_{i}", [128, 640], BF16)) for i in range(2)]
        o_tmp = [scB.enter_context(nc.sbuf_tensor(f"o_tmp{l}_{i}", [128, 128], BF16)) for i in range(2)]
        nmt = scB.enter_context(nc.sbuf_tensor(f"nmt{l}", [128, 8], F32))
        rrt = scB.enter_context(nc.sbuf_tensor(f"rrt{l}", [128, 8], F32))

        hpstate = {}
        prev_tail = [None]

        def flush_tail():
            if prev_tail[0] is not None:
                prev_tail[0]()
                prev_tail[0] = None

        for w in range(4):
            for s in range(4):
                pw_ = (4 - s) * 128
                cw = (s + 1) * 128
                for hp in range(4):
                    pok = ["pE", "pF"][hp % 2]
                    PO = self.PS[pok]
                    hpstate[(w, s, hp)] = {}
                    for hh in range(2):
                        h = 2 * hp + hh
                        up = self.par()
                        psb = p_sb[up]
                        psbk = f"psb{up}"
                        ptsb = pt_sb[up]
                        ptsbk = f"ptsb{up}"
                        tpk = ["pT1", "pT2"][up]
                        TPK = self.PS[tpk]
                        rrk = f"rrt{up}"
                        nmk = f"nmt{up}"
                        nmb = 4 * up
                        rb = 4 * up + 2 * hh
                        sA = ["pA", "pC"][up]
                        sB = ["pB", "pD"][up]
                        FULL = self.PS[["pABfull", "pCDfull"][up]]
                        st = hpstate[(w, s, hp)]
                        st[hh] = (rb, rrk, ptsb, ptsbk, TPK, tpk)
                        qsl = q_fm[hh * 64:(hh + 1) * 64,
                                   hp * 2048 + w * 512 + s * 128: hp * 2048 + w * 512 + (s + 1) * 128]
                        em.op("T", lambda t, qsl=qsl, hp=hp, hh=hh, w=w, s=s, pw_=pw_, FULL=FULL:
                              t.matmul(FULL[:, :pw_], qsl,
                                       k_fm[hh * 64:(hh + 1) * 64,
                                            hp * 2560 + w * 512 + s * 128:
                                            hp * 2560 + w * 512 + 512],
                                       start=True, stop=False, skip_group_check=True),
                              reads=[f"qw{w}", ("kwh" if w == 0 else f"kw{w - 1}")],
                              writes=[sA], tag=f"simP{w}{s}{h}")
                        if s > 0:
                            em.op("T", lambda t, qsl=qsl, hp=hp, hh=hh, w=w, s=s, pw_=pw_, FULL=FULL:
                                  t.matmul(FULL[:, pw_:512], qsl,
                                           k_fm[hh * 64:(hh + 1) * 64,
                                                hp * 2560 + 512 + w * 512:
                                                hp * 2560 + 512 + w * 512 + s * 128],
                                           start=False, stop=False, skip_group_check=True),
                                  reads=[f"qw{w}", f"kw{w}"], writes=[sA], tag=f"simC1_{w}{s}{h}")
                        em.op("T", lambda t, FULL=FULL, w=w: t.matmul(
                            FULL[:, 0:128], self.identb[:], self.maskl[:],
                            start=False, stop=(w != 0), skip_group_check=True),
                            reads=["identb", "maskl"], writes=[sA])
                        if w == 0:
                            em.op("T", lambda t, FULL=FULL, pw_=pw_: t.matmul(
                                FULL[:, 0:pw_], self.identb[:], self.maskpb[:, :pw_],
                                start=False, stop=True, skip_group_check=True),
                                reads=["identb", "maskpb"], writes=[sA])
                        hilo = (w == 0 and s == 0)
                        em.op("T", lambda t, qsl=qsl, hp=hp, hh=hh, w=w, s=s, FULL=FULL:
                              t.matmul(FULL[:, 512:640], qsl,
                                       k_fm[hh * 64:(hh + 1) * 64,
                                            hp * 2560 + 512 + w * 512 + s * 128:
                                            hp * 2560 + 512 + w * 512 + (s + 1) * 128],
                                       start=True, stop=False, skip_group_check=True),
                              reads=[f"qw{w}", f"kw{w}"], writes=[sB], tag=f"simC2_{w}{s}{h}")
                        if hilo:
                            em.op("T", lambda t, qsl=qsl, hp=hp, hh=hh, FULL=FULL:
                                  t.matmul(FULL[:, 512:640], qsl,
                                           klo_fm[hh * 64:(hh + 1) * 64,
                                                  hp * 128: (hp + 1) * 128],
                                           start=False, stop=False, skip_group_check=True),
                                  reads=[f"qw0", "klofm"], writes=[sB])
                            em.op("T", lambda t, hp=hp, hh=hh, FULL=FULL:
                                  t.matmul(FULL[:, 512:640],
                                           qlo_fm[hh * 64:(hh + 1) * 64,
                                                  hp * 128: (hp + 1) * 128],
                                           k_fm[hh * 64:(hh + 1) * 64,
                                                hp * 2560 + 512: hp * 2560 + 640],
                                           start=False, stop=False, skip_group_check=True),
                                  reads=["qlofm", f"kw0"], writes=[sB])
                        em.op("T", lambda t, FULL=FULL: t.matmul(
                            FULL[:, 512:640], self.identb[:], self.masku[:],
                            start=False, stop=True, skip_group_check=True),
                            reads=["identb", "masku"], writes=[sB])

                        flush_tail()

                        def tail(w=w, s=s, hp=hp, hh=hh, h=h, up=up, psb=psb,
                                 psbk=psbk, ptsb=ptsb, ptsbk=ptsbk, tpk=tpk,
                                 TPK=TPK, rrk=rrk, nmk=nmk, nmb=nmb, rb=rb,
                                 sA=sA, sB=sB, FULL=FULL, PO=PO, pok=pok):
                            em.op("V", lambda v: v.tensor_reduce(
                                nmt[:, nmb + hh: nmb + hh + 1], FULL[:, 0:640],
                                AX.X, ALU.max, negate=True),
                                reads=[sA, sB], writes=[nmk])
                            em.op("S", lambda s_: s_.activation(
                                psb[:, 0:640], FULL[:, 0:640], AF.Exp,
                                bias=nmt[:, nmb + hh: nmb + hh + 1],
                                accum_out=rrt[:, rb: rb + 1]),
                                reads=[sA, sB, nmk], writes=[psbk, rrk])
                            em.op("S", lambda s_: s_.activation(
                                rrt[:, rb + 1: rb + 2], rrt[:, rb: rb + 1], AF.Ln),
                                reads=[rrk], writes=[rrk])
                            em.op("S", lambda s_: s_.activation(
                                rrt[:, rb + 1: rb + 2], rrt[:, rb + 1: rb + 2],
                                AF.Exp, scale=-1.0),
                                reads=[rrk], writes=[rrk])
                            for j in range(5):
                                em.op("T", lambda t, j=j: t.matmul(
                                    TPK[:, j * 128:(j + 1) * 128],
                                    psb[:, j * 128:(j + 1) * 128],
                                    self.identb[:], start=(j == 0), stop=(j == 4),
                                    is_transpose=True, skip_group_check=True),
                                    reads=[psbk, "identb"], writes=[tpk])
                            em.op("V", lambda v: v.tensor_copy(ptsb[:], TPK[:, 0:640]),
                                  reads=[tpk], writes=[ptsbk])
                            for j in range(5):
                                vt = (4 * w + s + j) if j < (4 - s) else (4 + 4 * w + (j - (4 - s)))
                                em.op("T", lambda t, j=j, vt=vt: t.matmul(
                                    PO[:, hh * 64:(hh + 1) * 64],
                                    ptsb[:, j * 128:(j + 1) * 128],
                                    v_tm[:, vt * 512 + h * 64: vt * 512 + (h + 1) * 64],
                                    start=(hh == 0 and j == 0),
                                    stop=(hh == 1 and j == 4),
                                    skip_group_check=True),
                                    reads=[ptsbk, f"vt{vt}"], writes=[pok])
                            if hh == 1:
                                st2 = hpstate[(w, s, hp)]
                                ot = o_tmp[hp % 2]
                                otk = f"o_tmp{hp % 2}"
                                for h2 in range(2):
                                    rb2, rrk2, _, _, _, _ = st2[h2]
                                    rr_bc = bass.AP(rrt.ap().tensor, rb2 + 1,
                                                    [[rrt.ap().ap[0][0], 128], [0, 64]])
                                    em.op("V", lambda v, h2=h2, rr_bc=rr_bc: v.tensor_tensor(
                                        ot[:, h2 * 64:(h2 + 1) * 64],
                                        PO[:, h2 * 64:(h2 + 1) * 64],
                                        rr_bc, ALU.mult),
                                        reads=[pok, rrk2], writes=[otk])
                                _, _, _, _, TPK2, tpk2 = st2[1]
                                em.op("T", lambda t: t.matmul(
                                    TPK2[:, 768:896], ot[:], self.identb[:],
                                    start=True, stop=True,
                                    is_transpose=True, skip_group_check=True),
                                    reads=[otk, "identb"], writes=[tpk2])
                                em.op("S", lambda s_: s_.copy(
                                    o_fm[:, hp * 2048 + w * 512 + s * 128:
                                         hp * 2048 + w * 512 + (s + 1) * 128],
                                    TPK2[:, 768:896]),
                                    reads=[tpk2], writes=[f"ofm{w}"])
                                if os.environ.get("PBDBG") == f"{l},{w},{s},{hp}":
                                    dbg2 = self._pbdbg
                                    em.op("S", lambda s_: s_.copy(dbg2[:, 0:128], ot[:]),
                                          reads=[otk], writes=["pbdbg"])
                                    self.taps.add("pbdbg")
                                    self.tap_sbuf("pbdbg", dbg2, BF16, ["pbdbg"])

                        prev_tail[0] = tail
        flush_tail()
        if f"l{l}_ofm" in self.taps:
            self.tap_sbuf(f"l{l}_ofm", o_fm, BF16, [f"ofm{w}" for w in range(4)])

        for kc in range(4):
            em.dma(lambda q, kc=kc: q.dma_start(
                out=stgW[:], in_=self.in_wof[l][kc * 128:(kc + 1) * 128, :]),
                writes=["stgW"], tag=f"ldwo{kc}")
            em.op("G", lambda g, kc=kc: g.tensor_copy(
                wo[:, kc * 1024:(kc + 1) * 1024], stgW[:]),
                reads=["stgW"], writes=["wo"])
        for cb in range(4):
            for m in range(8):
                pwk = ["pA", "pB"][(cb * 8 + m) % 2]
                PW = self.PS[pwk]
                for kc in range(4):
                    em.op("T", lambda t, m=m, kc=kc, cb=cb, PW=PW: t.matmul(
                        PW[:], wo[:, kc * 1024 + m * 128: kc * 1024 + (m + 1) * 128],
                        o_fm[:, kc * 2048 + cb * 512:(kc) * 2048 + cb * 512 + 512],
                        start=(kc == 0), stop=(kc == 3)),
                        reads=["wo", f"ofm{cb}"], writes=[pwk])
                em.op("V", lambda v, m=m, cb=cb, PW=PW: v.tensor_tensor(
                    self.xc(m, 1 + cb * 512, 512), PW[:], self.xc(m, 1 + cb * 512, 512),
                    ALU.add),
                    reads=[pwk, f"x{m}"], writes=[f"x{m}"])
        if f"l{l}_xmid" in self.taps:
            self.tap_sbuf(f"l{l}_xmid", self.x, F32,
                          [f"x{c}" for c in range(8)] + [f"xh{c}" for c in range(8)])
        for c in range(8):
            em.dma(lambda q, c=c: q.dma_start(
                out=self.d_agxb_in[l][c * 128:(c + 1) * 128, 0:1],
                in_=self.xc(c, 2048, 1)),
                reads=[f"x{c}"], writes=["d_agxb_in"], tag=f"xbo{c}")
        em.coll(lambda g, l=l: g.collective_compute(
            "AllGather", ALU.bypass, replica_groups=AG,
            ins=[self.d_agxb_in[l]], outs=[self.d_agxb_out[l]]),
            reads=["d_agxb_in"], writes=["d_agxb_out"], tag=f"AGxb{l}")
        for c in range(8):
            em.dma(lambda q, c=c, l=l: q.dma_start(
                out=self.xc(c, 0, 1),
                in_=self.d_agxb_out[l][bass.ts(self.rt["rr"] * 8 + c, 128), 0:1]),
                reads=["d_agxb_out"], writes=[f"xh{c}"], tag=f"xbi{c}")
            em.op("S", lambda s_, c=c: s_.activation(
                self.xc(c, 0, 1), self.xc(c, 0, 1), AF.Identity,
                scale=self.gate[:]),
                reads=[f"xh{c}", "gate"], writes=[f"xh{c}"])
        em.barrier()
        scB.close()
        sc_outer.close()

        sc = ExitStack()
        hglu = sc.enter_context(nc.sbuf_tensor(f"hglu{l}", [128, NCA * 1024], BF16))
        w1a = [sc.enter_context(nc.sbuf_tensor(f"w1a{l}_{i}", [128, 1024], BF16)) for i in range(2)]
        w1g = [sc.enter_context(nc.sbuf_tensor(f"w1g{l}_{i}", [128, 1024], BF16)) for i in range(2)]
        stga = sc.enter_context(nc.sbuf_tensor(f"stga{l}", [128, 1024], F32))
        stgg = sc.enter_context(nc.sbuf_tensor(f"stgg{l}", [128, 1024], F32))
        w2m = [sc.enter_context(nc.sbuf_tensor(f"w2m{l}_{i}", [128, NCA * 128], BF16)) for i in range(2)]
        stg2 = sc.enter_context(nc.sbuf_tensor(f"stg2{l}", [128, NCA * 128], F32))
        glb = sc.enter_context(nc.sbuf_tensor(f"glb{l}", [128, 512], BF16))
        abf = sc.enter_context(nc.sbuf_tensor(f"abf{l}", [128, 512], F32))
        w2s = sc.enter_context(nc.sbuf_tensor(f"w2s{l}", [128, 512], F32))

        blocks4 = [(1, 512), (513, 512), (1025, 512), (1537, 512)]

        def apply_ln2(bi, col0, wdt):
            for c in range(8):
                sc0 = col0 - 1 if c >= 4 else col0
                em.op("V", lambda v, c=c, sc0=sc0, wdt=wdt: v.tensor_tensor(
                    self.lnt1[:, :wdt], self.xc(c, sc0, wdt), self.pE[:, :wdt], ALU.mult),
                    reads=self.xkeys(c, sc0) + ["pE"], writes=["lnt1"])
                em.op("V", lambda v, wdt=wdt: v.tensor_tensor(
                    self.lnt1[:, :wdt], self.lnt1[:, :wdt], self.pF[:, :wdt], ALU.add),
                    reads=["lnt1", "pF"], writes=["lnt1"])
                em.op("S", lambda s, c=c, col0=col0, wdt=wdt, l=l: s.activation(
                    self.hc(c, col0, wdt), self.lnt1[:, :wdt], AF.Identity,
                    scale=self.lgt[l][:, c:c + 1], bias=self.lbt[l][:, c:c + 1]),
                    reads=["lnt1", f"lgt{l}", f"lbt{l}"], writes=[f"h{c}"])

        self._ln_stats_blocks(True, blocks4, True, f"ln2_{l}", apply_fn=apply_ln2)
        if f"l{l}_ln2" in self.taps:
            self.tap_sbuf(f"l{l}_ln2", self.hbuf, BF16, [f"h{c}" for c in range(8)])

        for H in range(2):
            for ca in range(NCA):
                p = ca % 2
                em.dma(lambda q, ca=ca, l=l: q.dma_start(
                    out=stga[:].rearrange("p (kc n) -> p kc n", n=128),
                    in_=self.in_w1a[l][:, ca * 128:(ca + 1) * 128].rearrange(
                        "(kc p) n -> p kc n", p=128)),
                    writes=["stga"], tag=f"ldw1a{ca}")
                em.op("G", lambda g, p=p: g.tensor_copy(w1a[p][:], stga[:]),
                      reads=["stga"], writes=[f"w1a{p}"])
                em.dma(lambda q, ca=ca, l=l: q.dma_start(
                    out=stgg[:].rearrange("p (kc n) -> p kc n", n=128),
                    in_=self.in_w1g[l][:, ca * 128:(ca + 1) * 128].rearrange(
                        "(kc p) n -> p kc n", p=128)),
                    writes=["stgg"], tag=f"ldw1g{ca}")
                em.op("G", lambda g, p=p: g.tensor_copy(w1g[p][:], stgg[:]),
                      reads=["stgg"], writes=[f"w1g{p}"])
                for tb in range(2):
                    pak = ["pA", "pB"][tb]
                    pgk = ["pC", "pD"][tb]
                    PA_, PG_ = self.PS[pak], self.PS[pgk]
                    c0 = 1 + H * 1024 + tb * 512
                    for kc in range(8):
                        em.op("T", lambda t, kc=kc, p=p, c0=c0, PA_=PA_: t.matmul(
                            PA_[:], w1a[p][:, kc * 128:(kc + 1) * 128],
                            self.hc(kc, c0, 512), start=(kc == 0), stop=(kc == 7)),
                            reads=[f"w1a{p}", f"h{kc}"], writes=[pak])
                        em.op("T", lambda t, kc=kc, p=p, c0=c0, PG_=PG_: t.matmul(
                            PG_[:], w1g[p][:, kc * 128:(kc + 1) * 128],
                            self.hc(kc, c0, 512), start=(kc == 0), stop=(kc == 7)),
                            reads=[f"w1g{p}", f"h{kc}"], writes=[pgk])
                    em.op("S", lambda s, ca=ca, l=l, PG_=PG_: s.activation(
                        glb[:], PG_[:], AF.Gelu, bias=self.b1gt[l][:, ca:ca + 1]),
                        reads=[pgk, f"b1gt{l}"], writes=["glb"])
                    em.op("S", lambda s, ca=ca, l=l, PA_=PA_: s.activation(
                        abf[:], PA_[:], AF.Identity, bias=self.b1at[l][:, ca:ca + 1]),
                        reads=[pak, f"b1at{l}"], writes=["abf"])
                    em.op("V", lambda v, ca=ca, tb=tb: v.tensor_tensor(
                        hglu[:, ca * 1024 + tb * 512: ca * 1024 + (tb + 1) * 512],
                        abf[:], glb[:], ALU.mult),
                        reads=["abf", "glb"], writes=[f"hg{ca}"])
            for m in range(8):
                p = m % 2
                em.dma(lambda q, m=m, l=l: q.dma_start(
                    out=stg2[:].rearrange("p (kc n) -> p kc n", n=128),
                    in_=self.in_w2[l][:, m * 128:(m + 1) * 128].rearrange(
                        "(kc p) n -> p kc n", p=128)),
                    writes=["stg2"], tag=f"ldw2_{m}")
                em.op("G", lambda g, p=p: g.tensor_copy(w2m[p][:], stg2[:]),
                      reads=["stg2"], writes=[f"w2m{p}"])
                for tb in range(2):
                    pwk = ["pE", "pF"][tb]
                    PW = self.PS[pwk]
                    for ca in range(NCA):
                        em.op("T", lambda t, ca=ca, tb=tb, p=p, PW=PW: t.matmul(
                            PW[:], w2m[p][:, ca * 128:(ca + 1) * 128],
                            hglu[:, ca * 1024 + tb * 512: ca * 1024 + (tb + 1) * 512],
                            start=(ca == 0), stop=(ca == NCA - 1)),
                            reads=[f"w2m{p}", f"hg{ca}"], writes=[pwk])
                    em.op("S", lambda s, m=m, l=l, PW=PW: s.activation(
                        w2s[:], PW[:], AF.Identity, bias=self.b2t[l][:, m:m + 1]),
                        reads=[pwk, f"b2t{l}"], writes=["w2s"])
                    c0 = 1 + H * 1024 + tb * 512
                    em.op("V", lambda v, m=m, c0=c0: v.tensor_tensor(
                        self.xc(m, c0, 512), w2s[:], self.xc(m, c0, 512), ALU.add),
                        reads=["w2s", f"x{m}"], writes=[f"x{m}"])
        if f"l{l}_out" in self.taps:
            self.tap_sbuf(f"l{l}_out", self.x, F32,
                          [f"x{c}" for c in range(8)] + [f"xh{c}" for c in range(8)])
        em.barrier()
        sc.close()

    def final_ln(self):
        nc, em, es = self.nc, self.em, self.es
        sc = ExitStack()
        xr = sc.enter_context(nc.sbuf_tensor("xr_all", [128, 8 * 2048], F32))
        otm = [sc.enter_context(nc.sbuf_tensor(f"otm{i}", [128, 1024], F32)) for i in range(2)]
        for c in range(8):
            em.dma(lambda q, c=c: q.dma_start(out=xr[:, c * 2048:(c + 1) * 2048],
                                              in_=self.d_xres1[c]),
                   reads=["d_xres1"], writes=["xr"], tag=f"xr{c}")
        blocks4 = [(1, 512), (513, 512), (1025, 512), (1537, 512)]

        def apply_lnf(bi, col0, wdt):
            for c in range(8):
                em.op("V", lambda v, c=c, col0=col0, wdt=wdt: v.tensor_tensor(
                    self.lnt1[:, :wdt], self.xc(c, col0, wdt), self.pE[:, :wdt], ALU.mult),
                    reads=[f"x{c}", "pE"], writes=["lnt1"])
                em.op("V", lambda v, wdt=wdt: v.tensor_tensor(
                    self.lnt1[:, :wdt], self.lnt1[:, :wdt], self.pF[:, :wdt], ALU.add),
                    reads=["lnt1", "pF"], writes=["lnt1"])
                em.op("V", lambda v, c=c, col0=col0, wdt=wdt: v.tensor_tensor(
                    self.xc(c, col0, wdt), self.lnt1[:, :wdt],
                    xr[:, c * 2048 + col0 - 1: c * 2048 + col0 - 1 + wdt], ALU.add),
                    reads=["lnt1", "xr"], writes=[f"x{c}"])

        self._ln_stats_blocks(True, blocks4, None, "lnf", apply_fn=apply_lnf)
        for i in range(16):
            p = i % 2
            for half in range(2):
                pk = ["pA", "pB"][half]
                PK = self.PS[pk]
                for j in range(4):
                    c = half * 4 + j
                    em.op("T", lambda t, c=c, j=j, i=i, PK=PK: t.matmul(
                        PK[:, j * 128:(j + 1) * 128], self.xc(c, 1 + i * 128, 128),
                        self.identf[:], start=(j == 0), stop=(j == 3),
                        is_transpose=True, skip_group_check=True),
                        reads=[f"x{c}", "identf"], writes=[pk])
                em.op("V", lambda v, p=p, half=half, PK=PK: v.tensor_copy(
                    otm[p][:, half * 512:(half + 1) * 512], PK[:]),
                    reads=[pk], writes=[f"otm{p}"])
            em.dma(lambda q, i=i, p=p: q.dma_start(
                out=self.out[i * 128:(i + 1) * 128, :], in_=otm[p][:]),
                reads=[f"otm{p}"], writes=["out"], tag=f"st{i}")
        sc.close()

    def build(self):
        self.declare()
        self.load_consts()
        self.step0()
        self.load_biases()
        if self.stop == "step0":
            return self.em.finalize(self.es, list(self.tap_names))
        self.attn_block()
        if self.stop == "attn":
            return self.em.finalize(self.es, list(self.tap_names))
        for l in range(2):
            self.layer(l)
            if self.stop == f"l{l}":
                return self.em.finalize(self.es, list(self.tap_names))
        self.final_ln()
        self.em.finalize(self.es, ["out"] + list(self.tap_names))


QK_SCALE = 8.0


def build_program(taps=(), stop="full"):
    nc = bass.Bass("TRN2", target_bir_lowering=False)
    es = ExitStack()
    prog = Prog(nc, es, taps, stop)
    prog.build()
    return nc, es, prog



def make_core_inputs(inputs, c):
    b, qd = divmod(c, 4)
    t0 = qd * T
    g = lambda k: np.asarray(inputs[k], np.float32)
    x = np.ascontiguousarray(g("L0_attn_x")[b, t0:t0 + T])
    xh = (np.zeros((1, D), np.float32) if t0 == 0
          else np.ascontiguousarray(g("L0_attn_x")[b, t0 - 1].reshape(1, D)))
    gate = (np.ones if t0 > 0 else np.zeros)((128, 1), np.float32)
    maskp = (np.zeros((128, 512), BF16NP) if t0 > 0
             else np.full((128, 512), NEGBIG, BF16NP))
    tpos = np.arange(t0, t0 + T, dtype=np.float64)
    half_idx = np.arange(0, DH, 2, dtype=np.float64)
    inv_freq = 1.0 / 10000.0 ** (half_idx / DH)
    freqs = tpos[:, None] * inv_freq[None, :]
    base = (half_idx + 0.4 * DH) / (1.4 * DH)
    power = (tpos - N // 2) / float(WINDOW // 2)
    scale = base[None, :] ** power[:, None]
    cosf, sinf = np.cos(freqs), np.sin(freqs)
    rotc = np.concatenate([cosf * scale, sinf * scale, cosf / scale, sinf / scale],
                          axis=1).astype(np.float32)
    d = dict(x=x, xh=xh, gate=gate, maskp=maskp, rotc=rotc)
    for w in ("Wq", "Wk", "Wv", "Wo"):
        d[w] = g("L0_attn_" + w)
    for w in ("bq", "bk", "bv", "bo"):
        d[w] = g("L0_attn_" + w).reshape(8, 128)
    for l in range(2):
        d[f"wqkv{l}"] = np.ascontiguousarray(g("L0_ffn_wqkv")[l])
        d[f"qs{l}"] = g("L0_ffn_q_scale")[l].reshape(1, DH)
        d[f"ks{l}"] = g("L0_ffn_k_scale")[l].reshape(1, DH)
        d[f"wof{l}"] = np.ascontiguousarray(g("L0_ffn_wo")[l])
        d[f"lg{l}"] = g("L0_ffn_ln_g")[l].reshape(8, 128)
        d[f"lb{l}"] = g("L0_ffn_ln_b")[l].reshape(8, 128)
        w1 = g("L0_ffn_w1")[l]
        w1a = np.zeros((D, HIDP), np.float32); w1a[:, :HID] = w1[:, :HID]
        w1g = np.zeros((D, HIDP), np.float32); w1g[:, :HID] = w1[:, HID:]
        d[f"w1a{l}"] = w1a
        d[f"w1g{l}"] = w1g
        b1 = g("L0_ffn_b1")[l]
        b1a = np.zeros((NCA * 128,), np.float32); b1a[:HID] = b1[:HID]
        b1g = np.zeros((NCA * 128,), np.float32); b1g[:HID] = b1[HID:]
        d[f"b1a{l}"] = b1a.reshape(NCA, 128)
        d[f"b1g{l}"] = b1g.reshape(NCA, 128)
        w2 = np.zeros((HIDP, D), np.float32); w2[:HID] = g("L0_ffn_w2")[l]
        d[f"w2{l}"] = w2
        d[f"b2{l}"] = g("L0_ffn_b2")[l].reshape(8, 128)
    return {k: np.ascontiguousarray(v if k == "maskp" else np.asarray(v, np.float32))
            for k, v in d.items()}


def kernel(**inputs):
    taps = tuple(t for t in os.environ.get("KTAPS", "").split(",") if t)
    stop = os.environ.get("KSTOP", "full")
    nc, es, prog = build_program(taps, stop)
    with es:
        in_maps = [make_core_inputs(inputs, c) for c in range(NCORES)]
        res = run_bass_kernel_spmd(nc, in_maps, core_ids=list(range(NCORES)))
    if taps or stop != "full":
        kernel.tap_results = res.results
        kernel.tap_names = prog.tap_names
    if stop != "full":
        return np.zeros((B, N, D), np.float32)
    outs = [res.results[c]["out"] for c in range(NCORES)]
    out = np.empty((B, N, D), np.float32)
    for c in range(NCORES):
        b, qd = divmod(c, 4)
        out[b, qd * T:(qd + 1) * T] = outs[c]
    return out


# version 15
# speedup vs baseline: 1.3632x; 1.3632x over previous
import os
import numpy as np
import ml_dtypes
from contextlib import ExitStack

import concourse.bass as bass
import concourse.mybir as mybir
from concourse.bass_utils import run_bass_kernel_spmd

F32 = mybir.dt.float32
BF16 = mybir.dt.bfloat16
AF = mybir.ActivationFunctionType
ALU = mybir.AluOpType
AX = mybir.AxisListType

NCORES = 8
B = 2
N = 8192
D = 1024
T = 2048
WCOL = T + 1
NH = 8
DH = 64
INNER = 512
WINDOW = 512
HID = 2730
NCA = 22
HIDP = NCA * 128
NEGBIG = -1.0e30
BF16NP = ml_dtypes.bfloat16

ENGINES = ("T", "V", "S", "G", "Q")



class Emitter:

    def __init__(self, nc):
        self.nc = nc
        self.ops = {e: [] for e in ENGINES}
        self.last_w = {}
        self.readers = {}
        self.pending_bar = {e: None for e in ENGINES}
        self.prolog = {}

    def _is_async(self, eng, idx):
        return self.ops[eng][idx]["kind"] in ("d", "cc")

    def _dep(self, eng, reads, writes):
        waits = {}

        def add(se, si):
            if se == eng and not self._is_async(se, si):
                self.ops[se][si]["same_dep"] = True
                return
            if si > waits.get(se, -1):
                waits[se] = si

        for k in reads:
            lw = self.last_w.get(k)
            if lw is not None:
                add(*lw)
        for k in writes:
            lw = self.last_w.get(k)
            if lw is not None:
                add(*lw)
            for se, si in self.readers.get(k, ()):
                add(se, si)
        if self.pending_bar[eng] is not None:
            for se, si in self.pending_bar[eng].items():
                add(se, si)
            self.pending_bar[eng] = None
        return waits

    def _record(self, eng, fn, reads, writes, kind, tag):
        waits = self._dep(eng, reads, writes)
        idx = len(self.ops[eng])
        self.ops[eng].append(dict(fn=fn, waits=waits, inc=False, kind=kind,
                                  tag=tag, own_sem=None, same_dep=False))
        for k in writes:
            self.last_w[k] = (eng, idx)
            self.readers[k] = []
        for k in reads:
            self.readers.setdefault(k, []).append((eng, idx))
        return (eng, idx)

    def op(self, eng, fn, reads=(), writes=(), tag=""):
        return self._record(eng, fn, reads, writes, "c", tag)

    def dma(self, fn, reads=(), writes=(), tag=""):
        return self._record("Q", fn, reads, writes, "d", tag)

    def coll(self, fn, reads=(), writes=(), tag=""):
        return self._record("G", fn, reads, writes, "cc", tag)

    def barrier(self):
        snap = {e: len(self.ops[e]) - 1 for e in ENGINES if self.ops[e]}
        for e in ENGINES:
            pb = {se: si for se, si in snap.items() if se != e}
            self.pending_bar[e] = pb

    def finalize(self, es, final_out_keys=()):
        nc = self.nc
        for e in ENGINES:
            for o in self.ops[e]:
                if o["same_dep"] and o["kind"] == "c":
                    o["inc"] = True
                for se, si in o["waits"].items():
                    self.ops[se][si]["inc"] = True
        for k in final_out_keys:
            lw = self.last_w.get(k)
            if lw is not None:
                self.ops[lw[0]][lw[1]]["inc"] = True
        for o in self.ops["Q"]:
            o["inc"] = True
        ncc = 0
        for e in ENGINES:
            c = 0
            for o in self.ops[e]:
                if o["kind"] == "cc":
                    o["cum"] = None
                    if o["inc"]:
                        o["own_sem"] = es.enter_context(nc.semaphore(f"cc{ncc}"))
                        ncc += 1
                else:
                    if o["inc"]:
                        c += 16 if o["kind"] == "d" else 1
                    o["cum"] = c
        sems = {e: es.enter_context(nc.semaphore(f"sem_{e}")) for e in ENGINES}
        self.sems = sems

        out_fence = []
        for k in final_out_keys:
            lw = self.last_w.get(k)
            if lw is not None:
                se, si = lw
                out_fence.append((se, self.ops[se][si]["cum"]))

        def make_body(e):
            def body(eng):
                if e in self.prolog:
                    self.prolog[e](eng)
                hwm = {}
                ccseen = set()
                for o in self.ops[e]:
                    for se in sorted(o["waits"]):
                        si = o["waits"][se]
                        src = self.ops[se][si]
                        if src["kind"] == "cc":
                            if (se, si) not in ccseen:
                                eng.wait_ge(src["own_sem"], 1)
                                ccseen.add((se, si))
                        else:
                            tgt = src["cum"]
                            if tgt > hwm.get(se, 0):
                                eng.wait_ge(sems[se], tgt)
                                hwm[se] = tgt
                    inst = o["fn"](eng)
                    if o["inc"]:
                        if o["kind"] == "cc":
                            inst.then_inc(o["own_sem"], 1)
                        else:
                            inst.then_inc(sems[e], 16 if o["kind"] == "d" else 1)
                if e == "Q":
                    for se, tgt in out_fence:
                        eng.wait_ge(sems[se], tgt)
            return body

        blk = es.enter_context(nc.Block())
        blk.sync(make_body("Q"))
        blk.tensor(make_body("T"))
        blk.vector(make_body("V"))
        blk.scalar(make_body("S"))
        blk.gpsimd(make_body("G"))

    def n_instructions(self):
        return sum(len(v) for v in self.ops.values())


def bc_free(tile, reps, offset=0, pcnt=128):
    a = tile.ap()
    return bass.AP(a.tensor, offset, [[a.ap[0][0], pcnt], [0, reps]])


def bc_free3(tile, offset, reps, inner, pcnt=128):
    a = tile.ap()
    return bass.AP(a.tensor, offset, [[a.ap[0][0], pcnt], [0, reps], [1, inner]])


def view3(ap, offset, step, count, inner, pcnt=128):
    return bass.AP(ap.tensor, offset, [[ap.ap[0][0], pcnt], [step, count], [1, inner]])



class Prog:
    def __init__(self, nc, es, taps, stop):
        self.nc, self.es = nc, es
        self.em = Emitter(nc)
        self.taps = set(taps)
        self.stop = stop
        self.tap_names = []
        self._par = 0

    def par(self):
        self._par += 1
        return self._par % 2

    def declare(self):
        nc, es = self.nc, self.es
        f = lambda name, shape: nc.dram_tensor(name, shape, F32, kind="ExternalInput").ap()
        self.in_x = f("x", [T, D])
        self.in_xh = f("xh", [1, D])
        self.in_rotc = f("rotc", [T, 128])
        self.in_gate = f("gate", [128, 1])
        self.in_maskp = nc.dram_tensor("maskp", [128, 512], BF16, kind="ExternalInput").ap()
        self.in_W = {w: f(w, [D, D]) for w in ("Wq", "Wk", "Wv", "Wo")}
        self.in_b = {w: f(w, [8, 128]) for w in ("bq", "bk", "bv", "bo")}
        self.in_wqkv = [f(f"wqkv{l}", [D, 3 * INNER]) for l in range(2)]
        self.in_qs = [f(f"qs{l}", [1, DH]) for l in range(2)]
        self.in_ks = [f(f"ks{l}", [1, DH]) for l in range(2)]
        self.in_wof = [f(f"wof{l}", [INNER, D]) for l in range(2)]
        self.in_lg = [f(f"lg{l}", [8, 128]) for l in range(2)]
        self.in_lb = [f(f"lb{l}", [8, 128]) for l in range(2)]
        self.in_w1a = [f(f"w1a{l}", [D, HIDP]) for l in range(2)]
        self.in_w1g = [f(f"w1g{l}", [D, HIDP]) for l in range(2)]
        self.in_b1a = [f(f"b1a{l}", [NCA, 128]) for l in range(2)]
        self.in_b1g = [f(f"b1g{l}", [NCA, 128]) for l in range(2)]
        self.in_w2 = [f(f"w2{l}", [HIDP, D]) for l in range(2)]
        self.in_b2 = [f(f"b2{l}", [8, 128]) for l in range(2)]
        self.out = nc.dram_tensor("out", [T, D], F32, kind="ExternalOutput").ap()

        self.d_xres1 = nc.dram_tensor("d_xres1", [8, 128, T], F32).ap()
        self.d_ags_in = nc.dram_tensor("d_ags_in", [2, D], F32).ap()
        self.d_ags_out = nc.dram_tensor("d_ags_out", [8, D], F32).ap()
        self.d_agkv_in = [nc.dram_tensor(f"d_agkv_in{l}", [1024, 512], BF16).ap() for l in range(2)]
        self.d_agkv_out = [nc.dram_tensor(f"d_agkv_out{l}", [4096, 512], BF16).ap() for l in range(2)]
        self.d_agx2_in = nc.dram_tensor("d_agx2_in", [D, 1], F32).ap()
        self.d_agx2_out = nc.dram_tensor("d_agx2_out", [4 * D, 1], F32).ap()
        self.d_agxb_in = [nc.dram_tensor(f"d_agxb_in{l}", [D, 1], F32).ap() for l in range(2)]
        self.d_agxb_out = [nc.dram_tensor(f"d_agxb_out{l}", [4 * D, 1], F32).ap() for l in range(2)]
        self.d_lh = nc.dram_tensor("d_lh", [1, D], BF16).ap()

        identf = nc.inline_tensor(np.eye(128, dtype=np.float32), "c_identf")
        identb = nc.inline_tensor(np.eye(128, dtype=BF16NP).view(np.uint16), "c_identb")
        ii, jj = np.arange(128)[:, None], np.arange(128)[None, :]
        msku = np.where(ii < jj, np.float32(NEGBIG), np.float32(0)).astype(np.float32)
        mskl = np.where(ii > jj, np.float32(NEGBIG), np.float32(0)).astype(np.float32)
        masku = nc.inline_tensor(msku, "c_masku")
        maskl = nc.inline_tensor(mskl, "c_maskl")
        maskub = nc.inline_tensor(msku.astype(BF16NP).view(np.uint16), "c_maskub")
        masklb = nc.inline_tensor(mskl.astype(BF16NP).view(np.uint16), "c_masklb")
        onesb = nc.inline_tensor(np.ones((128, 1), dtype=BF16NP).view(np.uint16), "c_onesb")
        onesrb = nc.inline_tensor(np.ones((1, 128), dtype=BF16NP).view(np.uint16), "c_onesrb")
        onesf = nc.inline_tensor(np.ones((1, 128), np.float32), "c_onesf")
        self.ACT_CONSTS = [1e-5, 1e-6, -1.0, -0.5, 8.0, 1.0 / D, 1.0, 0.0, 2.0, 1e-24]
        cval = nc.inline_tensor(np.broadcast_to(
            np.array(self.ACT_CONSTS, np.float32), (128, len(self.ACT_CONSTS))).copy(), "c_cval")
        self.c = dict(identf=identf.ap(), identb=identb.ap().bitcast(BF16),
                      masku=masku.ap(), maskl=maskl.ap(),
                      maskub=maskub.ap().bitcast(BF16), masklb=masklb.ap().bitcast(BF16),
                      onesb=onesb.ap().bitcast(BF16), onesrb=onesrb.ap().bitcast(BF16),
                      onesf=onesf.ap(), cval=cval.ap())

        sb = lambda name, shape, dt: es.enter_context(nc.sbuf_tensor(name, shape, dt))
        self.x = sb("x_fm", [128, 8 * WCOL], F32)
        self.hbuf = sb("hbuf", [128, 8 * WCOL], BF16)
        self.identf = sb("identf", [128, 128], F32)
        self.identb = sb("identb", [128, 128], BF16)
        self.masku = sb("masku", [128, 128], BF16)
        self.maskl = sb("maskl", [128, 128], BF16)
        self.maskpb = sb("maskpb", [128, 512], BF16)
        self.gate = sb("gate_s", [128, 1], F32)
        self.onesb = sb("onesb", [128, 1], BF16)
        self.onesrb = sb("onesrb", [1, 128], BF16)
        self.rowhi = sb("rowhi", [1, 512], BF16)
        self.rowlo = sb("rowlo", [1, 512], BF16)
        self.rowtmp = sb("rowtmp", [1, 512], F32)
        if self.taps:
            self._lndbg = sb("lndbg", [128, 8], F32)
            self._pbdbg = sb("pbdbg", [128, 144], BF16)
        self.cval = sb("cval_s", [128, len(self.ACT_CONSTS)], F32)
        for i, v in enumerate(self.ACT_CONSTS):
            try:
                nc.const_aps.register(F32, v, self.cval[:, i:i + 1])
            except Exception:
                pass
        self.bqt = sb("bqt", [128, 8], F32)
        self.nbqt = sb("nbqt", [128, 8], F32)
        self.bkt = sb("bkt", [128, 8], F32)
        self.nbkt = sb("nbkt", [128, 8], F32)
        self.bvt = sb("bvt", [128, 8], F32)
        self.bot = sb("bot", [128, 8], F32)
        self.lgt = [sb(f"lgt{l}", [128, 8], F32) for l in range(2)]
        self.lbt = [sb(f"lbt{l}", [128, 8], F32) for l in range(2)]
        self.b1at = [sb(f"b1at{l}", [128, NCA], F32) for l in range(2)]
        self.b1gt = [sb(f"b1gt{l}", [128, NCA], F32) for l in range(2)]
        self.b2t = [sb(f"b2t{l}", [128, 8], F32) for l in range(2)]
        self.kvtot = sb("kvtot", [128, 8], F32)
        self.kstot = sb("kstot", [128, 8], F32)
        self.qsrep = sb("qsrep", [128, DH], F32)
        self.ksrep = sb("ksrep", [128, DH], F32)
        self.strow = [sb(f"strow{i}", [1, 512], F32) for i in range(4)]
        self.pp = [sb(f"pp{i}", [128, 8], F32) for i in range(4)]
        self.lncast = sb("lncast", [128, 512], BF16)
        self.lnsq = sb("lnsq", [128, 512], BF16)
        self.lnt1 = sb("lnt1", [128, 512], F32)

        ps = lambda name, shape, dt: es.enter_context(nc.psum_tensor(name, shape, dt))
        self.pAB = ps("pAB", [128, 1024], F32)
        self.pCD = ps("pCD", [128, 1024], F32)
        self.pA = self.pAB.ap()[:, 0:512]
        self.pB = self.pAB.ap()[:, 512:1024]
        self.pC = self.pCD.ap()[:, 0:512]
        self.pD = self.pCD.ap()[:, 512:1024]
        self.pE = ps("pE", [128, 512], F32).ap()
        self.pF = ps("pF", [128, 512], F32).ap()
        self.pT1 = ps("pT1", [128, 1024], BF16)
        self.pT2 = ps("pT2", [128, 1024], BF16)
        self.PS = {"pA": self.pA, "pB": self.pB, "pC": self.pC, "pD": self.pD,
                   "pE": self.pE, "pF": self.pF, "pT1": self.pT1, "pT2": self.pT2,
                   "pABfull": self.pAB.ap(), "pCDfull": self.pCD.ap()}

        self.rt = {}

        def qprolog(eng):
            pid = eng.partition_id()
            self.rt["rr"] = (pid + 3) % 4
        self.em.prolog["Q"] = qprolog

    def xc(self, c, col0, w):
        return self.x[:, c * WCOL + col0: c * WCOL + col0 + w]

    def hc(self, c, col0, w):
        return self.hbuf[:, c * WCOL + col0: c * WCOL + col0 + w]

    def xkeys(self, c, col0, w=None):
        ks = [f"x{c}"]
        if col0 == 0:
            ks.append(f"xh{c}")
        return ks

    def hkeys(self, c, col0, w=None):
        ks = [f"h{c}"]
        if col0 == 0:
            ks.append(f"hh{c}")
        return ks

    def tap_sbuf(self, name, sbt, dt, keys):
        if name not in self.taps:
            return
        nc, em = self.nc, self.em
        out = nc.dram_tensor("tap_" + name, [sbt.shape[0], sbt.shape[1]], dt,
                             kind="ExternalOutput").ap()
        em.dma(lambda q: q.dma_start(out=out, in_=sbt[:]), reads=list(keys),
               writes=["tap_" + name], tag=f"tap_{name}")
        self.tap_names.append("tap_" + name)

    def load_consts(self):
        em = self.em
        for nm in ("identf", "identb", "onesb", "onesrb"):
            dst = getattr(self, nm)
            em.dma(lambda q, dst=dst, nm=nm: q.dma_start(out=dst[:], in_=self.c[nm]),
                   writes=[nm], tag=f"c_{nm}")
        em.dma(lambda q: q.dma_start(out=self.masku[:], in_=self.c["maskub"]), writes=["masku"])
        em.dma(lambda q: q.dma_start(out=self.maskl[:], in_=self.c["masklb"]), writes=["maskl"])
        em.dma(lambda q: q.dma_start(out=self.maskpb[:], in_=self.in_maskp), writes=["maskpb"])
        em.dma(lambda q: q.dma_start(out=self.cval[:], in_=self.c["cval"]), writes=["cval"])
        em.dma(lambda q: q.dma_start(out=self.gate[:], in_=self.in_gate), writes=["gate"])

        self.em.barrier()
        for i in range(160):
            em.op("T", lambda t: t.matmul(
                self.pE[:, 0:512], self.identb[:], self.hbuf[:, 0:512],
                start=True, stop=True, skip_group_check=True),
                reads=["identb"], writes=["warmup"], tag=f"warm{i}")

    def load_biases(self):
        em = self.em

        def colload(dst, src, ncol, key):
            for c in range(ncol):
                self.em.dma(lambda q, c=c, dst=dst, src=src: q.dma_start(
                    out=dst[:, c:c + 1], in_=src[c:c + 1, :].rearrange("a p -> p a")),
                    writes=[key], tag=f"{key}{c}")

        colload(self.bqt, self.in_b["bq"], 8, "bqt")
        colload(self.bkt, self.in_b["bk"], 8, "bkt")
        colload(self.bvt, self.in_b["bv"], 8, "bvt")
        colload(self.bot, self.in_b["bo"], 8, "bot")
        for l in range(2):
            colload(self.lgt[l], self.in_lg[l], 8, f"lgt{l}")
            colload(self.lbt[l], self.in_lb[l], 8, f"lbt{l}")
            colload(self.b2t[l], self.in_b2[l], 8, f"b2t{l}")
            colload(self.b1at[l], self.in_b1a[l], NCA, f"b1at{l}")
            colload(self.b1gt[l], self.in_b1g[l], NCA, f"b1gt{l}")
        em.op("V", lambda v: v.tensor_single_scalar(self.nbqt[:], self.bqt[:], -1.0, ALU.mult),
              reads=["bqt"], writes=["nbqt"])
        em.op("V", lambda v: v.tensor_single_scalar(self.nbkt[:], self.bkt[:], -1.0, ALU.mult),
              reads=["bkt"], writes=["nbkt"])
        em.dma(lambda q: q.dma_start(out=self.cval[:], in_=self.c["cval"]), writes=["cval"])

    def emit_replicate(self, row, rowk, ps, psk, wdt, tag, scale=None):
        em = self.em
        src = row
        if scale is not None:
            em.op("S", lambda s: s.activation(self.rowtmp[0:1, :wdt], row[0:1, :wdt],
                                              AF.Copy, scale=scale),
                  reads=[rowk], writes=["rowtmp"], tag=tag + "sc")
            src = self.rowtmp
            rowk = "rowtmp"
        em.op("S", lambda s: s.copy(self.rowhi[0:1, :wdt], src[0:1, :wdt]),
              reads=[rowk], writes=["rowhi"], tag=tag + "hi")
        em.op("V", lambda v: v.tensor_tensor(self.rowtmp[0:1, :wdt], src[0:1, :wdt],
                                             self.rowhi[0:1, :wdt], ALU.subtract),
              reads=[rowk, "rowhi"], writes=["rowtmp"], tag=tag + "sub")
        em.op("S", lambda s: s.copy(self.rowlo[0:1, :wdt], self.rowtmp[0:1, :wdt]),
              reads=["rowtmp"], writes=["rowlo"], tag=tag + "lo")
        em.op("T", lambda t: t.matmul(ps[:, :wdt], self.onesrb[:], self.rowhi[0:1, :wdt],
                                      start=True, stop=False, skip_group_check=True),
              reads=["rowhi", "onesrb"], writes=[psk], tag=tag + "mm1")
        em.op("T", lambda t: t.matmul(ps[:, :wdt], self.onesrb[:], self.rowlo[0:1, :wdt],
                                      start=False, stop=True, skip_group_check=True),
              reads=["rowlo", "onesrb"], writes=[psk], tag=tag + "mm2")

    def _rowstats(self, em, pSum, kSum, pSq, kSq, wdt, eps, rstd_row, nmr_row, krow, tag):
        em.op("S", lambda s: s.activation(rstd_row[0:1, :wdt], pSum[0:1, :wdt], AF.Copy,
                                          scale=1.0 / D),
              reads=[kSum], writes=[krow], tag=tag + "_mu")
        em.op("S", lambda s: s.activation(nmr_row[0:1, :wdt], rstd_row[0:1, :wdt], AF.Square),
              reads=[krow], writes=[krow])
        em.op("V", lambda v: v.scalar_tensor_tensor(nmr_row[0:1, :wdt], pSq[0:1, :wdt],
                                                    1.0 / D, nmr_row[0:1, :wdt],
                                                    ALU.mult, ALU.subtract),
              reads=[kSq, krow], writes=[krow])
        em.op("S", lambda s: s.activation(nmr_row[0:1, :wdt], nmr_row[0:1, :wdt], AF.Ln,
                                          bias=eps),
              reads=[krow], writes=[krow])
        em.op("S", lambda s: s.activation(nmr_row[0:1, :wdt], nmr_row[0:1, :wdt], AF.Exp,
                                          scale=-0.5),
              reads=[krow], writes=[krow])
        em.op("V", lambda v: v.scalar_tensor_tensor(rstd_row[0:1, :wdt], rstd_row[0:1, :wdt],
                                                    -1.0, nmr_row[0:1, :wdt],
                                                    ALU.mult, ALU.mult),
              reads=[krow], writes=[krow])

    def _ln_stats_blocks(self, src_is_x, blocks, layer_shift, tag, apply_fn=None):
        em = self.em
        pend = [None]

        def flush():
            if pend[0] is not None:
                pend[0]()
                pend[0] = None

        out = []
        for bi, (col0, wdt) in enumerate(blocks):
            par = bi % 2
            kSa, kSb = [("pC", "pD"), ("pA", "pB")][par]
            pSa, pSb = self.PS[kSa], self.PS[kSb]
            r1, r2 = self.strow[2 * par], self.strow[2 * par + 1]
            krow = f"strow{par}{tag}"
            for c in range(8):
                if src_is_x:
                    sc0 = col0 - 1 if (layer_shift and c >= 4) else col0
                    srck = self.xkeys(c, sc0)
                    src_ap = self.xc(c, sc0, wdt)
                    em.op("G", lambda g, src_ap=src_ap, wdt=wdt: g.tensor_copy(
                        self.lncast[:, :wdt], src_ap), reads=srck, writes=["lncast"],
                        tag=tag + "cast")
                    data = self.lncast[:, :wdt]
                    datak = ["lncast"]
                    em.op("S", lambda s, src_ap=src_ap, wdt=wdt: s.activation(
                        self.lnsq[:, :wdt], src_ap, AF.Square), reads=srck,
                        writes=["lnsq"], tag=tag + "sq")
                else:
                    srck = self.hkeys(c, col0)
                    data = self.hc(c, col0, wdt)
                    datak = srck
                    em.op("S", lambda s, c=c, col0=col0, wdt=wdt: s.activation(
                        self.lnsq[:, :wdt], self.hc(c, col0, wdt), AF.Square),
                        reads=srck, writes=["lnsq"], tag=tag + "sq")
                em.op("T", lambda t, data=data, wdt=wdt, c=c, pSa=pSa: t.matmul(
                    pSa[0:1, :wdt], self.onesb[:], data,
                    start=(c == 0), stop=(c == 7)),
                    reads=datak + ["onesb"], writes=[kSa], tag=tag + "mmS")
                em.op("T", lambda t, wdt=wdt, c=c, pSb=pSb: t.matmul(
                    pSb[0:1, :wdt], self.onesb[:], self.lnsq[:, :wdt],
                    start=(c == 0), stop=(c == 7)),
                    reads=["lnsq", "onesb"], writes=[kSb], tag=tag + "mmQ")
            self._rowstats(em, pSa, kSa, pSb, kSb, wdt, 1e-5, r1, r2, krow, tag)
            flush()

            def tail(bi=bi, col0=col0, wdt=wdt, r1=r1, r2=r2, krow=krow):
                self.emit_replicate(r2, krow, self.pE, "pE", wdt, tag + "rep1")
                self.emit_replicate(r1, krow, self.pF, "pF", wdt, tag + "rep2")
                if apply_fn is not None:
                    apply_fn(bi, col0, wdt)
            if apply_fn is None:
                tail()
                out.append((bi, col0, wdt))
            else:
                pend[0] = tail
        flush()
        if apply_fn is None:
            return out
        return None

    def step0(self):
        nc, em, es = self.nc, self.em, self.es
        sc = ExitStack()
        xt = [sc.enter_context(nc.sbuf_tensor(f"xt{i}", [128, 1024], F32)) for i in range(2)]
        lt = [sc.enter_context(nc.sbuf_tensor(f"lt{i}", [128, 1024], BF16)) for i in range(2)]
        sq0 = [sc.enter_context(nc.sbuf_tensor(f"sq0{i}", [128, 1024], F32)) for i in range(2)]
        st0 = [sc.enter_context(nc.sbuf_tensor(f"st0{i}", [128, 6], F32)) for i in range(2)]
        sth = sc.enter_context(nc.sbuf_tensor("sth", [1, 6], F32))
        self.xrow = sc.enter_context(nc.sbuf_tensor("xrow", [1, 1024], F32))
        sqh = sc.enter_context(nc.sbuf_tensor("sqh", [1, 1024], F32))
        lh = sc.enter_context(nc.sbuf_tensor("lh", [1, 1024], BF16))

        def tok_stats(src_ap, sq_t, st_t, pcnt, kin, kst):
            em.op("S", lambda s: s.activation(sq_t[:pcnt, :], src_ap, AF.Square),
                  reads=[kin], writes=[kst + "sq"])
            em.op("V", lambda v: v.tensor_reduce(st_t[:pcnt, 0:1], src_ap, AX.X, ALU.add),
                  reads=[kin], writes=[kst])
            em.op("V", lambda v: v.tensor_reduce(st_t[:pcnt, 1:2], sq_t[:pcnt, :], AX.X, ALU.add),
                  reads=[kst + "sq"], writes=[kst])
            em.op("V", lambda v: v.tensor_single_scalar(st_t[:pcnt, 2:3], st_t[:pcnt, 0:1],
                                                        1.0 / D, ALU.mult),
                  reads=[kst], writes=[kst])
            em.op("S", lambda s: s.activation(st_t[:pcnt, 3:4], st_t[:pcnt, 2:3], AF.Square),
                  reads=[kst], writes=[kst])
            em.op("V", lambda v: v.scalar_tensor_tensor(st_t[:pcnt, 3:4], st_t[:pcnt, 1:2],
                                                        1.0 / D, st_t[:pcnt, 3:4],
                                                        ALU.mult, ALU.subtract),
                  reads=[kst], writes=[kst])
            em.op("S", lambda s: s.activation(st_t[:pcnt, 3:4], st_t[:pcnt, 3:4], AF.Ln,
                                              bias=1e-5),
                  reads=[kst], writes=[kst])
            em.op("S", lambda s: s.activation(st_t[:pcnt, 4:5], st_t[:pcnt, 3:4], AF.Exp,
                                              scale=-0.5),
                  reads=[kst], writes=[kst])
            em.op("V", lambda v: v.scalar_tensor_tensor(st_t[:pcnt, 5:6], st_t[:pcnt, 2:3],
                                                        -1.0, st_t[:pcnt, 4:5],
                                                        ALU.mult, ALU.mult),
                  reads=[kst], writes=[kst])

        em.dma(lambda q: q.dma_start(out=self.xrow[:], in_=self.in_xh), writes=["xrow"])
        tok_stats(self.xrow[0:1, :], sqh, sth, 1, "xrow", "sth")
        em.op("S", lambda s: s.activation(lh[:], self.xrow[0:1, :], AF.Identity,
                                          scale=sth[0:1, 4:5], bias=sth[0:1, 5:6]),
              reads=["xrow", "sth"], writes=["lhrow"])
        em.dma(lambda q: q.dma_start(out=self.d_lh, in_=lh[:]), reads=["lhrow"],
               writes=["d_lh"])
        for c in range(8):
            em.dma(lambda q, c=c: q.dma_start(
                out=self.xc(c, 0, 1),
                in_=self.in_xh[0:1, c * 128:(c + 1) * 128].rearrange("a p -> p a")),
                writes=[f"xh{c}"], tag=f"xh{c}")
            em.dma(lambda q, c=c: q.dma_start(
                out=self.hc(c, 0, 1),
                in_=self.d_lh[0:1, c * 128:(c + 1) * 128].rearrange("a p -> p a")),
                reads=["d_lh"], writes=[f"hh{c}"], tag=f"lh{c}")

        for i in range(16):
            p = i % 2
            kx, kl, kst = f"xt{p}", f"lt{p}", f"st{p}"
            em.dma(lambda q, i=i, p=p: q.dma_start(
                out=xt[p][:], in_=self.in_x[i * 128:(i + 1) * 128, :]),
                writes=[kx], tag=f"ldx{i}")
            tok_stats(xt[p][:, :], sq0[p], st0[p], 128, kx, kst)
            em.op("S", lambda s, p=p: s.activation(lt[p][:], xt[p][:], AF.Identity,
                                                   scale=st0[p][:, 4:5], bias=st0[p][:, 5:6]),
                  reads=[kx, kst], writes=[kl])
            for half in range(2):
                pk = "pA" if half == 0 else "pB"
                pst = self.PS[pk]
                for j in range(4):
                    c = half * 4 + j
                    em.op("T", lambda t, p=p, c=c, j=j, pst=pst: t.matmul(
                        pst[:, j * 128:(j + 1) * 128], xt[p][:, c * 128:(c + 1) * 128],
                        self.identf[:], start=(j == 0), stop=(j == 3),
                        is_transpose=True, skip_group_check=True),
                        reads=[kx, "identf"], writes=[pk])
                dst = view3(self.x[:], half * 4 * WCOL + 1 + i * 128, WCOL, 4, 128)
                em.op("V", lambda v, pst=pst, dst=dst: v.tensor_copy(dst, pst[:]),
                      reads=[pk], writes=[f"x{half * 4 + j2}" for j2 in range(4)])
            ptk = "pT1" if p == 0 else "pT2"
            pt = self.PS[ptk]
            for c in range(8):
                em.op("T", lambda t, p=p, c=c, pt=pt: t.matmul(
                    pt[:, c * 128:(c + 1) * 128], lt[p][:, c * 128:(c + 1) * 128],
                    self.identb[:], start=(c == 0), stop=(c == 7),
                    is_transpose=True, skip_group_check=True),
                    reads=[kl, "identb"], writes=[ptk])
            dsth = view3(self.hbuf[:], 1 + i * 128, WCOL, 8, 128)
            em.op("S", lambda s, pt=pt, dsth=dsth: s.copy(dsth, pt[:]),
                  reads=[ptk], writes=[f"h{c2}" for c2 in range(8)])
        self.tap_sbuf("x_fm", self.x, F32,
                      [f"x{c}" for c in range(8)] + [f"xh{c}" for c in range(8)])
        self.tap_sbuf("ln1", self.hbuf, BF16,
                      [f"h{c}" for c in range(8)] + [f"hh{c}" for c in range(8)])
        em.barrier()
        sc.close()

    def attn_block(self):
        nc, em, es = self.nc, self.em, self.es
        sc = ExitStack()
        wA = sc.enter_context(nc.sbuf_tensor("wA", [128, 8 * 1024], BF16))
        wB = sc.enter_context(nc.sbuf_tensor("wB", [128, 8 * 1024], BF16))
        stg = [sc.enter_context(nc.sbuf_tensor(f"stg{i}", [128, 1024], F32)) for i in range(2)]
        attnv = sc.enter_context(nc.sbuf_tensor("attnv", [128, 8 * WCOL], BF16))
        scr = [sc.enter_context(nc.sbuf_tensor(f"scrA{i}", [128, 512], F32)) for i in range(6)]
        kvacc = sc.enter_context(nc.sbuf_tensor("kvacc", [128, 32], F32))
        ksacc = sc.enter_context(nc.sbuf_tensor("ksacc", [128, 32], F32))

        def load_w(src, dst, dkey, tag):
            for kc in range(8):
                p = kc % 2
                em.dma(lambda q, kc=kc, p=p, src=src: q.dma_start(
                    out=stg[p][:], in_=src[kc * 128:(kc + 1) * 128, :]),
                    writes=[f"stg{p}"], tag=f"{tag}{kc}")
                em.op("G", lambda g, kc=kc, p=p, dst=dst: g.tensor_copy(
                    dst[:, kc * 1024:(kc + 1) * 1024], stg[p][:]),
                    reads=[f"stg{p}"], writes=[dkey], tag=f"{tag}c{kc}")

        def fmap(ps, pk, bias, nbias, biask, s1, k1, s2, k2, outt, outk, wdt):
            em.op("S", lambda s: s.activation(s1[:, :wdt], ps[:, :wdt], AF.Relu, bias=bias),
                  reads=[pk, biask], writes=[k1])
            em.op("S", lambda s: s.activation(s2[:, :wdt], ps[:, :wdt], AF.Relu,
                                              bias=nbias, scale=-1.0),
                  reads=[pk, "n" + biask], writes=[k2])
            em.op("S", lambda s: s.activation(s2[:, :wdt], s2[:, :wdt], AF.Exp, scale=-1.0),
                  reads=[k2], writes=[k2])
            em.op("G", lambda g: g.tensor_add(outt[:, :wdt], s1[:, :wdt], s2[:, :wdt]),
                  reads=[k1, k2], writes=[outk])

        load_w(self.in_W["Wk"], wA, "wA", "ldWk")
        load_w(self.in_W["Wv"], wB, "wB", "ldWv")

        for m in range(8):
            for cb in range(4):
                col0 = 1 + cb * 512
                pkk = "pA" if cb % 2 == 0 else "pB"
                pvk = "pC" if cb % 2 == 0 else "pD"
                pk, pv = self.PS[pkk], self.PS[pvk]
                for kc in range(8):
                    em.op("T", lambda t, m=m, kc=kc, col0=col0, pk=pk: t.matmul(
                        pk[:], wA[:, kc * 1024 + m * 128: kc * 1024 + (m + 1) * 128],
                        self.hc(kc, col0, 512), start=(kc == 0), stop=(kc == 7)),
                        reads=["wA", f"h{kc}"], writes=[pkk])
                for kc in range(8):
                    em.op("T", lambda t, m=m, kc=kc, col0=col0, pv=pv: t.matmul(
                        pv[:], wB[:, kc * 1024 + m * 128: kc * 1024 + (m + 1) * 128],
                        self.hc(kc, col0, 512), start=(kc == 0), stop=(kc == 7)),
                        reads=["wB", f"h{kc}"], writes=[pvk])
                fmap(pk, pkk, self.bkt[:, m:m + 1], self.nbkt[:, m:m + 1], "bkt",
                     scr[1], "scr1", scr[2], "scr2", scr[0], "scr0", 512)
                em.op("S", lambda s, pv=pv, m=m: s.activation(
                    scr[3][:], pv[:], AF.Identity, bias=self.bvt[:, m:m + 1]),
                    reads=[pvk, "bvt"], writes=["scr3"])
                em.op("G", lambda g: g.tensor_mul(scr[4][:], scr[0][:], scr[3][:]),
                      reads=["scr0", "scr3"], writes=["scr4"])
                em.op("V", lambda v, m=m, cb=cb: v.tensor_reduce(
                    kvacc[:, m * 4 + cb: m * 4 + cb + 1], scr[4][:], AX.X, ALU.add),
                    reads=["scr4"], writes=["kvacc"])
                em.op("V", lambda v, m=m, cb=cb: v.tensor_reduce(
                    ksacc[:, m * 4 + cb: m * 4 + cb + 1], scr[0][:], AX.X, ALU.add),
                    reads=["scr0"], writes=["ksacc"])
        em.op("V", lambda v: v.tensor_reduce(self.pp[0][:, 0:8],
                                             view3(kvacc[:], 0, 4, 8, 4), AX.X, ALU.add),
              reads=["kvacc"], writes=["pp0"])
        em.op("V", lambda v: v.tensor_reduce(self.pp[1][:, 0:8],
                                             view3(ksacc[:], 0, 4, 8, 4), AX.X, ALU.add),
              reads=["ksacc"], writes=["pp1"])
        for m in range(8):
            em.dma(lambda q, m=m: q.dma_start(
                out=self.d_ags_in[0:1, m * 128:(m + 1) * 128].rearrange("a p -> p a"),
                in_=self.pp[0][:, m:m + 1]),
                reads=["pp0"], writes=["d_ags_in"])
            em.dma(lambda q, m=m: q.dma_start(
                out=self.d_ags_in[1:2, m * 128:(m + 1) * 128].rearrange("a p -> p a"),
                in_=self.pp[1][:, m:m + 1]),
                reads=["pp1"], writes=["d_ags_in"])
        em.coll(lambda g: g.collective_compute(
            "AllGather", ALU.bypass, replica_groups=[[0, 1, 2, 3], [4, 5, 6, 7]],
            ins=[self.d_ags_in], outs=[self.d_ags_out]),
            reads=["d_ags_in"], writes=["d_ags_out"], tag="AGsums")
        load_w(self.in_W["Wq"], wA, "wA", "ldWq")
        for m in range(8):
            for r in range(8):
                em.dma(lambda q, m=m, r=r: q.dma_start(
                    out=self.pp[2][:, r:r + 1],
                    in_=self.d_ags_out[r:r + 1, m * 128:(m + 1) * 128].rearrange("a p -> p a")),
                    reads=["d_ags_out"], writes=["pp2"], tag=f"agr{m}_{r}")
            pstep = self.pp[2].ap().ap[0][0]
            kvv = bass.AP(self.pp[2].ap().tensor, 0, [[pstep, 128], [2, 4]])
            ksv = bass.AP(self.pp[2].ap().tensor, 1, [[pstep, 128], [2, 4]])
            em.op("V", lambda v, m=m, kvv=kvv: v.tensor_reduce(
                self.kvtot[:, m:m + 1], kvv, AX.X, ALU.add),
                reads=["pp2"], writes=["kvtot"])
            em.op("V", lambda v, m=m, ksv=ksv: v.tensor_reduce(
                self.kstot[:, m:m + 1], ksv, AX.X, ALU.add),
                reads=["pp2"], writes=["kstot"])
        blocks5 = [(0, 512), (512, 512), (1024, 512), (1536, 512), (2048, 1)]
        for m in range(8):
            for bi, (col0, wdt) in enumerate(blocks5):
                pqk = "pA" if bi % 2 == 0 else "pB"
                pq = self.PS[pqk]
                for kc in range(8):
                    em.op("T", lambda t, m=m, kc=kc, col0=col0, wdt=wdt, pq=pq: t.matmul(
                        pq[:, :wdt], wA[:, kc * 1024 + m * 128: kc * 1024 + (m + 1) * 128],
                        self.hc(kc, col0, wdt), start=(kc == 0), stop=(kc == 7)),
                        reads=["wA"] + self.hkeys(kc, col0), writes=[pqk])
                fmap(pq, pqk, self.bqt[:, m:m + 1], self.nbqt[:, m:m + 1], "bqt",
                     scr[1], "scr1", scr[2], "scr2", scr[5], "scr5", wdt)
                em.op("G", lambda g, m=m, wdt=wdt: g.tensor_tensor(
                    scr[3][:, :wdt], scr[5][:, :wdt], bc_free(self.kstot, wdt, m), ALU.mult),
                    reads=["scr5", "kstot"], writes=["scr3"])
                em.op("S", lambda s, wdt=wdt: s.activation(scr[3][:, :wdt], scr[3][:, :wdt],
                                                           AF.Ln, bias=1e-6),
                      reads=["scr3"], writes=["scr3"])
                em.op("S", lambda s, wdt=wdt: s.activation(scr[3][:, :wdt], scr[3][:, :wdt],
                                                           AF.Exp, scale=-1.0),
                      reads=["scr3"], writes=["scr3"])
                em.op("G", lambda g, m=m, wdt=wdt: g.tensor_tensor(
                    scr[4][:, :wdt], scr[5][:, :wdt], bc_free(self.kvtot, wdt, m), ALU.mult),
                    reads=["scr5", "kvtot"], writes=["scr4"])
                em.op("V", lambda v, m=m, col0=col0, wdt=wdt: v.tensor_tensor(
                    attnv[:, m * WCOL + col0: m * WCOL + col0 + wdt],
                    scr[4][:, :wdt], scr[3][:, :wdt], ALU.mult),
                    reads=["scr3", "scr4"], writes=[f"av{m}"])
        self.tap_sbuf("attn_v", attnv, BF16, [f"av{m}" for m in range(8)])

        load_w(self.in_W["Wo"], wB, "wB", "ldWo")
        for bi, (col0, wdt) in enumerate(blocks5):
            for m in range(8):
                pwk = "pA" if (bi * 8 + m) % 2 == 0 else "pB"
                pw = self.PS[pwk]
                for kc in range(8):
                    em.op("T", lambda t, m=m, kc=kc, col0=col0, wdt=wdt, pw=pw: t.matmul(
                        pw[:, :wdt], wB[:, kc * 1024 + m * 128: kc * 1024 + (m + 1) * 128],
                        attnv[:, kc * WCOL + col0: kc * WCOL + col0 + wdt],
                        start=(kc == 0), stop=(kc == 7)),
                        reads=["wB", f"av{kc}"], writes=[pwk])
                wk = [f"h{m}"] + ([f"hh{m}"] if col0 == 0 else [])
                em.op("S", lambda s, m=m, col0=col0, wdt=wdt, pw=pw: s.activation(
                    self.hc(m, col0, wdt), pw[:, :wdt], AF.Gelu_apprx_tanh,
                    bias=self.bot[:, m:m + 1]),
                    reads=[pwk, "bot"], writes=wk)
        self.tap_sbuf("res", self.hbuf, BF16,
                      [f"h{c}" for c in range(8)] + [f"hh{c}" for c in range(8)])

        blocks5o = [(0, 512), (512, 512), (1024, 512), (1536, 512), (1537, 512)]

        def apply_lnres(bi, col0, wdt):
            ov = (bi == 4)
            for c in range(8):
                em.op("V", lambda v, c=c, col0=col0, wdt=wdt: v.tensor_tensor(
                    self.lnt1[:, :wdt], self.hc(c, col0, wdt), self.pE[:, :wdt], ALU.mult),
                    reads=self.hkeys(c, col0) + ["pE"], writes=["lnt1"])
                em.op("V", lambda v, wdt=wdt: v.tensor_tensor(
                    self.lnt1[:, :wdt], self.lnt1[:, :wdt], self.pF[:, :wdt], ALU.add),
                    reads=["lnt1", "pF"], writes=["lnt1"])
                wk = self.xkeys(c, col0)
                if not ov:
                    em.op("V", lambda v, c=c, col0=col0, wdt=wdt: v.tensor_tensor(
                        self.xc(c, col0, wdt), self.xc(c, col0, wdt), self.lnt1[:, :wdt],
                        ALU.add),
                        reads=["lnt1"] + wk, writes=wk)
                else:
                    em.op("S", lambda s_, c=c: s_.activation(
                        self.xc(c, 2048, 1), self.lnt1[:, 511:512], AF.Identity,
                        bias=self.xc(c, 2048, 1)),
                        reads=["lnt1", f"x{c}"], writes=[f"x{c}"])

        self._ln_stats_blocks(False, blocks5o, None, "lnres", apply_fn=apply_lnres)
        self.tap_sbuf("x_res1", self.x, F32,
                      [f"x{c}" for c in range(8)] + [f"xh{c}" for c in range(8)])
        for c in range(8):
            em.dma(lambda q, c=c: q.dma_start(out=self.d_xres1[c], in_=self.xc(c, 1, T)),
                   reads=[f"x{c}"], writes=["d_xres1"], tag=f"spill{c}")
        blocks5x = [(1537, 512), (0, 512), (512, 512), (1024, 512), (1536, 512)]

        def apply_lnx1(bi, col0, wdt):
            ov = (bi == 0)
            for c in range(8):
                rk = self.xkeys(c, col0)
                em.op("V", lambda v, c=c, col0=col0, wdt=wdt: v.tensor_tensor(
                    self.lnt1[:, :wdt], self.xc(c, col0, wdt), self.pE[:, :wdt], ALU.mult),
                    reads=rk + ["pE"], writes=["lnt1"])
                if not ov:
                    em.op("V", lambda v, c=c, col0=col0, wdt=wdt: v.tensor_tensor(
                        self.xc(c, col0, wdt), self.lnt1[:, :wdt], self.pF[:, :wdt], ALU.add),
                        reads=["lnt1", "pF"], writes=rk)
                else:
                    em.op("V", lambda v, wdt=wdt: v.tensor_tensor(
                        self.lnt1[:, :wdt], self.lnt1[:, :wdt], self.pF[:, :wdt], ALU.add),
                        reads=["lnt1", "pF"], writes=["lnt1"])
                    em.op("S", lambda s_, c=c: s_.copy(
                        self.xc(c, 2048, 1), self.lnt1[:, 511:512]),
                        reads=["lnt1"], writes=[f"x{c}"])

        self._ln_stats_blocks(True, blocks5x, None, "lnx1", apply_fn=apply_lnx1)
        for c in range(8):
            em.op("S", lambda s_, c=c: s_.activation(
                self.xc(c, 0, 1), self.xc(c, 0, 1), AF.Identity, scale=self.gate[:]),
                reads=[f"xh{c}", "gate"], writes=[f"xh{c}"])
        self.tap_sbuf("x1", self.x, F32,
                      [f"x{c}" for c in range(8)] + [f"xh{c}" for c in range(8)])
        em.barrier()
        sc.close()

    def layer(self, l):
        nc, em, es = self.nc, self.em, self.es
        sc_outer = ExitStack()
        q_fm = sc_outer.enter_context(nc.sbuf_tensor(f"q_fm{l}", [128, 4 * 2048], BF16))
        k_fm = sc_outer.enter_context(nc.sbuf_tensor(f"k_fm{l}", [128, 4 * 2560], BF16))
        v_tm = sc_outer.enter_context(nc.sbuf_tensor(f"v_tm{l}", [128, 20 * 512], BF16))
        qlo_fm = sc_outer.enter_context(nc.sbuf_tensor(f"qlo_fm{l}", [128, 512], BF16))
        klo_fm = sc_outer.enter_context(nc.sbuf_tensor(f"klo_fm{l}", [128, 512], BF16))
        AG = [[0, 1, 2, 3], [4, 5, 6, 7]]

        scA = ExitStack()
        self.rotc = scA.enter_context(nc.sbuf_tensor(f"rotcL{l}", [128, 16 * 128], F32))
        tabq = scA.enter_context(nc.sbuf_tensor(f"tabqL{l}", [128, 16 * 128], BF16))
        tabk = scA.enter_context(nc.sbuf_tensor(f"tabkL{l}", [128, 16 * 128], BF16))
        wpart = scA.enter_context(nc.sbuf_tensor(f"wpart{l}", [128, 8 * 512], BF16))
        stg = [scA.enter_context(nc.sbuf_tensor(f"stgL{l}_{i}", [128, 512], F32)) for i in range(1)]
        sqs = scA.enter_context(nc.sbuf_tensor(f"sqs{l}", [128, 512], F32))
        ss8 = scA.enter_context(nc.sbuf_tensor(f"ss8{l}", [128, 8], F32))
        rs8 = scA.enter_context(nc.sbuf_tensor(f"rs8{l}", [128, 8], F32))
        qn = scA.enter_context(nc.sbuf_tensor(f"qn{l}", [128, 512], F32))
        u = [self.rotc[:, 1024 + i * 256: 1024 + (i + 1) * 256] for i in range(4)]
        qrot = scA.enter_context(nc.sbuf_tensor(f"qrot{l}", [128, 512], BF16))
        qsrow = scA.enter_context(nc.sbuf_tensor(f"qsrow{l}", [1, DH], F32))
        ksrow = scA.enter_context(nc.sbuf_tensor(f"ksrow{l}", [1, DH], F32))

        if l == 1:
            for c in range(8):
                em.dma(lambda q, c=c: q.dma_start(
                    out=self.d_agx2_in[c * 128:(c + 1) * 128, 0:1],
                    in_=self.xc(c, 2048, 1)),
                    reads=[f"x{c}"], writes=["d_agx2_in"], tag=f"x2o{c}")
            em.coll(lambda g: g.collective_compute(
                "AllGather", ALU.bypass, replica_groups=AG,
                ins=[self.d_agx2_in], outs=[self.d_agx2_out]),
                reads=["d_agx2_in"], writes=["d_agx2_out"], tag="AGx2")
            for c in range(8):
                em.dma(lambda q, c=c: q.dma_start(
                    out=self.xc(c, 0, 1),
                    in_=self.d_agx2_out[bass.ts(self.rt["rr"] * 8 + c, 128), 0:1]),
                    reads=["d_agx2_out"], writes=[f"xh{c}"], tag=f"x2i{c}")
                em.op("S", lambda s_, c=c: s_.activation(
                    self.xc(c, 0, 1), self.xc(c, 0, 1), AF.Identity,
                    scale=self.gate[:]),
                    reads=[f"xh{c}", "gate"], writes=[f"xh{c}"])

        em.dma(lambda q: q.dma_start(
            out=self.rotc[:].rearrange("p (i c) -> p i c", c=128),
            in_=self.in_rotc.rearrange("(i p) c -> p i c", p=128)),
            writes=["rotc"])
        em.dma(lambda q: q.dma_start(out=qsrow[:], in_=self.in_qs[l]), writes=["qsrow"])
        em.dma(lambda q: q.dma_start(out=ksrow[:], in_=self.in_ks[l]), writes=["ksrow"])
        self.emit_replicate(qsrow, "qsrow", self.pE, "pE", DH, "repqs", scale=8.0)
        em.op("S", lambda s: s.copy(self.qsrep[:], self.pE[:, 0:DH]),
              reads=["pE"], writes=["qsrep"])
        self.emit_replicate(ksrow, "ksrow", self.pF, "pF", DH, "repks")
        em.op("S", lambda s: s.copy(self.ksrep[:], self.pF[:, 0:DH]),
              reads=["pF"], writes=["ksrep"])
        for i in range(16):
            base = i * 128
            for (tab, tkey, rep, repk, c0) in ((tabq, "tabq", self.qsrep, "qsrep", 0),
                                               (tabk, "tabk", self.ksrep, "ksrep", 64)):
                em.op("G", lambda g, base=base, tab=tab, rep=rep, c0=c0: g.tensor_tensor(
                    view3(tab[:], base, 32, 2, 32),
                    bc_free3(self.rotc, base + c0, 2, 32),
                    view3(rep[:], 0, 32, 2, 32), ALU.mult),
                    reads=["rotc", repk], writes=[tkey])
                em.op("G", lambda g, base=base, tab=tab, rep=rep, c0=c0: g.tensor_tensor(
                    tab[:, base + 64: base + 96],
                    self.rotc[:, base + c0 + 32: base + c0 + 64],
                    rep[:, 32:64], ALU.mult),
                    reads=["rotc", repk], writes=[tkey])
                em.op("G", lambda g, base=base, tab=tab, rep=rep, c0=c0: g.tensor_tensor(
                    tab[:, base + 96: base + 128],
                    self.rotc[:, base + c0 + 32: base + c0 + 64],
                    rep[:, 0:32], ALU.mult),
                    reads=["rotc", repk], writes=[tkey])

        order = [12, 13, 14, 15] + (list(range(0, 12)) if l == 0 else list(range(1, 12)) + [0])

        def load_part(off, tag):
            flushA()
            for kc in range(8):
                p = 0
                em.dma(lambda q, kc=kc, p=p, off=off: q.dma_start(
                    out=stg[p][:],
                    in_=self.in_wqkv[l][kc * 128:(kc + 1) * 128, off:off + 512]),
                    writes=[f"stgL{p}"], tag=f"{tag}{kc}")
                em.op("G", lambda g, kc=kc, p=p: g.tensor_copy(
                    wpart[:, kc * 512:(kc + 1) * 512], stg[p][:]),
                    reads=[f"stgL{p}"], writes=["wpart"])

        def qk_post(i, which, PP, ppk, tab, tkey, dst, dkey, dst_off, dst_stride, tpk, TPK, toff):
            em.op("S", lambda s, PP=PP: s.activation(sqs[:], PP[:], AF.Square),
                  reads=[ppk], writes=["sqs"])
            em.op("V", lambda v: v.tensor_reduce(ss8[:, 0:8], view3(sqs[:], 0, 64, 8, 64),
                                                 AX.X, ALU.add),
                  reads=["sqs"], writes=["ss8"])
            em.op("S", lambda s: s.activation(ss8[:], ss8[:], AF.Sqrt, bias=1e-24),
                  reads=["ss8"], writes=["ss8"])
            em.op("V", lambda v: v.reciprocal(rs8[:], ss8[:]),
                  reads=["ss8"], writes=["rs8"])
            rinv_bc = bass.AP(rs8.ap().tensor, 0,
                              [[rs8.ap().ap[0][0], 128], [1, 8], [0, 64]])
            em.op("V", lambda v, PP=PP, rinv_bc=rinv_bc: v.tensor_tensor(
                qn[:].rearrange("p (h d) -> p h d", h=8),
                PP[:].rearrange("p (h d) -> p h d", h=8), rinv_bc, ALU.mult),
                reads=[ppk, "rs8"], writes=["qn"])
            base = i * 128
            qlo = view3(qn[:], 0, 64, 8, 32)
            qhi = view3(qn[:], 32, 64, 8, 32)
            em.op("G", lambda g, tab=tab, base=base, qlo=qlo: g.tensor_tensor(
                u[0].rearrange("p (h d) -> p h d", h=8), qlo,
                bc_free3(tab, base, 8, 32), ALU.mult),
                reads=["qn", tkey], writes=["u0", "rotc"])
            em.op("G", lambda g, tab=tab, base=base, qhi=qhi: g.tensor_tensor(
                u[1].rearrange("p (h d) -> p h d", h=8), qhi,
                bc_free3(tab, base + 64, 8, 32), ALU.mult),
                reads=["qn", tkey], writes=["u1", "rotc"])
            em.op("V", lambda v: v.tensor_tensor(
                view3(qrot[:], 0, 64, 8, 32), u[0].rearrange("p (h d) -> p h d", h=8),
                u[1].rearrange("p (h d) -> p h d", h=8), ALU.subtract),
                reads=["u0", "u1"], writes=["qrot"])
            em.op("G", lambda g, tab=tab, base=base, qhi=qhi: g.tensor_tensor(
                u[2].rearrange("p (h d) -> p h d", h=8), qhi,
                bc_free3(tab, base + 32, 8, 32), ALU.mult),
                reads=["qn", tkey], writes=["u2", "rotc"])
            em.op("G", lambda g, tab=tab, base=base, qlo=qlo: g.tensor_tensor(
                u[3].rearrange("p (h d) -> p h d", h=8), qlo,
                bc_free3(tab, base + 96, 8, 32), ALU.mult),
                reads=["qn", tkey], writes=["u3", "rotc"])
            em.op("V", lambda v: v.tensor_tensor(
                view3(qrot[:], 32, 64, 8, 32), u[2].rearrange("p (h d) -> p h d", h=8),
                u[3].rearrange("p (h d) -> p h d", h=8), ALU.add),
                reads=["u2", "u3"], writes=["qrot"])
            for j in range(4):
                em.op("T", lambda t, j=j, toff=toff, TPK=TPK: t.matmul(
                    TPK[:, toff + j * 128: toff + (j + 1) * 128],
                    qrot[:, j * 128:(j + 1) * 128], self.identb[:],
                    start=(j == 0), stop=(j == 3),
                    is_transpose=True, skip_group_check=True),
                    reads=["qrot", "identb"], writes=[tpk])
            ddst = view3(dst[:], dst_off, dst_stride, 4, 128)
            em.op("S", lambda s, ddst=ddst, toff=toff, TPK=TPK: s.copy(
                ddst, TPK[:, toff:toff + 512]),
                reads=[tpk], writes=[dkey])
            if i == 0:
                lo_sb = qn.ap().bitcast(BF16)[:, 0:512]
                em.op("V", lambda v: v.tensor_tensor(
                    view3(sqs[:], 0, 64, 8, 32), u[0].rearrange("p (h d) -> p h d", h=8),
                    u[1].rearrange("p (h d) -> p h d", h=8), ALU.subtract),
                    reads=["u0", "u1"], writes=["sqs"])
                em.op("V", lambda v: v.tensor_tensor(
                    view3(sqs[:], 32, 64, 8, 32), u[2].rearrange("p (h d) -> p h d", h=8),
                    u[3].rearrange("p (h d) -> p h d", h=8), ALU.add),
                    reads=["u2", "u3"], writes=["sqs"])
                em.op("V", lambda v: v.tensor_tensor(
                    sqs[:], sqs[:], qrot[:], ALU.subtract),
                    reads=["sqs", "qrot"], writes=["sqs"])
                em.op("S", lambda s, lo_sb=lo_sb: s.copy(lo_sb, sqs[:]),
                      reads=["sqs"], writes=["qn"])
                lodst = qlo_fm if which == "q" else klo_fm
                lokey = "qlofm" if which == "q" else "klofm"
                for j in range(4):
                    em.op("T", lambda t, j=j, toff=toff, TPK=TPK, lo_sb=lo_sb: t.matmul(
                        TPK[:, toff + j * 128: toff + (j + 1) * 128],
                        lo_sb[:, j * 128:(j + 1) * 128], self.identb[:],
                        start=(j == 0), stop=(j == 3),
                        is_transpose=True, skip_group_check=True),
                        reads=["qn", "identb"], writes=[tpk])
                lddst = view3(lodst[:], 0, 128, 4, 128)
                em.op("S", lambda s, lddst=lddst, toff=toff, TPK=TPK: s.copy(
                    lddst, TPK[:, toff:toff + 512]),
                    reads=[tpk], writes=[lokey])

        prevA = [None]

        def flushA():
            if prevA[0] is not None:
                prevA[0]()
                prevA[0] = None

        for pi, part in enumerate(("k", "v", "q")):
            off = {"q": 0, "k": 512, "v": 1024}[part]
            load_part(off, f"ldw{part}")
            for i in order:
                if pi == 0:
                    for c in range(8):
                        sc0 = (1 + i * 128) if c < 4 else (0 + i * 128)
                        em.op("G", lambda g, c=c, i=i, sc0=sc0: g.tensor_copy(
                            self.hc(c, 1 + i * 128, 128), self.xc(c, sc0, 128)),
                            reads=self.xkeys(c, sc0 if c >= 4 else 1), writes=[f"h{c}"],
                            tag=f"hcast{i}_{c}")
                ppk = [["pA", "pB"], ["pC", "pD"], ["pE", "pF"]][pi][i % 2]
                PP = self.PS[ppk]
                for kc in range(8):
                    em.op("T", lambda t, kc=kc, i=i, PP=PP: t.matmul(
                        PP[:], self.hc(kc, 1 + i * 128, 128),
                        wpart[:, kc * 512:(kc + 1) * 512],
                        start=(kc == 0), stop=(kc == 7)),
                        reads=[f"h{kc}", "wpart"], writes=[ppk])
                flushA()
                if part == "v":
                    em.op("S", lambda s, i=i, PP=PP: s.copy(
                        v_tm[:, (4 + i) * 512:(5 + i) * 512], PP[:]),
                        reads=[ppk], writes=[f"vt{4 + i}"])
                elif part == "k":
                    tpk = ["pT1", "pT2"][i % 2]
                    prevA[0] = (lambda i=i, PP=PP, ppk=ppk, tpk=tpk: qk_post(
                        i, "k", PP, ppk, tabk, "tabk", k_fm, f"kw{i // 4}",
                        512 + i * 128, 2560, tpk, self.PS[tpk], 512))
                else:
                    tpk = ["pT1", "pT2"][i % 2]
                    prevA[0] = (lambda i=i, PP=PP, ppk=ppk, tpk=tpk: qk_post(
                        i, "q", PP, ppk, tabq, "tabq", q_fm, f"qw{i // 4}",
                        i * 128, 2048, tpk, self.PS[tpk], 0))
                if part == "v" and i == 15:
                    for c in range(4):
                        em.dma(lambda q, c=c: q.dma_start(
                            out=self.d_agkv_in[l][c * 128:(c + 1) * 128, :],
                            in_=k_fm[:, c * 2560 + 512 + 1536: c * 2560 + 512 + 2048]),
                            reads=["kw3"], writes=["d_agkv_in"], tag=f"kvout{c}")
                    for j in range(4):
                        em.dma(lambda q, j=j: q.dma_start(
                            out=self.d_agkv_in[l][512 + j * 128: 512 + (j + 1) * 128, :],
                            in_=v_tm[:, (16 + j) * 512:(17 + j) * 512]),
                            reads=[f"vt{16 + j}"], writes=["d_agkv_in"], tag=f"vvout{j}")
                    em.coll(lambda g, l=l: g.collective_compute(
                        "AllGather", ALU.bypass, replica_groups=AG,
                        ins=[self.d_agkv_in[l]], outs=[self.d_agkv_out[l]]),
                        reads=["d_agkv_in"], writes=["d_agkv_out"], tag=f"AGkv{l}")
                    for c in range(4):
                        em.dma(lambda q, c=c, l=l: q.dma_start(
                            out=k_fm[:, c * 2560: c * 2560 + 512],
                            in_=self.d_agkv_out[l][bass.ts(self.rt["rr"] * 8 + c, 128), :]),
                            reads=["d_agkv_out"], writes=["kwh"], tag=f"kvin{c}")
                    for j in range(4):
                        em.dma(lambda q, j=j, l=l: q.dma_start(
                            out=v_tm[:, j * 512:(j + 1) * 512],
                            in_=self.d_agkv_out[l][bass.ts(self.rt["rr"] * 8 + 4 + j, 128), :]),
                            reads=["d_agkv_out"], writes=[f"vt{j}"], tag=f"vvin{j}")

        flushA()
        if f"l{l}_qfm" in self.taps:
            self.tap_sbuf(f"l{l}_qfm", q_fm, BF16, [f"qw{w}" for w in range(4)])
            self.tap_sbuf(f"l{l}_kfm", k_fm, BF16, [f"kw{w}" for w in range(4)] + ["kwh"])
            self.tap_sbuf(f"l{l}_vtm", v_tm, BF16, [f"vt{j}" for j in range(20)])
        em.barrier()
        scA.close()

        scB = ExitStack()
        o_fm = scB.enter_context(nc.sbuf_tensor(f"o_fm{l}", [128, 4 * 2048], BF16))
        wo = scB.enter_context(nc.sbuf_tensor(f"wo{l}", [128, 4 * 1024], BF16))
        stgW = scB.enter_context(nc.sbuf_tensor(f"stgW{l}", [128, 1024], F32))
        p_sb = [scB.enter_context(nc.sbuf_tensor(f"p_sb{l}_{i}", [128, 640], BF16)) for i in range(2)]
        pt_sb = [scB.enter_context(nc.sbuf_tensor(f"pt_sb{l}_{i}", [128, 640], BF16)) for i in range(2)]
        o_tmp = [scB.enter_context(nc.sbuf_tensor(f"o_tmp{l}_{i}", [128, 128], BF16)) for i in range(2)]
        nmt = scB.enter_context(nc.sbuf_tensor(f"nmt{l}", [128, 8], F32))
        rrt = scB.enter_context(nc.sbuf_tensor(f"rrt{l}", [128, 8], F32))

        hpstate = {}
        prev_tail = [None]

        def flush_tail():
            if prev_tail[0] is not None:
                prev_tail[0]()
                prev_tail[0] = None

        for w in range(4):
            for s in range(4):
                pw_ = (4 - s) * 128
                cw = (s + 1) * 128
                for hp in range(4):
                    pok = ["pE", "pF"][hp % 2]
                    PO = self.PS[pok]
                    hpstate[(w, s, hp)] = {}
                    for hh in range(2):
                        h = 2 * hp + hh
                        up = self.par()
                        psb = p_sb[up]
                        psbk = f"psb{up}"
                        ptsb = pt_sb[up]
                        ptsbk = f"ptsb{up}"
                        tpk = ["pT1", "pT2"][up]
                        TPK = self.PS[tpk]
                        rrk = f"rrt{up}"
                        nmk = f"nmt{up}"
                        nmb = 4 * up
                        rb = 4 * up + 2 * hh
                        sA = ["pA", "pC"][up]
                        sB = ["pB", "pD"][up]
                        FULL = self.PS[["pABfull", "pCDfull"][up]]
                        st = hpstate[(w, s, hp)]
                        st[hh] = (rb, rrk, ptsb, ptsbk, TPK, tpk)
                        qsl = q_fm[hh * 64:(hh + 1) * 64,
                                   hp * 2048 + w * 512 + s * 128: hp * 2048 + w * 512 + (s + 1) * 128]
                        em.op("T", lambda t, qsl=qsl, hp=hp, hh=hh, w=w, s=s, pw_=pw_, FULL=FULL:
                              t.matmul(FULL[:, :pw_], qsl,
                                       k_fm[hh * 64:(hh + 1) * 64,
                                            hp * 2560 + w * 512 + s * 128:
                                            hp * 2560 + w * 512 + 512],
                                       start=True, stop=False, skip_group_check=True),
                              reads=[f"qw{w}", ("kwh" if w == 0 else f"kw{w - 1}")],
                              writes=[sA], tag=f"simP{w}{s}{h}")
                        if s > 0:
                            em.op("T", lambda t, qsl=qsl, hp=hp, hh=hh, w=w, s=s, pw_=pw_, FULL=FULL:
                                  t.matmul(FULL[:, pw_:512], qsl,
                                           k_fm[hh * 64:(hh + 1) * 64,
                                                hp * 2560 + 512 + w * 512:
                                                hp * 2560 + 512 + w * 512 + s * 128],
                                           start=False, stop=False, skip_group_check=True),
                                  reads=[f"qw{w}", f"kw{w}"], writes=[sA], tag=f"simC1_{w}{s}{h}")
                        em.op("T", lambda t, FULL=FULL, w=w: t.matmul(
                            FULL[:, 0:128], self.identb[:], self.maskl[:],
                            start=False, stop=(w != 0), skip_group_check=True),
                            reads=["identb", "maskl"], writes=[sA])
                        if w == 0:
                            em.op("T", lambda t, FULL=FULL, pw_=pw_: t.matmul(
                                FULL[:, 0:pw_], self.identb[:], self.maskpb[:, :pw_],
                                start=False, stop=True, skip_group_check=True),
                                reads=["identb", "maskpb"], writes=[sA])
                        hilo = (w == 0 and s == 0)
                        em.op("T", lambda t, qsl=qsl, hp=hp, hh=hh, w=w, s=s, FULL=FULL:
                              t.matmul(FULL[:, 512:640], qsl,
                                       k_fm[hh * 64:(hh + 1) * 64,
                                            hp * 2560 + 512 + w * 512 + s * 128:
                                            hp * 2560 + 512 + w * 512 + (s + 1) * 128],
                                       start=True, stop=False, skip_group_check=True),
                              reads=[f"qw{w}", f"kw{w}"], writes=[sB], tag=f"simC2_{w}{s}{h}")
                        if hilo:
                            em.op("T", lambda t, qsl=qsl, hp=hp, hh=hh, FULL=FULL:
                                  t.matmul(FULL[:, 512:640], qsl,
                                           klo_fm[hh * 64:(hh + 1) * 64,
                                                  hp * 128: (hp + 1) * 128],
                                           start=False, stop=False, skip_group_check=True),
                                  reads=[f"qw0", "klofm"], writes=[sB])
                            em.op("T", lambda t, hp=hp, hh=hh, FULL=FULL:
                                  t.matmul(FULL[:, 512:640],
                                           qlo_fm[hh * 64:(hh + 1) * 64,
                                                  hp * 128: (hp + 1) * 128],
                                           k_fm[hh * 64:(hh + 1) * 64,
                                                hp * 2560 + 512: hp * 2560 + 640],
                                           start=False, stop=False, skip_group_check=True),
                                  reads=["qlofm", f"kw0"], writes=[sB])
                        em.op("T", lambda t, FULL=FULL: t.matmul(
                            FULL[:, 512:640], self.identb[:], self.masku[:],
                            start=False, stop=True, skip_group_check=True),
                            reads=["identb", "masku"], writes=[sB])

                        flush_tail()

                        def tail(w=w, s=s, hp=hp, hh=hh, h=h, up=up, psb=psb,
                                 psbk=psbk, ptsb=ptsb, ptsbk=ptsbk, tpk=tpk,
                                 TPK=TPK, rrk=rrk, nmk=nmk, nmb=nmb, rb=rb,
                                 sA=sA, sB=sB, FULL=FULL, PO=PO, pok=pok):
                            em.op("V", lambda v: v.tensor_reduce(
                                nmt[:, nmb + hh: nmb + hh + 1], FULL[:, 0:640],
                                AX.X, ALU.max, negate=True),
                                reads=[sA, sB], writes=[nmk])
                            em.op("S", lambda s_: s_.activation(
                                psb[:, 0:640], FULL[:, 0:640], AF.Exp,
                                bias=nmt[:, nmb + hh: nmb + hh + 1],
                                accum_out=rrt[:, rb: rb + 1]),
                                reads=[sA, sB, nmk], writes=[psbk, rrk])
                            em.op("S", lambda s_: s_.activation(
                                rrt[:, rb + 1: rb + 2], rrt[:, rb: rb + 1], AF.Ln),
                                reads=[rrk], writes=[rrk])
                            em.op("S", lambda s_: s_.activation(
                                rrt[:, rb + 1: rb + 2], rrt[:, rb + 1: rb + 2],
                                AF.Exp, scale=-1.0),
                                reads=[rrk], writes=[rrk])
                            for j in range(5):
                                em.op("T", lambda t, j=j: t.matmul(
                                    TPK[:, j * 128:(j + 1) * 128],
                                    psb[:, j * 128:(j + 1) * 128],
                                    self.identb[:], start=(j == 0), stop=(j == 4),
                                    is_transpose=True, skip_group_check=True),
                                    reads=[psbk, "identb"], writes=[tpk])
                            em.op("V", lambda v: v.tensor_copy(ptsb[:], TPK[:, 0:640]),
                                  reads=[tpk], writes=[ptsbk])
                            for j in range(5):
                                vt = (4 * w + s + j) if j < (4 - s) else (4 + 4 * w + (j - (4 - s)))
                                em.op("T", lambda t, j=j, vt=vt: t.matmul(
                                    PO[:, hh * 64:(hh + 1) * 64],
                                    ptsb[:, j * 128:(j + 1) * 128],
                                    v_tm[:, vt * 512 + h * 64: vt * 512 + (h + 1) * 64],
                                    start=(hh == 0 and j == 0),
                                    stop=(hh == 1 and j == 4),
                                    skip_group_check=True),
                                    reads=[ptsbk, f"vt{vt}"], writes=[pok])
                            if hh == 1:
                                st2 = hpstate[(w, s, hp)]
                                ot = o_tmp[hp % 2]
                                otk = f"o_tmp{hp % 2}"
                                for h2 in range(2):
                                    rb2, rrk2, _, _, _, _ = st2[h2]
                                    rr_bc = bass.AP(rrt.ap().tensor, rb2 + 1,
                                                    [[rrt.ap().ap[0][0], 128], [0, 64]])
                                    em.op("V", lambda v, h2=h2, rr_bc=rr_bc: v.tensor_tensor(
                                        ot[:, h2 * 64:(h2 + 1) * 64],
                                        PO[:, h2 * 64:(h2 + 1) * 64],
                                        rr_bc, ALU.mult),
                                        reads=[pok, rrk2], writes=[otk])
                                _, _, _, _, TPK2, tpk2 = st2[1]
                                em.op("T", lambda t: t.matmul(
                                    TPK2[:, 768:896], ot[:], self.identb[:],
                                    start=True, stop=True,
                                    is_transpose=True, skip_group_check=True),
                                    reads=[otk, "identb"], writes=[tpk2])
                                em.op("S", lambda s_: s_.copy(
                                    o_fm[:, hp * 2048 + w * 512 + s * 128:
                                         hp * 2048 + w * 512 + (s + 1) * 128],
                                    TPK2[:, 768:896]),
                                    reads=[tpk2], writes=[f"ofm{w}"])
                                if os.environ.get("PBDBG") == f"{l},{w},{s},{hp}":
                                    dbg2 = self._pbdbg
                                    em.op("S", lambda s_: s_.copy(dbg2[:, 0:128], ot[:]),
                                          reads=[otk], writes=["pbdbg"])
                                    self.taps.add("pbdbg")
                                    self.tap_sbuf("pbdbg", dbg2, BF16, ["pbdbg"])

                        prev_tail[0] = tail
        flush_tail()
        if f"l{l}_ofm" in self.taps:
            self.tap_sbuf(f"l{l}_ofm", o_fm, BF16, [f"ofm{w}" for w in range(4)])

        for kc in range(4):
            em.dma(lambda q, kc=kc: q.dma_start(
                out=stgW[:], in_=self.in_wof[l][kc * 128:(kc + 1) * 128, :]),
                writes=["stgW"], tag=f"ldwo{kc}")
            em.op("G", lambda g, kc=kc: g.tensor_copy(
                wo[:, kc * 1024:(kc + 1) * 1024], stgW[:]),
                reads=["stgW"], writes=["wo"])
        for cb in range(4):
            for m in range(8):
                pwk = ["pA", "pB"][(cb * 8 + m) % 2]
                PW = self.PS[pwk]
                for kc in range(4):
                    em.op("T", lambda t, m=m, kc=kc, cb=cb, PW=PW: t.matmul(
                        PW[:], wo[:, kc * 1024 + m * 128: kc * 1024 + (m + 1) * 128],
                        o_fm[:, kc * 2048 + cb * 512:(kc) * 2048 + cb * 512 + 512],
                        start=(kc == 0), stop=(kc == 3)),
                        reads=["wo", f"ofm{cb}"], writes=[pwk])
                em.op("V", lambda v, m=m, cb=cb, PW=PW: v.tensor_tensor(
                    self.xc(m, 1 + cb * 512, 512), PW[:], self.xc(m, 1 + cb * 512, 512),
                    ALU.add),
                    reads=[pwk, f"x{m}"], writes=[f"x{m}"])
        if f"l{l}_xmid" in self.taps:
            self.tap_sbuf(f"l{l}_xmid", self.x, F32,
                          [f"x{c}" for c in range(8)] + [f"xh{c}" for c in range(8)])
        for c in range(8):
            em.dma(lambda q, c=c: q.dma_start(
                out=self.d_agxb_in[l][c * 128:(c + 1) * 128, 0:1],
                in_=self.xc(c, 2048, 1)),
                reads=[f"x{c}"], writes=["d_agxb_in"], tag=f"xbo{c}")
        em.coll(lambda g, l=l: g.collective_compute(
            "AllGather", ALU.bypass, replica_groups=AG,
            ins=[self.d_agxb_in[l]], outs=[self.d_agxb_out[l]]),
            reads=["d_agxb_in"], writes=["d_agxb_out"], tag=f"AGxb{l}")
        for c in range(8):
            em.dma(lambda q, c=c, l=l: q.dma_start(
                out=self.xc(c, 0, 1),
                in_=self.d_agxb_out[l][bass.ts(self.rt["rr"] * 8 + c, 128), 0:1]),
                reads=["d_agxb_out"], writes=[f"xh{c}"], tag=f"xbi{c}")
            em.op("S", lambda s_, c=c: s_.activation(
                self.xc(c, 0, 1), self.xc(c, 0, 1), AF.Identity,
                scale=self.gate[:]),
                reads=[f"xh{c}", "gate"], writes=[f"xh{c}"])
        em.barrier()
        scB.close()
        sc_outer.close()

        sc = ExitStack()
        hglu = sc.enter_context(nc.sbuf_tensor(f"hglu{l}", [128, NCA * 1024], BF16))
        w1a = [sc.enter_context(nc.sbuf_tensor(f"w1a{l}_{i}", [128, 1024], BF16)) for i in range(2)]
        w1g = [sc.enter_context(nc.sbuf_tensor(f"w1g{l}_{i}", [128, 1024], BF16)) for i in range(2)]
        stga = sc.enter_context(nc.sbuf_tensor(f"stga{l}", [128, 1024], F32))
        stgg = sc.enter_context(nc.sbuf_tensor(f"stgg{l}", [128, 1024], F32))
        w2m = [sc.enter_context(nc.sbuf_tensor(f"w2m{l}_{i}", [128, NCA * 128], BF16)) for i in range(2)]
        stg2 = sc.enter_context(nc.sbuf_tensor(f"stg2{l}", [128, NCA * 128], F32))
        glb = sc.enter_context(nc.sbuf_tensor(f"glb{l}", [128, 512], BF16))
        abf = sc.enter_context(nc.sbuf_tensor(f"abf{l}", [128, 512], F32))
        w2s = sc.enter_context(nc.sbuf_tensor(f"w2s{l}", [128, 512], F32))

        blocks4 = [(1, 512), (513, 512), (1025, 512), (1537, 512)]

        def apply_ln2(bi, col0, wdt):
            for c in range(8):
                sc0 = col0 - 1 if c >= 4 else col0
                em.op("V", lambda v, c=c, sc0=sc0, wdt=wdt: v.tensor_tensor(
                    self.lnt1[:, :wdt], self.xc(c, sc0, wdt), self.pE[:, :wdt], ALU.mult),
                    reads=self.xkeys(c, sc0) + ["pE"], writes=["lnt1"])
                em.op("V", lambda v, wdt=wdt: v.tensor_tensor(
                    self.lnt1[:, :wdt], self.lnt1[:, :wdt], self.pF[:, :wdt], ALU.add),
                    reads=["lnt1", "pF"], writes=["lnt1"])
                em.op("S", lambda s, c=c, col0=col0, wdt=wdt, l=l: s.activation(
                    self.hc(c, col0, wdt), self.lnt1[:, :wdt], AF.Identity,
                    scale=self.lgt[l][:, c:c + 1], bias=self.lbt[l][:, c:c + 1]),
                    reads=["lnt1", f"lgt{l}", f"lbt{l}"], writes=[f"h{c}"])

        self._ln_stats_blocks(True, blocks4, True, f"ln2_{l}", apply_fn=apply_ln2)
        if f"l{l}_ln2" in self.taps:
            self.tap_sbuf(f"l{l}_ln2", self.hbuf, BF16, [f"h{c}" for c in range(8)])

        for H in range(2):
            for ca in range(NCA):
                p = ca % 2
                em.dma(lambda q, ca=ca, l=l: q.dma_start(
                    out=stga[:].rearrange("p (kc n) -> p kc n", n=128),
                    in_=self.in_w1a[l][:, ca * 128:(ca + 1) * 128].rearrange(
                        "(kc p) n -> p kc n", p=128)),
                    writes=["stga"], tag=f"ldw1a{ca}")
                em.op("G", lambda g, p=p: g.tensor_copy(w1a[p][:], stga[:]),
                      reads=["stga"], writes=[f"w1a{p}"])
                em.dma(lambda q, ca=ca, l=l: q.dma_start(
                    out=stgg[:].rearrange("p (kc n) -> p kc n", n=128),
                    in_=self.in_w1g[l][:, ca * 128:(ca + 1) * 128].rearrange(
                        "(kc p) n -> p kc n", p=128)),
                    writes=["stgg"], tag=f"ldw1g{ca}")
                em.op("G", lambda g, p=p: g.tensor_copy(w1g[p][:], stgg[:]),
                      reads=["stgg"], writes=[f"w1g{p}"])
                for tb in range(2):
                    pak = ["pA", "pB"][tb]
                    pgk = ["pC", "pD"][tb]
                    PA_, PG_ = self.PS[pak], self.PS[pgk]
                    c0 = 1 + H * 1024 + tb * 512
                    for kc in range(8):
                        em.op("T", lambda t, kc=kc, p=p, c0=c0, PA_=PA_: t.matmul(
                            PA_[:], w1a[p][:, kc * 128:(kc + 1) * 128],
                            self.hc(kc, c0, 512), start=(kc == 0), stop=(kc == 7)),
                            reads=[f"w1a{p}", f"h{kc}"], writes=[pak])
                        em.op("T", lambda t, kc=kc, p=p, c0=c0, PG_=PG_: t.matmul(
                            PG_[:], w1g[p][:, kc * 128:(kc + 1) * 128],
                            self.hc(kc, c0, 512), start=(kc == 0), stop=(kc == 7)),
                            reads=[f"w1g{p}", f"h{kc}"], writes=[pgk])
                    em.op("S", lambda s, ca=ca, l=l, PG_=PG_: s.activation(
                        glb[:], PG_[:], AF.Gelu, bias=self.b1gt[l][:, ca:ca + 1]),
                        reads=[pgk, f"b1gt{l}"], writes=["glb"])
                    em.op("S", lambda s, ca=ca, l=l, PA_=PA_: s.activation(
                        abf[:], PA_[:], AF.Identity, bias=self.b1at[l][:, ca:ca + 1]),
                        reads=[pak, f"b1at{l}"], writes=["abf"])
                    em.op("V", lambda v, ca=ca, tb=tb: v.tensor_tensor(
                        hglu[:, ca * 1024 + tb * 512: ca * 1024 + (tb + 1) * 512],
                        abf[:], glb[:], ALU.mult),
                        reads=["abf", "glb"], writes=[f"hg{ca}"])
            for m in range(8):
                p = m % 2
                em.dma(lambda q, m=m, l=l: q.dma_start(
                    out=stg2[:].rearrange("p (kc n) -> p kc n", n=128),
                    in_=self.in_w2[l][:, m * 128:(m + 1) * 128].rearrange(
                        "(kc p) n -> p kc n", p=128)),
                    writes=["stg2"], tag=f"ldw2_{m}")
                em.op("G", lambda g, p=p: g.tensor_copy(w2m[p][:], stg2[:]),
                      reads=["stg2"], writes=[f"w2m{p}"])
                for tb in range(2):
                    pwk = ["pE", "pF"][tb]
                    PW = self.PS[pwk]
                    for ca in range(NCA):
                        em.op("T", lambda t, ca=ca, tb=tb, p=p, PW=PW: t.matmul(
                            PW[:], w2m[p][:, ca * 128:(ca + 1) * 128],
                            hglu[:, ca * 1024 + tb * 512: ca * 1024 + (tb + 1) * 512],
                            start=(ca == 0), stop=(ca == NCA - 1)),
                            reads=[f"w2m{p}", f"hg{ca}"], writes=[pwk])
                    em.op("S", lambda s, m=m, l=l, PW=PW: s.activation(
                        w2s[:], PW[:], AF.Identity, bias=self.b2t[l][:, m:m + 1]),
                        reads=[pwk, f"b2t{l}"], writes=["w2s"])
                    c0 = 1 + H * 1024 + tb * 512
                    em.op("V", lambda v, m=m, c0=c0: v.tensor_tensor(
                        self.xc(m, c0, 512), w2s[:], self.xc(m, c0, 512), ALU.add),
                        reads=["w2s", f"x{m}"], writes=[f"x{m}"])
        if f"l{l}_out" in self.taps:
            self.tap_sbuf(f"l{l}_out", self.x, F32,
                          [f"x{c}" for c in range(8)] + [f"xh{c}" for c in range(8)])
        em.barrier()
        sc.close()

    def final_ln(self):
        nc, em, es = self.nc, self.em, self.es
        sc = ExitStack()
        xr = sc.enter_context(nc.sbuf_tensor("xr_all", [128, 8 * 2048], F32))
        otm = [sc.enter_context(nc.sbuf_tensor(f"otm{i}", [128, 1024], F32)) for i in range(2)]
        for c in range(8):
            em.dma(lambda q, c=c: q.dma_start(out=xr[:, c * 2048:(c + 1) * 2048],
                                              in_=self.d_xres1[c]),
                   reads=["d_xres1"], writes=["xr"], tag=f"xr{c}")
        blocks4 = [(1, 512), (513, 512), (1025, 512), (1537, 512)]

        def apply_lnf(bi, col0, wdt):
            for c in range(8):
                em.op("V", lambda v, c=c, col0=col0, wdt=wdt: v.tensor_tensor(
                    self.lnt1[:, :wdt], self.xc(c, col0, wdt), self.pE[:, :wdt], ALU.mult),
                    reads=[f"x{c}", "pE"], writes=["lnt1"])
                em.op("V", lambda v, wdt=wdt: v.tensor_tensor(
                    self.lnt1[:, :wdt], self.lnt1[:, :wdt], self.pF[:, :wdt], ALU.add),
                    reads=["lnt1", "pF"], writes=["lnt1"])
                em.op("V", lambda v, c=c, col0=col0, wdt=wdt: v.tensor_tensor(
                    self.xc(c, col0, wdt), self.lnt1[:, :wdt],
                    xr[:, c * 2048 + col0 - 1: c * 2048 + col0 - 1 + wdt], ALU.add),
                    reads=["lnt1", "xr"], writes=[f"x{c}"])

        self._ln_stats_blocks(True, blocks4, None, "lnf", apply_fn=apply_lnf)
        for i in range(16):
            p = i % 2
            for half in range(2):
                pk = ["pA", "pB"][half]
                PK = self.PS[pk]
                for j in range(4):
                    c = half * 4 + j
                    em.op("T", lambda t, c=c, j=j, i=i, PK=PK: t.matmul(
                        PK[:, j * 128:(j + 1) * 128], self.xc(c, 1 + i * 128, 128),
                        self.identf[:], start=(j == 0), stop=(j == 3),
                        is_transpose=True, skip_group_check=True),
                        reads=[f"x{c}", "identf"], writes=[pk])
                em.op("V", lambda v, p=p, half=half, PK=PK: v.tensor_copy(
                    otm[p][:, half * 512:(half + 1) * 512], PK[:]),
                    reads=[pk], writes=[f"otm{p}"])
            em.dma(lambda q, i=i, p=p: q.dma_start(
                out=self.out[i * 128:(i + 1) * 128, :], in_=otm[p][:]),
                reads=[f"otm{p}"], writes=["out"], tag=f"st{i}")
        sc.close()

    def build(self):
        self.declare()
        self.load_consts()
        self.step0()
        self.load_biases()
        if self.stop == "step0":
            return self.em.finalize(self.es, list(self.tap_names))
        self.attn_block()
        if self.stop == "attn":
            return self.em.finalize(self.es, list(self.tap_names))
        for l in range(2):
            self.layer(l)
            if self.stop == f"l{l}":
                return self.em.finalize(self.es, list(self.tap_names))
        self.final_ln()
        self.em.finalize(self.es, ["out"] + list(self.tap_names))


QK_SCALE = 8.0


def build_program(taps=(), stop="full"):
    nc = bass.Bass("TRN2", target_bir_lowering=False)
    es = ExitStack()
    prog = Prog(nc, es, taps, stop)
    prog.build()
    return nc, es, prog



def make_core_inputs(inputs, c):
    b, qd = divmod(c, 4)
    t0 = qd * T
    g = lambda k: np.asarray(inputs[k], np.float32)
    x = np.ascontiguousarray(g("L0_attn_x")[b, t0:t0 + T])
    xh = (np.zeros((1, D), np.float32) if t0 == 0
          else np.ascontiguousarray(g("L0_attn_x")[b, t0 - 1].reshape(1, D)))
    gate = (np.ones if t0 > 0 else np.zeros)((128, 1), np.float32)
    maskp = (np.zeros((128, 512), BF16NP) if t0 > 0
             else np.full((128, 512), NEGBIG, BF16NP))
    tpos = np.arange(t0, t0 + T, dtype=np.float64)
    half_idx = np.arange(0, DH, 2, dtype=np.float64)
    inv_freq = 1.0 / 10000.0 ** (half_idx / DH)
    freqs = tpos[:, None] * inv_freq[None, :]
    base = (half_idx + 0.4 * DH) / (1.4 * DH)
    power = (tpos - N // 2) / float(WINDOW // 2)
    scale = base[None, :] ** power[:, None]
    cosf, sinf = np.cos(freqs), np.sin(freqs)
    rotc = np.concatenate([cosf * scale, sinf * scale, cosf / scale, sinf / scale],
                          axis=1).astype(np.float32)
    d = dict(x=x, xh=xh, gate=gate, maskp=maskp, rotc=rotc)
    for w in ("Wq", "Wk", "Wv", "Wo"):
        d[w] = g("L0_attn_" + w)
    for w in ("bq", "bk", "bv", "bo"):
        d[w] = g("L0_attn_" + w).reshape(8, 128)
    for l in range(2):
        d[f"wqkv{l}"] = np.ascontiguousarray(g("L0_ffn_wqkv")[l])
        d[f"qs{l}"] = g("L0_ffn_q_scale")[l].reshape(1, DH)
        d[f"ks{l}"] = g("L0_ffn_k_scale")[l].reshape(1, DH)
        d[f"wof{l}"] = np.ascontiguousarray(g("L0_ffn_wo")[l])
        d[f"lg{l}"] = g("L0_ffn_ln_g")[l].reshape(8, 128)
        d[f"lb{l}"] = g("L0_ffn_ln_b")[l].reshape(8, 128)
        w1 = g("L0_ffn_w1")[l]
        w1a = np.zeros((D, HIDP), np.float32); w1a[:, :HID] = w1[:, :HID]
        w1g = np.zeros((D, HIDP), np.float32); w1g[:, :HID] = w1[:, HID:]
        d[f"w1a{l}"] = w1a
        d[f"w1g{l}"] = w1g
        b1 = g("L0_ffn_b1")[l]
        b1a = np.zeros((NCA * 128,), np.float32); b1a[:HID] = b1[:HID]
        b1g = np.zeros((NCA * 128,), np.float32); b1g[:HID] = b1[HID:]
        d[f"b1a{l}"] = b1a.reshape(NCA, 128)
        d[f"b1g{l}"] = b1g.reshape(NCA, 128)
        w2 = np.zeros((HIDP, D), np.float32); w2[:HID] = g("L0_ffn_w2")[l]
        d[f"w2{l}"] = w2
        d[f"b2{l}"] = g("L0_ffn_b2")[l].reshape(8, 128)
    return {k: np.ascontiguousarray(v if k == "maskp" else np.asarray(v, np.float32))
            for k, v in d.items()}


def kernel(**inputs):
    taps = tuple(t for t in os.environ.get("KTAPS", "").split(",") if t)
    stop = os.environ.get("KSTOP", "full")
    nc, es, prog = build_program(taps, stop)
    with es:
        in_maps = [make_core_inputs(inputs, c) for c in range(NCORES)]
        res = run_bass_kernel_spmd(nc, in_maps, core_ids=list(range(NCORES)))
    if taps or stop != "full":
        kernel.tap_results = res.results
        kernel.tap_names = prog.tap_names
    if stop != "full":
        return np.zeros((B, N, D), np.float32)
    outs = [res.results[c]["out"] for c in range(NCORES)]
    out = np.empty((B, N, D), np.float32)
    for c in range(NCORES):
        b, qd = divmod(c, 4)
        out[b, qd * T:(qd + 1) * T] = outs[c]
    return out


# version 16
# speedup vs baseline: 1.3676x; 1.0032x over previous
import os
import numpy as np
import ml_dtypes
from contextlib import ExitStack

import concourse.bass as bass
import concourse.mybir as mybir
from concourse.bass_utils import run_bass_kernel_spmd

F32 = mybir.dt.float32
BF16 = mybir.dt.bfloat16
AF = mybir.ActivationFunctionType
ALU = mybir.AluOpType
AX = mybir.AxisListType

NCORES = 8
B = 2
N = 8192
D = 1024
T = 2048
WCOL = T + 1
NH = 8
DH = 64
INNER = 512
WINDOW = 512
HID = 2730
NCA = 22
HIDP = NCA * 128
NEGBIG = -1.0e30
BF16NP = ml_dtypes.bfloat16

ENGINES = ("T", "V", "S", "G", "Q")



class Emitter:

    def __init__(self, nc):
        self.nc = nc
        self.ops = {e: [] for e in ENGINES}
        self.last_w = {}
        self.readers = {}
        self.pending_bar = {e: None for e in ENGINES}
        self.prolog = {}

    def _is_async(self, eng, idx):
        return self.ops[eng][idx]["kind"] in ("d", "cc")

    def _dep(self, eng, reads, writes):
        waits = {}

        def add(se, si):
            if se == eng and not self._is_async(se, si):
                self.ops[se][si]["same_dep"] = True
                return
            if si > waits.get(se, -1):
                waits[se] = si

        for k in reads:
            lw = self.last_w.get(k)
            if lw is not None:
                add(*lw)
        for k in writes:
            lw = self.last_w.get(k)
            if lw is not None:
                add(*lw)
            for se, si in self.readers.get(k, ()):
                add(se, si)
        if self.pending_bar[eng] is not None:
            for se, si in self.pending_bar[eng].items():
                add(se, si)
            self.pending_bar[eng] = None
        return waits

    def _record(self, eng, fn, reads, writes, kind, tag):
        waits = self._dep(eng, reads, writes)
        idx = len(self.ops[eng])
        self.ops[eng].append(dict(fn=fn, waits=waits, inc=False, kind=kind,
                                  tag=tag, own_sem=None, same_dep=False))
        for k in writes:
            self.last_w[k] = (eng, idx)
            self.readers[k] = []
        for k in reads:
            self.readers.setdefault(k, []).append((eng, idx))
        return (eng, idx)

    def op(self, eng, fn, reads=(), writes=(), tag=""):
        return self._record(eng, fn, reads, writes, "c", tag)

    def dma(self, fn, reads=(), writes=(), tag=""):
        return self._record("Q", fn, reads, writes, "d", tag)

    def coll(self, fn, reads=(), writes=(), tag=""):
        return self._record("G", fn, reads, writes, "cc", tag)

    def barrier(self):
        snap = {e: len(self.ops[e]) - 1 for e in ENGINES if self.ops[e]}
        for e in ENGINES:
            pb = {se: si for se, si in snap.items() if se != e}
            self.pending_bar[e] = pb

    def finalize(self, es, final_out_keys=()):
        nc = self.nc
        for e in ENGINES:
            for o in self.ops[e]:
                if o["same_dep"] and o["kind"] == "c":
                    o["inc"] = True
                for se, si in o["waits"].items():
                    self.ops[se][si]["inc"] = True
        for k in final_out_keys:
            lw = self.last_w.get(k)
            if lw is not None:
                self.ops[lw[0]][lw[1]]["inc"] = True
        for o in self.ops["Q"]:
            o["inc"] = True
        ncc = 0
        for e in ENGINES:
            c = 0
            for o in self.ops[e]:
                if o["kind"] == "cc":
                    o["cum"] = None
                    if o["inc"]:
                        o["own_sem"] = es.enter_context(nc.semaphore(f"cc{ncc}"))
                        ncc += 1
                else:
                    if o["inc"]:
                        c += 16 if o["kind"] == "d" else 1
                    o["cum"] = c
        sems = {e: es.enter_context(nc.semaphore(f"sem_{e}")) for e in ENGINES}
        self.sems = sems

        out_fence = []
        for k in final_out_keys:
            lw = self.last_w.get(k)
            if lw is not None:
                se, si = lw
                out_fence.append((se, self.ops[se][si]["cum"]))

        def make_body(e):
            def body(eng):
                if e in self.prolog:
                    self.prolog[e](eng)
                hwm = {}
                ccseen = set()
                for o in self.ops[e]:
                    for se in sorted(o["waits"]):
                        si = o["waits"][se]
                        src = self.ops[se][si]
                        if src["kind"] == "cc":
                            if (se, si) not in ccseen:
                                eng.wait_ge(src["own_sem"], 1)
                                ccseen.add((se, si))
                        else:
                            tgt = src["cum"]
                            if tgt > hwm.get(se, 0):
                                eng.wait_ge(sems[se], tgt)
                                hwm[se] = tgt
                    inst = o["fn"](eng)
                    if o["inc"]:
                        if o["kind"] == "cc":
                            inst.then_inc(o["own_sem"], 1)
                        else:
                            inst.then_inc(sems[e], 16 if o["kind"] == "d" else 1)
                if e == "Q":
                    for se, tgt in out_fence:
                        eng.wait_ge(sems[se], tgt)
            return body

        blk = es.enter_context(nc.Block())
        blk.sync(make_body("Q"))
        blk.tensor(make_body("T"))
        blk.vector(make_body("V"))
        blk.scalar(make_body("S"))
        blk.gpsimd(make_body("G"))

    def n_instructions(self):
        return sum(len(v) for v in self.ops.values())


def bc_free(tile, reps, offset=0, pcnt=128):
    a = tile.ap()
    return bass.AP(a.tensor, offset, [[a.ap[0][0], pcnt], [0, reps]])


def bc_free3(tile, offset, reps, inner, pcnt=128):
    a = tile.ap()
    return bass.AP(a.tensor, offset, [[a.ap[0][0], pcnt], [0, reps], [1, inner]])


def view3(ap, offset, step, count, inner, pcnt=128):
    return bass.AP(ap.tensor, offset, [[ap.ap[0][0], pcnt], [step, count], [1, inner]])



class Prog:
    def __init__(self, nc, es, taps, stop):
        self.nc, self.es = nc, es
        self.em = Emitter(nc)
        self.taps = set(taps)
        self.stop = stop
        self.tap_names = []
        self._par = 0

    def par(self):
        self._par += 1
        return self._par % 2

    def declare(self):
        nc, es = self.nc, self.es
        f = lambda name, shape: nc.dram_tensor(name, shape, F32, kind="ExternalInput").ap()
        self.in_x = f("x", [T, D])
        self.in_xh = f("xh", [1, D])
        self.in_rotc = f("rotc", [T, 128])
        self.in_gate = f("gate", [128, 1])
        self.in_maskp = nc.dram_tensor("maskp", [128, 512], BF16, kind="ExternalInput").ap()
        self.in_W = {w: f(w, [D, D]) for w in ("Wq", "Wk", "Wv", "Wo")}
        self.in_b = {w: f(w, [8, 128]) for w in ("bq", "bk", "bv", "bo")}
        self.in_wqkv = [f(f"wqkv{l}", [D, 3 * INNER]) for l in range(2)]
        self.in_qs = [f(f"qs{l}", [1, DH]) for l in range(2)]
        self.in_ks = [f(f"ks{l}", [1, DH]) for l in range(2)]
        self.in_wof = [f(f"wof{l}", [INNER, D]) for l in range(2)]
        self.in_lg = [f(f"lg{l}", [8, 128]) for l in range(2)]
        self.in_lb = [f(f"lb{l}", [8, 128]) for l in range(2)]
        self.in_w1a = [f(f"w1a{l}", [D, HIDP]) for l in range(2)]
        self.in_w1g = [f(f"w1g{l}", [D, HIDP]) for l in range(2)]
        self.in_b1a = [f(f"b1a{l}", [NCA, 128]) for l in range(2)]
        self.in_b1g = [f(f"b1g{l}", [NCA, 128]) for l in range(2)]
        self.in_w2 = [f(f"w2{l}", [HIDP, D]) for l in range(2)]
        self.in_b2 = [f(f"b2{l}", [8, 128]) for l in range(2)]
        self.out = nc.dram_tensor("out", [T, D], F32, kind="ExternalOutput").ap()

        self.d_xres1 = nc.dram_tensor("d_xres1", [8, 128, T], F32).ap()
        self.d_ags_in = nc.dram_tensor("d_ags_in", [2, D], F32).ap()
        self.d_ags_out = nc.dram_tensor("d_ags_out", [8, D], F32).ap()
        self.d_agkv_in = [nc.dram_tensor(f"d_agkv_in{l}", [1024, 512], BF16).ap() for l in range(2)]
        self.d_agkv_out = [nc.dram_tensor(f"d_agkv_out{l}", [4096, 512], BF16).ap() for l in range(2)]
        self.d_agx2_in = nc.dram_tensor("d_agx2_in", [D, 1], F32).ap()
        self.d_agx2_out = nc.dram_tensor("d_agx2_out", [4 * D, 1], F32).ap()
        self.d_agxb_in = [nc.dram_tensor(f"d_agxb_in{l}", [D, 1], F32).ap() for l in range(2)]
        self.d_agxb_out = [nc.dram_tensor(f"d_agxb_out{l}", [4 * D, 1], F32).ap() for l in range(2)]
        self.d_lh = nc.dram_tensor("d_lh", [1, D], BF16).ap()

        identf = nc.inline_tensor(np.eye(128, dtype=np.float32), "c_identf")
        identb = nc.inline_tensor(np.eye(128, dtype=BF16NP).view(np.uint16), "c_identb")
        ii, jj = np.arange(128)[:, None], np.arange(128)[None, :]
        msku = np.where(ii < jj, np.float32(NEGBIG), np.float32(0)).astype(np.float32)
        mskl = np.where(ii > jj, np.float32(NEGBIG), np.float32(0)).astype(np.float32)
        masku = nc.inline_tensor(msku, "c_masku")
        maskl = nc.inline_tensor(mskl, "c_maskl")
        maskub = nc.inline_tensor(msku.astype(BF16NP).view(np.uint16), "c_maskub")
        masklb = nc.inline_tensor(mskl.astype(BF16NP).view(np.uint16), "c_masklb")
        onesb = nc.inline_tensor(np.ones((128, 1), dtype=BF16NP).view(np.uint16), "c_onesb")
        onesrb = nc.inline_tensor(np.ones((1, 128), dtype=BF16NP).view(np.uint16), "c_onesrb")
        onesf = nc.inline_tensor(np.ones((1, 128), np.float32), "c_onesf")
        self.ACT_CONSTS = [1e-5, 1e-6, -1.0, -0.5, 8.0, 1.0 / D, 1.0, 0.0, 2.0, 1e-24]
        cval = nc.inline_tensor(np.broadcast_to(
            np.array(self.ACT_CONSTS, np.float32), (128, len(self.ACT_CONSTS))).copy(), "c_cval")
        self.c = dict(identf=identf.ap(), identb=identb.ap().bitcast(BF16),
                      masku=masku.ap(), maskl=maskl.ap(),
                      maskub=maskub.ap().bitcast(BF16), masklb=masklb.ap().bitcast(BF16),
                      onesb=onesb.ap().bitcast(BF16), onesrb=onesrb.ap().bitcast(BF16),
                      onesf=onesf.ap(), cval=cval.ap())

        sb = lambda name, shape, dt: es.enter_context(nc.sbuf_tensor(name, shape, dt))
        self.x = sb("x_fm", [128, 8 * WCOL], F32)
        self.hbuf = sb("hbuf", [128, 8 * WCOL], BF16)
        self.identf = sb("identf", [128, 128], F32)
        self.identb = sb("identb", [128, 128], BF16)
        self.masku = sb("masku", [128, 128], BF16)
        self.maskl = sb("maskl", [128, 128], BF16)
        self.maskpb = sb("maskpb", [128, 512], BF16)
        self.gate = sb("gate_s", [128, 1], F32)
        self.onesb = sb("onesb", [128, 1], BF16)
        self.onesrb = sb("onesrb", [1, 128], BF16)
        self.rowhi = sb("rowhi", [1, 512], BF16)
        self.rowlo = sb("rowlo", [1, 512], BF16)
        self.rowtmp = sb("rowtmp", [1, 512], F32)
        if self.taps:
            self._lndbg = sb("lndbg", [128, 8], F32)
            self._pbdbg = sb("pbdbg", [128, 144], BF16)
        self.cval = sb("cval_s", [128, len(self.ACT_CONSTS)], F32)
        for i, v in enumerate(self.ACT_CONSTS):
            try:
                nc.const_aps.register(F32, v, self.cval[:, i:i + 1])
            except Exception:
                pass
        self.bqt = sb("bqt", [128, 8], F32)
        self.nbqt = sb("nbqt", [128, 8], F32)
        self.bkt = sb("bkt", [128, 8], F32)
        self.nbkt = sb("nbkt", [128, 8], F32)
        self.bvt = sb("bvt", [128, 8], F32)
        self.bot = sb("bot", [128, 8], F32)
        self.lgt = [sb(f"lgt{l}", [128, 8], F32) for l in range(2)]
        self.lbt = [sb(f"lbt{l}", [128, 8], F32) for l in range(2)]
        self.b1at = [sb(f"b1at{l}", [128, NCA], F32) for l in range(2)]
        self.b1gt = [sb(f"b1gt{l}", [128, NCA], F32) for l in range(2)]
        self.b2t = [sb(f"b2t{l}", [128, 8], F32) for l in range(2)]
        self.kvtot = sb("kvtot", [128, 8], F32)
        self.kstot = sb("kstot", [128, 8], F32)
        self.qsrep = sb("qsrep", [128, DH], F32)
        self.ksrep = sb("ksrep", [128, DH], F32)
        self.strow = [sb(f"strow{i}", [1, 512], F32) for i in range(4)]
        self.pp = [sb(f"pp{i}", [128, 8], F32) for i in range(4)]
        self.lncast = sb("lncast", [128, 512], BF16)
        self.lnsq = sb("lnsq", [128, 512], BF16)
        self.lnt1 = sb("lnt1", [128, 512], F32)

        ps = lambda name, shape, dt: es.enter_context(nc.psum_tensor(name, shape, dt))
        self.pAB = ps("pAB", [128, 1024], F32)
        self.pCD = ps("pCD", [128, 1024], F32)
        self.pA = self.pAB.ap()[:, 0:512]
        self.pB = self.pAB.ap()[:, 512:1024]
        self.pC = self.pCD.ap()[:, 0:512]
        self.pD = self.pCD.ap()[:, 512:1024]
        self.pE = ps("pE", [128, 512], F32).ap()
        self.pF = ps("pF", [128, 512], F32).ap()
        self.pT1 = ps("pT1", [128, 1024], BF16)
        self.pT2 = ps("pT2", [128, 1024], BF16)
        self.PS = {"pA": self.pA, "pB": self.pB, "pC": self.pC, "pD": self.pD,
                   "pE": self.pE, "pF": self.pF, "pT1": self.pT1, "pT2": self.pT2,
                   "pABfull": self.pAB.ap(), "pCDfull": self.pCD.ap()}

        self.rt = {}

        def qprolog(eng):
            pid = eng.partition_id()
            self.rt["rr"] = (pid + 3) % 4
        self.em.prolog["Q"] = qprolog

    def xc(self, c, col0, w):
        return self.x[:, c * WCOL + col0: c * WCOL + col0 + w]

    def hc(self, c, col0, w):
        return self.hbuf[:, c * WCOL + col0: c * WCOL + col0 + w]

    def xkeys(self, c, col0, w=None):
        ks = [f"x{c}"]
        if col0 == 0:
            ks.append(f"xh{c}")
        return ks

    def hkeys(self, c, col0, w=None):
        ks = [f"h{c}"]
        if col0 == 0:
            ks.append(f"hh{c}")
        return ks

    def tap_sbuf(self, name, sbt, dt, keys):
        if name not in self.taps:
            return
        nc, em = self.nc, self.em
        out = nc.dram_tensor("tap_" + name, [sbt.shape[0], sbt.shape[1]], dt,
                             kind="ExternalOutput").ap()
        em.dma(lambda q: q.dma_start(out=out, in_=sbt[:]), reads=list(keys),
               writes=["tap_" + name], tag=f"tap_{name}")
        self.tap_names.append("tap_" + name)

    def load_consts(self):
        em = self.em
        for nm in ("identf", "identb", "onesb", "onesrb"):
            dst = getattr(self, nm)
            em.dma(lambda q, dst=dst, nm=nm: q.dma_start(out=dst[:], in_=self.c[nm]),
                   writes=[nm], tag=f"c_{nm}")
        em.dma(lambda q: q.dma_start(out=self.masku[:], in_=self.c["maskub"]), writes=["masku"])
        em.dma(lambda q: q.dma_start(out=self.maskl[:], in_=self.c["masklb"]), writes=["maskl"])
        em.dma(lambda q: q.dma_start(out=self.maskpb[:], in_=self.in_maskp), writes=["maskpb"])
        em.dma(lambda q: q.dma_start(out=self.cval[:], in_=self.c["cval"]), writes=["cval"])
        em.dma(lambda q: q.dma_start(out=self.gate[:], in_=self.in_gate), writes=["gate"])

        self.em.barrier()
        for i in range(160):
            em.op("T", lambda t: t.matmul(
                self.pE[:, 0:512], self.identb[:], self.hbuf[:, 0:512],
                start=True, stop=True, skip_group_check=True),
                reads=["identb"], writes=["warmup"], tag=f"warm{i}")

    def load_biases(self):
        em = self.em

        def colload(dst, src, ncol, key):
            for c in range(ncol):
                self.em.dma(lambda q, c=c, dst=dst, src=src: q.dma_start(
                    out=dst[:, c:c + 1], in_=src[c:c + 1, :].rearrange("a p -> p a")),
                    writes=[key], tag=f"{key}{c}")

        colload(self.bqt, self.in_b["bq"], 8, "bqt")
        colload(self.bkt, self.in_b["bk"], 8, "bkt")
        colload(self.bvt, self.in_b["bv"], 8, "bvt")
        colload(self.bot, self.in_b["bo"], 8, "bot")
        for l in range(2):
            colload(self.lgt[l], self.in_lg[l], 8, f"lgt{l}")
            colload(self.lbt[l], self.in_lb[l], 8, f"lbt{l}")
            colload(self.b2t[l], self.in_b2[l], 8, f"b2t{l}")
            colload(self.b1at[l], self.in_b1a[l], NCA, f"b1at{l}")
            colload(self.b1gt[l], self.in_b1g[l], NCA, f"b1gt{l}")
        em.op("V", lambda v: v.tensor_single_scalar(self.nbqt[:], self.bqt[:], -1.0, ALU.mult),
              reads=["bqt"], writes=["nbqt"])
        em.op("V", lambda v: v.tensor_single_scalar(self.nbkt[:], self.bkt[:], -1.0, ALU.mult),
              reads=["bkt"], writes=["nbkt"])
        em.dma(lambda q: q.dma_start(out=self.cval[:], in_=self.c["cval"]), writes=["cval"])

    def emit_replicate(self, row, rowk, ps, psk, wdt, tag, scale=None):
        em = self.em
        src = row
        if scale is not None:
            em.op("S", lambda s: s.activation(self.rowtmp[0:1, :wdt], row[0:1, :wdt],
                                              AF.Copy, scale=scale),
                  reads=[rowk], writes=["rowtmp"], tag=tag + "sc")
            src = self.rowtmp
            rowk = "rowtmp"
        em.op("S", lambda s: s.copy(self.rowhi[0:1, :wdt], src[0:1, :wdt]),
              reads=[rowk], writes=["rowhi"], tag=tag + "hi")
        em.op("V", lambda v: v.tensor_tensor(self.rowtmp[0:1, :wdt], src[0:1, :wdt],
                                             self.rowhi[0:1, :wdt], ALU.subtract),
              reads=[rowk, "rowhi"], writes=["rowtmp"], tag=tag + "sub")
        em.op("S", lambda s: s.copy(self.rowlo[0:1, :wdt], self.rowtmp[0:1, :wdt]),
              reads=["rowtmp"], writes=["rowlo"], tag=tag + "lo")
        em.op("T", lambda t: t.matmul(ps[:, :wdt], self.onesrb[:], self.rowhi[0:1, :wdt],
                                      start=True, stop=False, skip_group_check=True),
              reads=["rowhi", "onesrb"], writes=[psk], tag=tag + "mm1")
        em.op("T", lambda t: t.matmul(ps[:, :wdt], self.onesrb[:], self.rowlo[0:1, :wdt],
                                      start=False, stop=True, skip_group_check=True),
              reads=["rowlo", "onesrb"], writes=[psk], tag=tag + "mm2")

    def _rowstats(self, em, pSum, kSum, pSq, kSq, wdt, eps, rstd_row, nmr_row, krow, tag):
        em.op("S", lambda s: s.activation(rstd_row[0:1, :wdt], pSum[0:1, :wdt], AF.Copy,
                                          scale=1.0 / D),
              reads=[kSum], writes=[krow], tag=tag + "_mu")
        em.op("S", lambda s: s.activation(nmr_row[0:1, :wdt], rstd_row[0:1, :wdt], AF.Square),
              reads=[krow], writes=[krow])
        em.op("V", lambda v: v.scalar_tensor_tensor(nmr_row[0:1, :wdt], pSq[0:1, :wdt],
                                                    1.0 / D, nmr_row[0:1, :wdt],
                                                    ALU.mult, ALU.subtract),
              reads=[kSq, krow], writes=[krow])
        em.op("S", lambda s: s.activation(nmr_row[0:1, :wdt], nmr_row[0:1, :wdt], AF.Ln,
                                          bias=eps),
              reads=[krow], writes=[krow])
        em.op("S", lambda s: s.activation(nmr_row[0:1, :wdt], nmr_row[0:1, :wdt], AF.Exp,
                                          scale=-0.5),
              reads=[krow], writes=[krow])
        em.op("V", lambda v: v.scalar_tensor_tensor(rstd_row[0:1, :wdt], rstd_row[0:1, :wdt],
                                                    -1.0, nmr_row[0:1, :wdt],
                                                    ALU.mult, ALU.mult),
              reads=[krow], writes=[krow])

    def _ln_stats_blocks(self, src_is_x, blocks, layer_shift, tag, apply_fn=None):
        em = self.em
        pend = [None]

        def flush():
            if pend[0] is not None:
                pend[0]()
                pend[0] = None

        out = []
        for bi, (col0, wdt) in enumerate(blocks):
            par = bi % 2
            kSa, kSb = [("pC", "pD"), ("pA", "pB")][par]
            pSa, pSb = self.PS[kSa], self.PS[kSb]
            r1, r2 = self.strow[2 * par], self.strow[2 * par + 1]
            krow = f"strow{par}{tag}"
            for c in range(8):
                if src_is_x:
                    sc0 = col0 - 1 if (layer_shift and c >= 4) else col0
                    srck = self.xkeys(c, sc0)
                    src_ap = self.xc(c, sc0, wdt)
                    em.op("G", lambda g, src_ap=src_ap, wdt=wdt: g.tensor_copy(
                        self.lncast[:, :wdt], src_ap), reads=srck, writes=["lncast"],
                        tag=tag + "cast")
                    data = self.lncast[:, :wdt]
                    datak = ["lncast"]
                    em.op("S", lambda s, src_ap=src_ap, wdt=wdt: s.activation(
                        self.lnsq[:, :wdt], src_ap, AF.Square), reads=srck,
                        writes=["lnsq"], tag=tag + "sq")
                else:
                    srck = self.hkeys(c, col0)
                    data = self.hc(c, col0, wdt)
                    datak = srck
                    em.op("S", lambda s, c=c, col0=col0, wdt=wdt: s.activation(
                        self.lnsq[:, :wdt], self.hc(c, col0, wdt), AF.Square),
                        reads=srck, writes=["lnsq"], tag=tag + "sq")
                em.op("T", lambda t, data=data, wdt=wdt, c=c, pSa=pSa: t.matmul(
                    pSa[0:1, :wdt], self.onesb[:], data,
                    start=(c == 0), stop=(c == 7)),
                    reads=datak + ["onesb"], writes=[kSa], tag=tag + "mmS")
                em.op("T", lambda t, wdt=wdt, c=c, pSb=pSb: t.matmul(
                    pSb[0:1, :wdt], self.onesb[:], self.lnsq[:, :wdt],
                    start=(c == 0), stop=(c == 7)),
                    reads=["lnsq", "onesb"], writes=[kSb], tag=tag + "mmQ")
            self._rowstats(em, pSa, kSa, pSb, kSb, wdt, 1e-5, r1, r2, krow, tag)
            flush()

            def tail(bi=bi, col0=col0, wdt=wdt, r1=r1, r2=r2, krow=krow):
                self.emit_replicate(r2, krow, self.pE, "pE", wdt, tag + "rep1")
                self.emit_replicate(r1, krow, self.pF, "pF", wdt, tag + "rep2")
                if apply_fn is not None:
                    apply_fn(bi, col0, wdt)
            if apply_fn is None:
                tail()
                out.append((bi, col0, wdt))
            else:
                pend[0] = tail
        flush()
        if apply_fn is None:
            return out
        return None

    def step0(self):
        nc, em, es = self.nc, self.em, self.es
        sc = ExitStack()
        xt = [sc.enter_context(nc.sbuf_tensor(f"xt{i}", [128, 1024], F32)) for i in range(2)]
        lt = [sc.enter_context(nc.sbuf_tensor(f"lt{i}", [128, 1024], BF16)) for i in range(2)]
        sq0 = [sc.enter_context(nc.sbuf_tensor(f"sq0{i}", [128, 1024], F32)) for i in range(2)]
        st0 = [sc.enter_context(nc.sbuf_tensor(f"st0{i}", [128, 6], F32)) for i in range(2)]
        sth = sc.enter_context(nc.sbuf_tensor("sth", [1, 6], F32))
        self.xrow = sc.enter_context(nc.sbuf_tensor("xrow", [1, 1024], F32))
        sqh = sc.enter_context(nc.sbuf_tensor("sqh", [1, 1024], F32))
        lh = sc.enter_context(nc.sbuf_tensor("lh", [1, 1024], BF16))

        def tok_stats(src_ap, sq_t, st_t, pcnt, kin, kst):
            em.op("S", lambda s: s.activation(sq_t[:pcnt, :], src_ap, AF.Square),
                  reads=[kin], writes=[kst + "sq"])
            em.op("V", lambda v: v.tensor_reduce(st_t[:pcnt, 0:1], src_ap, AX.X, ALU.add),
                  reads=[kin], writes=[kst])
            em.op("V", lambda v: v.tensor_reduce(st_t[:pcnt, 1:2], sq_t[:pcnt, :], AX.X, ALU.add),
                  reads=[kst + "sq"], writes=[kst])
            em.op("V", lambda v: v.tensor_single_scalar(st_t[:pcnt, 2:3], st_t[:pcnt, 0:1],
                                                        1.0 / D, ALU.mult),
                  reads=[kst], writes=[kst])
            em.op("S", lambda s: s.activation(st_t[:pcnt, 3:4], st_t[:pcnt, 2:3], AF.Square),
                  reads=[kst], writes=[kst])
            em.op("V", lambda v: v.scalar_tensor_tensor(st_t[:pcnt, 3:4], st_t[:pcnt, 1:2],
                                                        1.0 / D, st_t[:pcnt, 3:4],
                                                        ALU.mult, ALU.subtract),
                  reads=[kst], writes=[kst])
            em.op("S", lambda s: s.activation(st_t[:pcnt, 3:4], st_t[:pcnt, 3:4], AF.Ln,
                                              bias=1e-5),
                  reads=[kst], writes=[kst])
            em.op("S", lambda s: s.activation(st_t[:pcnt, 4:5], st_t[:pcnt, 3:4], AF.Exp,
                                              scale=-0.5),
                  reads=[kst], writes=[kst])
            em.op("V", lambda v: v.scalar_tensor_tensor(st_t[:pcnt, 5:6], st_t[:pcnt, 2:3],
                                                        -1.0, st_t[:pcnt, 4:5],
                                                        ALU.mult, ALU.mult),
                  reads=[kst], writes=[kst])

        em.dma(lambda q: q.dma_start(out=self.xrow[:], in_=self.in_xh), writes=["xrow"])
        tok_stats(self.xrow[0:1, :], sqh, sth, 1, "xrow", "sth")
        em.op("S", lambda s: s.activation(lh[:], self.xrow[0:1, :], AF.Identity,
                                          scale=sth[0:1, 4:5], bias=sth[0:1, 5:6]),
              reads=["xrow", "sth"], writes=["lhrow"])
        em.dma(lambda q: q.dma_start(out=self.d_lh, in_=lh[:]), reads=["lhrow"],
               writes=["d_lh"])
        for c in range(8):
            em.dma(lambda q, c=c: q.dma_start(
                out=self.xc(c, 0, 1),
                in_=self.in_xh[0:1, c * 128:(c + 1) * 128].rearrange("a p -> p a")),
                writes=[f"xh{c}"], tag=f"xh{c}")
            em.dma(lambda q, c=c: q.dma_start(
                out=self.hc(c, 0, 1),
                in_=self.d_lh[0:1, c * 128:(c + 1) * 128].rearrange("a p -> p a")),
                reads=["d_lh"], writes=[f"hh{c}"], tag=f"lh{c}")

        for i in range(16):
            p = i % 2
            kx, kl, kst = f"xt{p}", f"lt{p}", f"st{p}"
            em.dma(lambda q, i=i, p=p: q.dma_start(
                out=xt[p][:], in_=self.in_x[i * 128:(i + 1) * 128, :]),
                writes=[kx], tag=f"ldx{i}")
            tok_stats(xt[p][:, :], sq0[p], st0[p], 128, kx, kst)
            em.op("S", lambda s, p=p: s.activation(lt[p][:], xt[p][:], AF.Identity,
                                                   scale=st0[p][:, 4:5], bias=st0[p][:, 5:6]),
                  reads=[kx, kst], writes=[kl])
            for half in range(2):
                pk = "pA" if half == 0 else "pB"
                pst = self.PS[pk]
                for j in range(4):
                    c = half * 4 + j
                    em.op("T", lambda t, p=p, c=c, j=j, pst=pst: t.matmul(
                        pst[:, j * 128:(j + 1) * 128], xt[p][:, c * 128:(c + 1) * 128],
                        self.identf[:], start=(j == 0), stop=(j == 3),
                        is_transpose=True, skip_group_check=True),
                        reads=[kx, "identf"], writes=[pk])
                dst = view3(self.x[:], half * 4 * WCOL + 1 + i * 128, WCOL, 4, 128)
                em.op("V", lambda v, pst=pst, dst=dst: v.tensor_copy(dst, pst[:]),
                      reads=[pk], writes=[f"x{half * 4 + j2}" for j2 in range(4)])
            ptk = "pT1" if p == 0 else "pT2"
            pt = self.PS[ptk]
            for c in range(8):
                em.op("T", lambda t, p=p, c=c, pt=pt: t.matmul(
                    pt[:, c * 128:(c + 1) * 128], lt[p][:, c * 128:(c + 1) * 128],
                    self.identb[:], start=(c == 0), stop=(c == 7),
                    is_transpose=True, skip_group_check=True),
                    reads=[kl, "identb"], writes=[ptk])
            dsth = view3(self.hbuf[:], 1 + i * 128, WCOL, 8, 128)
            em.op("S", lambda s, pt=pt, dsth=dsth: s.copy(dsth, pt[:]),
                  reads=[ptk], writes=[f"h{c2}" for c2 in range(8)])
        self.tap_sbuf("x_fm", self.x, F32,
                      [f"x{c}" for c in range(8)] + [f"xh{c}" for c in range(8)])
        self.tap_sbuf("ln1", self.hbuf, BF16,
                      [f"h{c}" for c in range(8)] + [f"hh{c}" for c in range(8)])
        em.barrier()
        sc.close()

    def attn_block(self):
        nc, em, es = self.nc, self.em, self.es
        sc = ExitStack()
        wA = sc.enter_context(nc.sbuf_tensor("wA", [128, 8 * 1024], BF16))
        wB = sc.enter_context(nc.sbuf_tensor("wB", [128, 8 * 1024], BF16))
        stg = [sc.enter_context(nc.sbuf_tensor(f"stg{i}", [128, 1024], F32)) for i in range(2)]
        attnv = sc.enter_context(nc.sbuf_tensor("attnv", [128, 8 * WCOL], BF16))
        scr = [sc.enter_context(nc.sbuf_tensor(f"scrA{i}", [128, 512], F32)) for i in range(6)]
        kvacc = sc.enter_context(nc.sbuf_tensor("kvacc", [128, 32], F32))
        ksacc = sc.enter_context(nc.sbuf_tensor("ksacc", [128, 32], F32))

        def load_w(src, dst, dkey, tag):
            for kc in range(8):
                p = kc % 2
                em.dma(lambda q, kc=kc, p=p, src=src: q.dma_start(
                    out=stg[p][:], in_=src[kc * 128:(kc + 1) * 128, :]),
                    writes=[f"stg{p}"], tag=f"{tag}{kc}")
                em.op("G", lambda g, kc=kc, p=p, dst=dst: g.tensor_copy(
                    dst[:, kc * 1024:(kc + 1) * 1024], stg[p][:]),
                    reads=[f"stg{p}"], writes=[dkey], tag=f"{tag}c{kc}")

        def fmap(ps, pk, bias, nbias, biask, s1, k1, s2, k2, outt, outk, wdt):
            em.op("S", lambda s: s.activation(s1[:, :wdt], ps[:, :wdt], AF.Relu, bias=bias),
                  reads=[pk, biask], writes=[k1])
            em.op("S", lambda s: s.activation(s2[:, :wdt], ps[:, :wdt], AF.Relu,
                                              bias=nbias, scale=-1.0),
                  reads=[pk, "n" + biask], writes=[k2])
            em.op("S", lambda s: s.activation(s2[:, :wdt], s2[:, :wdt], AF.Exp, scale=-1.0),
                  reads=[k2], writes=[k2])
            em.op("G", lambda g: g.tensor_add(outt[:, :wdt], s1[:, :wdt], s2[:, :wdt]),
                  reads=[k1, k2], writes=[outk])

        load_w(self.in_W["Wk"], wA, "wA", "ldWk")
        load_w(self.in_W["Wv"], wB, "wB", "ldWv")

        for m in range(8):
            for cb in range(4):
                col0 = 1 + cb * 512
                pkk = "pA" if cb % 2 == 0 else "pB"
                pvk = "pC" if cb % 2 == 0 else "pD"
                pk, pv = self.PS[pkk], self.PS[pvk]
                for kc in range(8):
                    em.op("T", lambda t, m=m, kc=kc, col0=col0, pk=pk: t.matmul(
                        pk[:], wA[:, kc * 1024 + m * 128: kc * 1024 + (m + 1) * 128],
                        self.hc(kc, col0, 512), start=(kc == 0), stop=(kc == 7)),
                        reads=["wA", f"h{kc}"], writes=[pkk])
                for kc in range(8):
                    em.op("T", lambda t, m=m, kc=kc, col0=col0, pv=pv: t.matmul(
                        pv[:], wB[:, kc * 1024 + m * 128: kc * 1024 + (m + 1) * 128],
                        self.hc(kc, col0, 512), start=(kc == 0), stop=(kc == 7)),
                        reads=["wB", f"h{kc}"], writes=[pvk])
                fmap(pk, pkk, self.bkt[:, m:m + 1], self.nbkt[:, m:m + 1], "bkt",
                     scr[1], "scr1", scr[2], "scr2", scr[0], "scr0", 512)
                em.op("S", lambda s, pv=pv, m=m: s.activation(
                    scr[3][:], pv[:], AF.Identity, bias=self.bvt[:, m:m + 1]),
                    reads=[pvk, "bvt"], writes=["scr3"])
                em.op("G", lambda g: g.tensor_mul(scr[4][:], scr[0][:], scr[3][:]),
                      reads=["scr0", "scr3"], writes=["scr4"])
                em.op("V", lambda v, m=m, cb=cb: v.tensor_reduce(
                    kvacc[:, m * 4 + cb: m * 4 + cb + 1], scr[4][:], AX.X, ALU.add),
                    reads=["scr4"], writes=["kvacc"])
                em.op("V", lambda v, m=m, cb=cb: v.tensor_reduce(
                    ksacc[:, m * 4 + cb: m * 4 + cb + 1], scr[0][:], AX.X, ALU.add),
                    reads=["scr0"], writes=["ksacc"])
        em.op("V", lambda v: v.tensor_reduce(self.pp[0][:, 0:8],
                                             view3(kvacc[:], 0, 4, 8, 4), AX.X, ALU.add),
              reads=["kvacc"], writes=["pp0"])
        em.op("V", lambda v: v.tensor_reduce(self.pp[1][:, 0:8],
                                             view3(ksacc[:], 0, 4, 8, 4), AX.X, ALU.add),
              reads=["ksacc"], writes=["pp1"])
        for m in range(8):
            em.dma(lambda q, m=m: q.dma_start(
                out=self.d_ags_in[0:1, m * 128:(m + 1) * 128].rearrange("a p -> p a"),
                in_=self.pp[0][:, m:m + 1]),
                reads=["pp0"], writes=["d_ags_in"])
            em.dma(lambda q, m=m: q.dma_start(
                out=self.d_ags_in[1:2, m * 128:(m + 1) * 128].rearrange("a p -> p a"),
                in_=self.pp[1][:, m:m + 1]),
                reads=["pp1"], writes=["d_ags_in"])
        em.coll(lambda g: g.collective_compute(
            "AllGather", ALU.bypass, replica_groups=[[0, 1, 2, 3], [4, 5, 6, 7]],
            ins=[self.d_ags_in], outs=[self.d_ags_out]),
            reads=["d_ags_in"], writes=["d_ags_out"], tag="AGsums")
        load_w(self.in_W["Wq"], wA, "wA", "ldWq")
        load_w(self.in_W["Wo"], wB, "wB", "ldWo")
        for m in range(8):
            for r in range(8):
                em.dma(lambda q, m=m, r=r: q.dma_start(
                    out=self.pp[2][:, r:r + 1],
                    in_=self.d_ags_out[r:r + 1, m * 128:(m + 1) * 128].rearrange("a p -> p a")),
                    reads=["d_ags_out"], writes=["pp2"], tag=f"agr{m}_{r}")
            pstep = self.pp[2].ap().ap[0][0]
            kvv = bass.AP(self.pp[2].ap().tensor, 0, [[pstep, 128], [2, 4]])
            ksv = bass.AP(self.pp[2].ap().tensor, 1, [[pstep, 128], [2, 4]])
            em.op("V", lambda v, m=m, kvv=kvv: v.tensor_reduce(
                self.kvtot[:, m:m + 1], kvv, AX.X, ALU.add),
                reads=["pp2"], writes=["kvtot"])
            em.op("V", lambda v, m=m, ksv=ksv: v.tensor_reduce(
                self.kstot[:, m:m + 1], ksv, AX.X, ALU.add),
                reads=["pp2"], writes=["kstot"])
        blocks5 = [(0, 512), (512, 512), (1024, 512), (1536, 512), (2048, 1)]
        for m in range(8):
            for bi, (col0, wdt) in enumerate(blocks5):
                pqk = "pA" if bi % 2 == 0 else "pB"
                pq = self.PS[pqk]
                for kc in range(8):
                    em.op("T", lambda t, m=m, kc=kc, col0=col0, wdt=wdt, pq=pq: t.matmul(
                        pq[:, :wdt], wA[:, kc * 1024 + m * 128: kc * 1024 + (m + 1) * 128],
                        self.hc(kc, col0, wdt), start=(kc == 0), stop=(kc == 7)),
                        reads=["wA"] + self.hkeys(kc, col0), writes=[pqk])
                fmap(pq, pqk, self.bqt[:, m:m + 1], self.nbqt[:, m:m + 1], "bqt",
                     scr[1], "scr1", scr[2], "scr2", scr[5], "scr5", wdt)
                em.op("G", lambda g, m=m, wdt=wdt: g.tensor_tensor(
                    scr[3][:, :wdt], scr[5][:, :wdt], bc_free(self.kstot, wdt, m), ALU.mult),
                    reads=["scr5", "kstot"], writes=["scr3"])
                em.op("S", lambda s, wdt=wdt: s.activation(scr[3][:, :wdt], scr[3][:, :wdt],
                                                           AF.Ln, bias=1e-6),
                      reads=["scr3"], writes=["scr3"])
                em.op("S", lambda s, wdt=wdt: s.activation(scr[3][:, :wdt], scr[3][:, :wdt],
                                                           AF.Exp, scale=-1.0),
                      reads=["scr3"], writes=["scr3"])
                em.op("G", lambda g, m=m, wdt=wdt: g.tensor_tensor(
                    scr[4][:, :wdt], scr[5][:, :wdt], bc_free(self.kvtot, wdt, m), ALU.mult),
                    reads=["scr5", "kvtot"], writes=["scr4"])
                em.op("V", lambda v, m=m, col0=col0, wdt=wdt: v.tensor_tensor(
                    attnv[:, m * WCOL + col0: m * WCOL + col0 + wdt],
                    scr[4][:, :wdt], scr[3][:, :wdt], ALU.mult),
                    reads=["scr3", "scr4"], writes=[f"av{m}"])
        self.tap_sbuf("attn_v", attnv, BF16, [f"av{m}" for m in range(8)])

        for bi, (col0, wdt) in enumerate(blocks5):
            for m in range(8):
                pwk = "pA" if (bi * 8 + m) % 2 == 0 else "pB"
                pw = self.PS[pwk]
                for kc in range(8):
                    em.op("T", lambda t, m=m, kc=kc, col0=col0, wdt=wdt, pw=pw: t.matmul(
                        pw[:, :wdt], wB[:, kc * 1024 + m * 128: kc * 1024 + (m + 1) * 128],
                        attnv[:, kc * WCOL + col0: kc * WCOL + col0 + wdt],
                        start=(kc == 0), stop=(kc == 7)),
                        reads=["wB", f"av{kc}"], writes=[pwk])
                wk = [f"h{m}"] + ([f"hh{m}"] if col0 == 0 else [])
                em.op("S", lambda s, m=m, col0=col0, wdt=wdt, pw=pw: s.activation(
                    self.hc(m, col0, wdt), pw[:, :wdt], AF.Gelu_apprx_tanh,
                    bias=self.bot[:, m:m + 1]),
                    reads=[pwk, "bot"], writes=wk)
        self.tap_sbuf("res", self.hbuf, BF16,
                      [f"h{c}" for c in range(8)] + [f"hh{c}" for c in range(8)])

        blocks5o = [(0, 512), (512, 512), (1024, 512), (1536, 512), (1537, 512)]

        def apply_lnres(bi, col0, wdt):
            ov = (bi == 4)
            for c in range(8):
                em.op("V", lambda v, c=c, col0=col0, wdt=wdt: v.tensor_tensor(
                    self.lnt1[:, :wdt], self.hc(c, col0, wdt), self.pE[:, :wdt], ALU.mult),
                    reads=self.hkeys(c, col0) + ["pE"], writes=["lnt1"])
                em.op("V", lambda v, wdt=wdt: v.tensor_tensor(
                    self.lnt1[:, :wdt], self.lnt1[:, :wdt], self.pF[:, :wdt], ALU.add),
                    reads=["lnt1", "pF"], writes=["lnt1"])
                wk = self.xkeys(c, col0)
                if not ov:
                    em.op("V", lambda v, c=c, col0=col0, wdt=wdt: v.tensor_tensor(
                        self.xc(c, col0, wdt), self.xc(c, col0, wdt), self.lnt1[:, :wdt],
                        ALU.add),
                        reads=["lnt1"] + wk, writes=wk)
                else:
                    em.op("S", lambda s_, c=c: s_.activation(
                        self.xc(c, 2048, 1), self.lnt1[:, 511:512], AF.Identity,
                        bias=self.xc(c, 2048, 1)),
                        reads=["lnt1", f"x{c}"], writes=[f"x{c}"])

        self._ln_stats_blocks(False, blocks5o, None, "lnres", apply_fn=apply_lnres)
        self.tap_sbuf("x_res1", self.x, F32,
                      [f"x{c}" for c in range(8)] + [f"xh{c}" for c in range(8)])
        for c in range(8):
            em.dma(lambda q, c=c: q.dma_start(out=self.d_xres1[c], in_=self.xc(c, 1, T)),
                   reads=[f"x{c}"], writes=["d_xres1"], tag=f"spill{c}")
        blocks5x = [(1537, 512), (0, 512), (512, 512), (1024, 512), (1536, 512)]

        def apply_lnx1(bi, col0, wdt):
            ov = (bi == 0)
            for c in range(8):
                rk = self.xkeys(c, col0)
                em.op("V", lambda v, c=c, col0=col0, wdt=wdt: v.tensor_tensor(
                    self.lnt1[:, :wdt], self.xc(c, col0, wdt), self.pE[:, :wdt], ALU.mult),
                    reads=rk + ["pE"], writes=["lnt1"])
                if not ov:
                    em.op("V", lambda v, c=c, col0=col0, wdt=wdt: v.tensor_tensor(
                        self.xc(c, col0, wdt), self.lnt1[:, :wdt], self.pF[:, :wdt], ALU.add),
                        reads=["lnt1", "pF"], writes=rk)
                else:
                    em.op("V", lambda v, wdt=wdt: v.tensor_tensor(
                        self.lnt1[:, :wdt], self.lnt1[:, :wdt], self.pF[:, :wdt], ALU.add),
                        reads=["lnt1", "pF"], writes=["lnt1"])
                    em.op("S", lambda s_, c=c: s_.copy(
                        self.xc(c, 2048, 1), self.lnt1[:, 511:512]),
                        reads=["lnt1"], writes=[f"x{c}"])

        self._ln_stats_blocks(True, blocks5x, None, "lnx1", apply_fn=apply_lnx1)
        for c in range(8):
            em.op("S", lambda s_, c=c: s_.activation(
                self.xc(c, 0, 1), self.xc(c, 0, 1), AF.Identity, scale=self.gate[:]),
                reads=[f"xh{c}", "gate"], writes=[f"xh{c}"])
        self.tap_sbuf("x1", self.x, F32,
                      [f"x{c}" for c in range(8)] + [f"xh{c}" for c in range(8)])
        em.barrier()
        sc.close()

    def layer(self, l):
        nc, em, es = self.nc, self.em, self.es
        sc_outer = ExitStack()
        q_fm = sc_outer.enter_context(nc.sbuf_tensor(f"q_fm{l}", [128, 4 * 2048], BF16))
        k_fm = sc_outer.enter_context(nc.sbuf_tensor(f"k_fm{l}", [128, 4 * 2560], BF16))
        v_tm = sc_outer.enter_context(nc.sbuf_tensor(f"v_tm{l}", [128, 20 * 512], BF16))
        qlo_fm = sc_outer.enter_context(nc.sbuf_tensor(f"qlo_fm{l}", [128, 512], BF16))
        klo_fm = sc_outer.enter_context(nc.sbuf_tensor(f"klo_fm{l}", [128, 512], BF16))
        AG = [[0, 1, 2, 3], [4, 5, 6, 7]]

        scA = ExitStack()
        self.rotc = scA.enter_context(nc.sbuf_tensor(f"rotcL{l}", [128, 16 * 128], F32))
        tabq = scA.enter_context(nc.sbuf_tensor(f"tabqL{l}", [128, 16 * 128], BF16))
        tabk = scA.enter_context(nc.sbuf_tensor(f"tabkL{l}", [128, 16 * 128], BF16))
        wpart = scA.enter_context(nc.sbuf_tensor(f"wpart{l}", [128, 8 * 512], BF16))
        stg = [scA.enter_context(nc.sbuf_tensor(f"stgL{l}_{i}", [128, 512], F32)) for i in range(1)]
        sqs = scA.enter_context(nc.sbuf_tensor(f"sqs{l}", [128, 512], F32))
        ss8 = scA.enter_context(nc.sbuf_tensor(f"ss8{l}", [128, 8], F32))
        rs8 = scA.enter_context(nc.sbuf_tensor(f"rs8{l}", [128, 8], F32))
        qn = scA.enter_context(nc.sbuf_tensor(f"qn{l}", [128, 512], F32))
        u = [self.rotc[:, 1024 + i * 256: 1024 + (i + 1) * 256] for i in range(4)]
        qrot = scA.enter_context(nc.sbuf_tensor(f"qrot{l}", [128, 512], BF16))
        qsrow = scA.enter_context(nc.sbuf_tensor(f"qsrow{l}", [1, DH], F32))
        ksrow = scA.enter_context(nc.sbuf_tensor(f"ksrow{l}", [1, DH], F32))

        if l == 1:
            for c in range(8):
                em.dma(lambda q, c=c: q.dma_start(
                    out=self.d_agx2_in[c * 128:(c + 1) * 128, 0:1],
                    in_=self.xc(c, 2048, 1)),
                    reads=[f"x{c}"], writes=["d_agx2_in"], tag=f"x2o{c}")
            em.coll(lambda g: g.collective_compute(
                "AllGather", ALU.bypass, replica_groups=AG,
                ins=[self.d_agx2_in], outs=[self.d_agx2_out]),
                reads=["d_agx2_in"], writes=["d_agx2_out"], tag="AGx2")
            for c in range(8):
                em.dma(lambda q, c=c: q.dma_start(
                    out=self.xc(c, 0, 1),
                    in_=self.d_agx2_out[bass.ts(self.rt["rr"] * 8 + c, 128), 0:1]),
                    reads=["d_agx2_out"], writes=[f"xh{c}"], tag=f"x2i{c}")
                em.op("S", lambda s_, c=c: s_.activation(
                    self.xc(c, 0, 1), self.xc(c, 0, 1), AF.Identity,
                    scale=self.gate[:]),
                    reads=[f"xh{c}", "gate"], writes=[f"xh{c}"])

        em.dma(lambda q: q.dma_start(
            out=self.rotc[:].rearrange("p (i c) -> p i c", c=128),
            in_=self.in_rotc.rearrange("(i p) c -> p i c", p=128)),
            writes=["rotc"])
        em.dma(lambda q: q.dma_start(out=qsrow[:], in_=self.in_qs[l]), writes=["qsrow"])
        em.dma(lambda q: q.dma_start(out=ksrow[:], in_=self.in_ks[l]), writes=["ksrow"])
        self.emit_replicate(qsrow, "qsrow", self.pE, "pE", DH, "repqs", scale=8.0)
        em.op("S", lambda s: s.copy(self.qsrep[:], self.pE[:, 0:DH]),
              reads=["pE"], writes=["qsrep"])
        self.emit_replicate(ksrow, "ksrow", self.pF, "pF", DH, "repks")
        em.op("S", lambda s: s.copy(self.ksrep[:], self.pF[:, 0:DH]),
              reads=["pF"], writes=["ksrep"])
        for i in range(16):
            base = i * 128
            for (tab, tkey, rep, repk, c0) in ((tabq, "tabq", self.qsrep, "qsrep", 0),
                                               (tabk, "tabk", self.ksrep, "ksrep", 64)):
                em.op("G", lambda g, base=base, tab=tab, rep=rep, c0=c0: g.tensor_tensor(
                    view3(tab[:], base, 32, 2, 32),
                    bc_free3(self.rotc, base + c0, 2, 32),
                    view3(rep[:], 0, 32, 2, 32), ALU.mult),
                    reads=["rotc", repk], writes=[tkey])
                em.op("G", lambda g, base=base, tab=tab, rep=rep, c0=c0: g.tensor_tensor(
                    tab[:, base + 64: base + 96],
                    self.rotc[:, base + c0 + 32: base + c0 + 64],
                    rep[:, 32:64], ALU.mult),
                    reads=["rotc", repk], writes=[tkey])
                em.op("G", lambda g, base=base, tab=tab, rep=rep, c0=c0: g.tensor_tensor(
                    tab[:, base + 96: base + 128],
                    self.rotc[:, base + c0 + 32: base + c0 + 64],
                    rep[:, 0:32], ALU.mult),
                    reads=["rotc", repk], writes=[tkey])

        order = [12, 13, 14, 15] + (list(range(0, 12)) if l == 0 else list(range(1, 12)) + [0])

        def load_part(off, tag):
            flushA()
            for kc in range(8):
                p = 0
                em.dma(lambda q, kc=kc, p=p, off=off: q.dma_start(
                    out=stg[p][:],
                    in_=self.in_wqkv[l][kc * 128:(kc + 1) * 128, off:off + 512]),
                    writes=[f"stgL{p}"], tag=f"{tag}{kc}")
                em.op("G", lambda g, kc=kc, p=p: g.tensor_copy(
                    wpart[:, kc * 512:(kc + 1) * 512], stg[p][:]),
                    reads=[f"stgL{p}"], writes=["wpart"])

        def qk_post(i, which, PP, ppk, tab, tkey, dst, dkey, dst_off, dst_stride, tpk, TPK, toff):
            em.op("S", lambda s, PP=PP: s.activation(sqs[:], PP[:], AF.Square),
                  reads=[ppk], writes=["sqs"])
            em.op("V", lambda v: v.tensor_reduce(ss8[:, 0:8], view3(sqs[:], 0, 64, 8, 64),
                                                 AX.X, ALU.add),
                  reads=["sqs"], writes=["ss8"])
            em.op("S", lambda s: s.activation(ss8[:], ss8[:], AF.Sqrt, bias=1e-24),
                  reads=["ss8"], writes=["ss8"])
            em.op("V", lambda v: v.reciprocal(rs8[:], ss8[:]),
                  reads=["ss8"], writes=["rs8"])
            rinv_bc = bass.AP(rs8.ap().tensor, 0,
                              [[rs8.ap().ap[0][0], 128], [1, 8], [0, 64]])
            em.op("V", lambda v, PP=PP, rinv_bc=rinv_bc: v.tensor_tensor(
                qn[:].rearrange("p (h d) -> p h d", h=8),
                PP[:].rearrange("p (h d) -> p h d", h=8), rinv_bc, ALU.mult),
                reads=[ppk, "rs8"], writes=["qn"])
            base = i * 128
            qlo = view3(qn[:], 0, 64, 8, 32)
            qhi = view3(qn[:], 32, 64, 8, 32)
            em.op("G", lambda g, tab=tab, base=base, qlo=qlo: g.tensor_tensor(
                u[0].rearrange("p (h d) -> p h d", h=8), qlo,
                bc_free3(tab, base, 8, 32), ALU.mult),
                reads=["qn", tkey], writes=["u0", "rotc"])
            em.op("G", lambda g, tab=tab, base=base, qhi=qhi: g.tensor_tensor(
                u[1].rearrange("p (h d) -> p h d", h=8), qhi,
                bc_free3(tab, base + 64, 8, 32), ALU.mult),
                reads=["qn", tkey], writes=["u1", "rotc"])
            em.op("V", lambda v: v.tensor_tensor(
                view3(qrot[:], 0, 64, 8, 32), u[0].rearrange("p (h d) -> p h d", h=8),
                u[1].rearrange("p (h d) -> p h d", h=8), ALU.subtract),
                reads=["u0", "u1"], writes=["qrot"])
            em.op("G", lambda g, tab=tab, base=base, qhi=qhi: g.tensor_tensor(
                u[2].rearrange("p (h d) -> p h d", h=8), qhi,
                bc_free3(tab, base + 32, 8, 32), ALU.mult),
                reads=["qn", tkey], writes=["u2", "rotc"])
            em.op("G", lambda g, tab=tab, base=base, qlo=qlo: g.tensor_tensor(
                u[3].rearrange("p (h d) -> p h d", h=8), qlo,
                bc_free3(tab, base + 96, 8, 32), ALU.mult),
                reads=["qn", tkey], writes=["u3", "rotc"])
            em.op("V", lambda v: v.tensor_tensor(
                view3(qrot[:], 32, 64, 8, 32), u[2].rearrange("p (h d) -> p h d", h=8),
                u[3].rearrange("p (h d) -> p h d", h=8), ALU.add),
                reads=["u2", "u3"], writes=["qrot"])
            for j in range(4):
                em.op("T", lambda t, j=j, toff=toff, TPK=TPK: t.matmul(
                    TPK[:, toff + j * 128: toff + (j + 1) * 128],
                    qrot[:, j * 128:(j + 1) * 128], self.identb[:],
                    start=(j == 0), stop=(j == 3),
                    is_transpose=True, skip_group_check=True),
                    reads=["qrot", "identb"], writes=[tpk])
            ddst = view3(dst[:], dst_off, dst_stride, 4, 128)
            em.op("S", lambda s, ddst=ddst, toff=toff, TPK=TPK: s.copy(
                ddst, TPK[:, toff:toff + 512]),
                reads=[tpk], writes=[dkey])
            if i == 0:
                lo_sb = qn.ap().bitcast(BF16)[:, 0:512]
                em.op("V", lambda v: v.tensor_tensor(
                    view3(sqs[:], 0, 64, 8, 32), u[0].rearrange("p (h d) -> p h d", h=8),
                    u[1].rearrange("p (h d) -> p h d", h=8), ALU.subtract),
                    reads=["u0", "u1"], writes=["sqs"])
                em.op("V", lambda v: v.tensor_tensor(
                    view3(sqs[:], 32, 64, 8, 32), u[2].rearrange("p (h d) -> p h d", h=8),
                    u[3].rearrange("p (h d) -> p h d", h=8), ALU.add),
                    reads=["u2", "u3"], writes=["sqs"])
                em.op("V", lambda v: v.tensor_tensor(
                    sqs[:], sqs[:], qrot[:], ALU.subtract),
                    reads=["sqs", "qrot"], writes=["sqs"])
                em.op("S", lambda s, lo_sb=lo_sb: s.copy(lo_sb, sqs[:]),
                      reads=["sqs"], writes=["qn"])
                lodst = qlo_fm if which == "q" else klo_fm
                lokey = "qlofm" if which == "q" else "klofm"
                for j in range(4):
                    em.op("T", lambda t, j=j, toff=toff, TPK=TPK, lo_sb=lo_sb: t.matmul(
                        TPK[:, toff + j * 128: toff + (j + 1) * 128],
                        lo_sb[:, j * 128:(j + 1) * 128], self.identb[:],
                        start=(j == 0), stop=(j == 3),
                        is_transpose=True, skip_group_check=True),
                        reads=["qn", "identb"], writes=[tpk])
                lddst = view3(lodst[:], 0, 128, 4, 128)
                em.op("S", lambda s, lddst=lddst, toff=toff, TPK=TPK: s.copy(
                    lddst, TPK[:, toff:toff + 512]),
                    reads=[tpk], writes=[lokey])

        prevA = [None]

        def flushA():
            if prevA[0] is not None:
                prevA[0]()
                prevA[0] = None

        for pi, part in enumerate(("k", "v", "q")):
            off = {"q": 0, "k": 512, "v": 1024}[part]
            load_part(off, f"ldw{part}")
            for i in order:
                if pi == 0:
                    for c in range(8):
                        sc0 = (1 + i * 128) if c < 4 else (0 + i * 128)
                        em.op("G", lambda g, c=c, i=i, sc0=sc0: g.tensor_copy(
                            self.hc(c, 1 + i * 128, 128), self.xc(c, sc0, 128)),
                            reads=self.xkeys(c, sc0 if c >= 4 else 1), writes=[f"h{c}"],
                            tag=f"hcast{i}_{c}")
                ppk = [["pA", "pB"], ["pC", "pD"], ["pE", "pF"]][pi][i % 2]
                PP = self.PS[ppk]
                for kc in range(8):
                    em.op("T", lambda t, kc=kc, i=i, PP=PP: t.matmul(
                        PP[:], self.hc(kc, 1 + i * 128, 128),
                        wpart[:, kc * 512:(kc + 1) * 512],
                        start=(kc == 0), stop=(kc == 7)),
                        reads=[f"h{kc}", "wpart"], writes=[ppk])
                flushA()
                if part == "v":
                    em.op("S", lambda s, i=i, PP=PP: s.copy(
                        v_tm[:, (4 + i) * 512:(5 + i) * 512], PP[:]),
                        reads=[ppk], writes=[f"vt{4 + i}"])
                elif part == "k":
                    tpk = ["pT1", "pT2"][i % 2]
                    prevA[0] = (lambda i=i, PP=PP, ppk=ppk, tpk=tpk: qk_post(
                        i, "k", PP, ppk, tabk, "tabk", k_fm, f"kw{i // 4}",
                        512 + i * 128, 2560, tpk, self.PS[tpk], 512))
                else:
                    tpk = ["pT1", "pT2"][i % 2]
                    prevA[0] = (lambda i=i, PP=PP, ppk=ppk, tpk=tpk: qk_post(
                        i, "q", PP, ppk, tabq, "tabq", q_fm, f"qw{i // 4}",
                        i * 128, 2048, tpk, self.PS[tpk], 0))
                if part == "v" and i == 15:
                    for c in range(4):
                        em.dma(lambda q, c=c: q.dma_start(
                            out=self.d_agkv_in[l][c * 128:(c + 1) * 128, :],
                            in_=k_fm[:, c * 2560 + 512 + 1536: c * 2560 + 512 + 2048]),
                            reads=["kw3"], writes=["d_agkv_in"], tag=f"kvout{c}")
                    for j in range(4):
                        em.dma(lambda q, j=j: q.dma_start(
                            out=self.d_agkv_in[l][512 + j * 128: 512 + (j + 1) * 128, :],
                            in_=v_tm[:, (16 + j) * 512:(17 + j) * 512]),
                            reads=[f"vt{16 + j}"], writes=["d_agkv_in"], tag=f"vvout{j}")
                    em.coll(lambda g, l=l: g.collective_compute(
                        "AllGather", ALU.bypass, replica_groups=AG,
                        ins=[self.d_agkv_in[l]], outs=[self.d_agkv_out[l]]),
                        reads=["d_agkv_in"], writes=["d_agkv_out"], tag=f"AGkv{l}")
                    for c in range(4):
                        em.dma(lambda q, c=c, l=l: q.dma_start(
                            out=k_fm[:, c * 2560: c * 2560 + 512],
                            in_=self.d_agkv_out[l][bass.ts(self.rt["rr"] * 8 + c, 128), :]),
                            reads=["d_agkv_out"], writes=["kwh"], tag=f"kvin{c}")
                    for j in range(4):
                        em.dma(lambda q, j=j, l=l: q.dma_start(
                            out=v_tm[:, j * 512:(j + 1) * 512],
                            in_=self.d_agkv_out[l][bass.ts(self.rt["rr"] * 8 + 4 + j, 128), :]),
                            reads=["d_agkv_out"], writes=[f"vt{j}"], tag=f"vvin{j}")

        flushA()
        if f"l{l}_qfm" in self.taps:
            self.tap_sbuf(f"l{l}_qfm", q_fm, BF16, [f"qw{w}" for w in range(4)])
            self.tap_sbuf(f"l{l}_kfm", k_fm, BF16, [f"kw{w}" for w in range(4)] + ["kwh"])
            self.tap_sbuf(f"l{l}_vtm", v_tm, BF16, [f"vt{j}" for j in range(20)])
        em.barrier()
        scA.close()

        scB = ExitStack()
        o_fm = scB.enter_context(nc.sbuf_tensor(f"o_fm{l}", [128, 4 * 2048], BF16))
        wo = scB.enter_context(nc.sbuf_tensor(f"wo{l}", [128, 4 * 1024], BF16))
        stgW = scB.enter_context(nc.sbuf_tensor(f"stgW{l}", [128, 1024], F32))
        p_sb = [scB.enter_context(nc.sbuf_tensor(f"p_sb{l}_{i}", [128, 640], BF16)) for i in range(2)]
        pt_sb = [scB.enter_context(nc.sbuf_tensor(f"pt_sb{l}_{i}", [128, 640], BF16)) for i in range(2)]
        o_tmp = [scB.enter_context(nc.sbuf_tensor(f"o_tmp{l}_{i}", [128, 128], BF16)) for i in range(2)]
        nmt = scB.enter_context(nc.sbuf_tensor(f"nmt{l}", [128, 8], F32))
        rrt = scB.enter_context(nc.sbuf_tensor(f"rrt{l}", [128, 8], F32))

        hpstate = {}
        prev_tail = [None]

        def flush_tail():
            if prev_tail[0] is not None:
                prev_tail[0]()
                prev_tail[0] = None

        for w in range(4):
            for s in range(4):
                pw_ = (4 - s) * 128
                cw = (s + 1) * 128
                for hp in range(4):
                    pok = ["pE", "pF"][hp % 2]
                    PO = self.PS[pok]
                    hpstate[(w, s, hp)] = {}
                    for hh in range(2):
                        h = 2 * hp + hh
                        up = self.par()
                        psb = p_sb[up]
                        psbk = f"psb{up}"
                        ptsb = pt_sb[up]
                        ptsbk = f"ptsb{up}"
                        tpk = ["pT1", "pT2"][up]
                        TPK = self.PS[tpk]
                        rrk = f"rrt{up}"
                        nmk = f"nmt{up}"
                        nmb = 4 * up
                        rb = 4 * up + 2 * hh
                        sA = ["pA", "pC"][up]
                        sB = ["pB", "pD"][up]
                        FULL = self.PS[["pABfull", "pCDfull"][up]]
                        st = hpstate[(w, s, hp)]
                        st[hh] = (rb, rrk, ptsb, ptsbk, TPK, tpk)
                        qsl = q_fm[hh * 64:(hh + 1) * 64,
                                   hp * 2048 + w * 512 + s * 128: hp * 2048 + w * 512 + (s + 1) * 128]
                        em.op("T", lambda t, qsl=qsl, hp=hp, hh=hh, w=w, s=s, pw_=pw_, FULL=FULL:
                              t.matmul(FULL[:, :pw_], qsl,
                                       k_fm[hh * 64:(hh + 1) * 64,
                                            hp * 2560 + w * 512 + s * 128:
                                            hp * 2560 + w * 512 + 512],
                                       start=True, stop=False, skip_group_check=True),
                              reads=[f"qw{w}", ("kwh" if w == 0 else f"kw{w - 1}")],
                              writes=[sA], tag=f"simP{w}{s}{h}")
                        if s > 0:
                            em.op("T", lambda t, qsl=qsl, hp=hp, hh=hh, w=w, s=s, pw_=pw_, FULL=FULL:
                                  t.matmul(FULL[:, pw_:512], qsl,
                                           k_fm[hh * 64:(hh + 1) * 64,
                                                hp * 2560 + 512 + w * 512:
                                                hp * 2560 + 512 + w * 512 + s * 128],
                                           start=False, stop=False, skip_group_check=True),
                                  reads=[f"qw{w}", f"kw{w}"], writes=[sA], tag=f"simC1_{w}{s}{h}")
                        em.op("T", lambda t, FULL=FULL, w=w: t.matmul(
                            FULL[:, 0:128], self.identb[:], self.maskl[:],
                            start=False, stop=(w != 0), skip_group_check=True),
                            reads=["identb", "maskl"], writes=[sA])
                        if w == 0:
                            em.op("T", lambda t, FULL=FULL, pw_=pw_: t.matmul(
                                FULL[:, 0:pw_], self.identb[:], self.maskpb[:, :pw_],
                                start=False, stop=True, skip_group_check=True),
                                reads=["identb", "maskpb"], writes=[sA])
                        hilo = (w == 0 and s == 0)
                        em.op("T", lambda t, qsl=qsl, hp=hp, hh=hh, w=w, s=s, FULL=FULL:
                              t.matmul(FULL[:, 512:640], qsl,
                                       k_fm[hh * 64:(hh + 1) * 64,
                                            hp * 2560 + 512 + w * 512 + s * 128:
                                            hp * 2560 + 512 + w * 512 + (s + 1) * 128],
                                       start=True, stop=False, skip_group_check=True),
                              reads=[f"qw{w}", f"kw{w}"], writes=[sB], tag=f"simC2_{w}{s}{h}")
                        if hilo:
                            em.op("T", lambda t, qsl=qsl, hp=hp, hh=hh, FULL=FULL:
                                  t.matmul(FULL[:, 512:640], qsl,
                                           klo_fm[hh * 64:(hh + 1) * 64,
                                                  hp * 128: (hp + 1) * 128],
                                           start=False, stop=False, skip_group_check=True),
                                  reads=[f"qw0", "klofm"], writes=[sB])
                            em.op("T", lambda t, hp=hp, hh=hh, FULL=FULL:
                                  t.matmul(FULL[:, 512:640],
                                           qlo_fm[hh * 64:(hh + 1) * 64,
                                                  hp * 128: (hp + 1) * 128],
                                           k_fm[hh * 64:(hh + 1) * 64,
                                                hp * 2560 + 512: hp * 2560 + 640],
                                           start=False, stop=False, skip_group_check=True),
                                  reads=["qlofm", f"kw0"], writes=[sB])
                        em.op("T", lambda t, FULL=FULL: t.matmul(
                            FULL[:, 512:640], self.identb[:], self.masku[:],
                            start=False, stop=True, skip_group_check=True),
                            reads=["identb", "masku"], writes=[sB])

                        flush_tail()

                        def tail(w=w, s=s, hp=hp, hh=hh, h=h, up=up, psb=psb,
                                 psbk=psbk, ptsb=ptsb, ptsbk=ptsbk, tpk=tpk,
                                 TPK=TPK, rrk=rrk, nmk=nmk, nmb=nmb, rb=rb,
                                 sA=sA, sB=sB, FULL=FULL, PO=PO, pok=pok):
                            em.op("V", lambda v: v.tensor_reduce(
                                nmt[:, nmb + hh: nmb + hh + 1], FULL[:, 0:640],
                                AX.X, ALU.max, negate=True),
                                reads=[sA, sB], writes=[nmk])
                            em.op("S", lambda s_: s_.activation(
                                psb[:, 0:640], FULL[:, 0:640], AF.Exp,
                                bias=nmt[:, nmb + hh: nmb + hh + 1],
                                accum_out=rrt[:, rb: rb + 1]),
                                reads=[sA, sB, nmk], writes=[psbk, rrk])
                            em.op("S", lambda s_: s_.activation(
                                rrt[:, rb + 1: rb + 2], rrt[:, rb: rb + 1], AF.Ln),
                                reads=[rrk], writes=[rrk])
                            em.op("S", lambda s_: s_.activation(
                                rrt[:, rb + 1: rb + 2], rrt[:, rb + 1: rb + 2],
                                AF.Exp, scale=-1.0),
                                reads=[rrk], writes=[rrk])
                            for j in range(5):
                                em.op("T", lambda t, j=j: t.matmul(
                                    TPK[:, j * 128:(j + 1) * 128],
                                    psb[:, j * 128:(j + 1) * 128],
                                    self.identb[:], start=(j == 0), stop=(j == 4),
                                    is_transpose=True, skip_group_check=True),
                                    reads=[psbk, "identb"], writes=[tpk])
                            em.op("V", lambda v: v.tensor_copy(ptsb[:], TPK[:, 0:640]),
                                  reads=[tpk], writes=[ptsbk])
                            for j in range(5):
                                vt = (4 * w + s + j) if j < (4 - s) else (4 + 4 * w + (j - (4 - s)))
                                em.op("T", lambda t, j=j, vt=vt: t.matmul(
                                    PO[:, hh * 64:(hh + 1) * 64],
                                    ptsb[:, j * 128:(j + 1) * 128],
                                    v_tm[:, vt * 512 + h * 64: vt * 512 + (h + 1) * 64],
                                    start=(hh == 0 and j == 0),
                                    stop=(hh == 1 and j == 4),
                                    skip_group_check=True),
                                    reads=[ptsbk, f"vt{vt}"], writes=[pok])
                            if hh == 1:
                                st2 = hpstate[(w, s, hp)]
                                ot = o_tmp[hp % 2]
                                otk = f"o_tmp{hp % 2}"
                                for h2 in range(2):
                                    rb2, rrk2, _, _, _, _ = st2[h2]
                                    rr_bc = bass.AP(rrt.ap().tensor, rb2 + 1,
                                                    [[rrt.ap().ap[0][0], 128], [0, 64]])
                                    em.op("V", lambda v, h2=h2, rr_bc=rr_bc: v.tensor_tensor(
                                        ot[:, h2 * 64:(h2 + 1) * 64],
                                        PO[:, h2 * 64:(h2 + 1) * 64],
                                        rr_bc, ALU.mult),
                                        reads=[pok, rrk2], writes=[otk])
                                _, _, _, _, TPK2, tpk2 = st2[1]
                                em.op("T", lambda t: t.matmul(
                                    TPK2[:, 768:896], ot[:], self.identb[:],
                                    start=True, stop=True,
                                    is_transpose=True, skip_group_check=True),
                                    reads=[otk, "identb"], writes=[tpk2])
                                em.op("S", lambda s_: s_.copy(
                                    o_fm[:, hp * 2048 + w * 512 + s * 128:
                                         hp * 2048 + w * 512 + (s + 1) * 128],
                                    TPK2[:, 768:896]),
                                    reads=[tpk2], writes=[f"ofm{w}"])
                                if os.environ.get("PBDBG") == f"{l},{w},{s},{hp}":
                                    dbg2 = self._pbdbg
                                    em.op("S", lambda s_: s_.copy(dbg2[:, 0:128], ot[:]),
                                          reads=[otk], writes=["pbdbg"])
                                    self.taps.add("pbdbg")
                                    self.tap_sbuf("pbdbg", dbg2, BF16, ["pbdbg"])

                        prev_tail[0] = tail
        flush_tail()
        if f"l{l}_ofm" in self.taps:
            self.tap_sbuf(f"l{l}_ofm", o_fm, BF16, [f"ofm{w}" for w in range(4)])

        for kc in range(4):
            em.dma(lambda q, kc=kc: q.dma_start(
                out=stgW[:], in_=self.in_wof[l][kc * 128:(kc + 1) * 128, :]),
                writes=["stgW"], tag=f"ldwo{kc}")
            em.op("G", lambda g, kc=kc: g.tensor_copy(
                wo[:, kc * 1024:(kc + 1) * 1024], stgW[:]),
                reads=["stgW"], writes=["wo"])
        for cb in range(4):
            for m in range(8):
                pwk = ["pA", "pB"][(cb * 8 + m) % 2]
                PW = self.PS[pwk]
                for kc in range(4):
                    em.op("T", lambda t, m=m, kc=kc, cb=cb, PW=PW: t.matmul(
                        PW[:], wo[:, kc * 1024 + m * 128: kc * 1024 + (m + 1) * 128],
                        o_fm[:, kc * 2048 + cb * 512:(kc) * 2048 + cb * 512 + 512],
                        start=(kc == 0), stop=(kc == 3)),
                        reads=["wo", f"ofm{cb}"], writes=[pwk])
                em.op("V", lambda v, m=m, cb=cb, PW=PW: v.tensor_tensor(
                    self.xc(m, 1 + cb * 512, 512), PW[:], self.xc(m, 1 + cb * 512, 512),
                    ALU.add),
                    reads=[pwk, f"x{m}"], writes=[f"x{m}"])
        if f"l{l}_xmid" in self.taps:
            self.tap_sbuf(f"l{l}_xmid", self.x, F32,
                          [f"x{c}" for c in range(8)] + [f"xh{c}" for c in range(8)])
        for c in range(8):
            em.dma(lambda q, c=c: q.dma_start(
                out=self.d_agxb_in[l][c * 128:(c + 1) * 128, 0:1],
                in_=self.xc(c, 2048, 1)),
                reads=[f"x{c}"], writes=["d_agxb_in"], tag=f"xbo{c}")
        em.coll(lambda g, l=l: g.collective_compute(
            "AllGather", ALU.bypass, replica_groups=AG,
            ins=[self.d_agxb_in[l]], outs=[self.d_agxb_out[l]]),
            reads=["d_agxb_in"], writes=["d_agxb_out"], tag=f"AGxb{l}")
        for c in range(8):
            em.dma(lambda q, c=c, l=l: q.dma_start(
                out=self.xc(c, 0, 1),
                in_=self.d_agxb_out[l][bass.ts(self.rt["rr"] * 8 + c, 128), 0:1]),
                reads=["d_agxb_out"], writes=[f"xh{c}"], tag=f"xbi{c}")
            em.op("S", lambda s_, c=c: s_.activation(
                self.xc(c, 0, 1), self.xc(c, 0, 1), AF.Identity,
                scale=self.gate[:]),
                reads=[f"xh{c}", "gate"], writes=[f"xh{c}"])
        em.barrier()
        scB.close()
        sc_outer.close()

        sc = ExitStack()
        hglu = sc.enter_context(nc.sbuf_tensor(f"hglu{l}", [128, NCA * 1024], BF16))
        w1a = [sc.enter_context(nc.sbuf_tensor(f"w1a{l}_{i}", [128, 1024], BF16)) for i in range(2)]
        w1g = [sc.enter_context(nc.sbuf_tensor(f"w1g{l}_{i}", [128, 1024], BF16)) for i in range(2)]
        stga = sc.enter_context(nc.sbuf_tensor(f"stga{l}", [128, 1024], F32))
        stgg = sc.enter_context(nc.sbuf_tensor(f"stgg{l}", [128, 1024], F32))
        w2m = [sc.enter_context(nc.sbuf_tensor(f"w2m{l}_{i}", [128, NCA * 128], BF16)) for i in range(2)]
        stg2 = sc.enter_context(nc.sbuf_tensor(f"stg2{l}", [128, NCA * 128], F32))
        glb = sc.enter_context(nc.sbuf_tensor(f"glb{l}", [128, 512], BF16))
        abf = sc.enter_context(nc.sbuf_tensor(f"abf{l}", [128, 512], F32))
        w2s = sc.enter_context(nc.sbuf_tensor(f"w2s{l}", [128, 512], F32))

        blocks4 = [(1, 512), (513, 512), (1025, 512), (1537, 512)]

        def apply_ln2(bi, col0, wdt):
            for c in range(8):
                sc0 = col0 - 1 if c >= 4 else col0
                em.op("V", lambda v, c=c, sc0=sc0, wdt=wdt: v.tensor_tensor(
                    self.lnt1[:, :wdt], self.xc(c, sc0, wdt), self.pE[:, :wdt], ALU.mult),
                    reads=self.xkeys(c, sc0) + ["pE"], writes=["lnt1"])
                em.op("V", lambda v, wdt=wdt: v.tensor_tensor(
                    self.lnt1[:, :wdt], self.lnt1[:, :wdt], self.pF[:, :wdt], ALU.add),
                    reads=["lnt1", "pF"], writes=["lnt1"])
                em.op("S", lambda s, c=c, col0=col0, wdt=wdt, l=l: s.activation(
                    self.hc(c, col0, wdt), self.lnt1[:, :wdt], AF.Identity,
                    scale=self.lgt[l][:, c:c + 1], bias=self.lbt[l][:, c:c + 1]),
                    reads=["lnt1", f"lgt{l}", f"lbt{l}"], writes=[f"h{c}"])

        self._ln_stats_blocks(True, blocks4, True, f"ln2_{l}", apply_fn=apply_ln2)
        if f"l{l}_ln2" in self.taps:
            self.tap_sbuf(f"l{l}_ln2", self.hbuf, BF16, [f"h{c}" for c in range(8)])

        for H in range(2):
            for ca in range(NCA):
                p = ca % 2
                em.dma(lambda q, ca=ca, l=l: q.dma_start(
                    out=stga[:].rearrange("p (kc n) -> p kc n", n=128),
                    in_=self.in_w1a[l][:, ca * 128:(ca + 1) * 128].rearrange(
                        "(kc p) n -> p kc n", p=128)),
                    writes=["stga"], tag=f"ldw1a{ca}")
                em.op("G", lambda g, p=p: g.tensor_copy(w1a[p][:], stga[:]),
                      reads=["stga"], writes=[f"w1a{p}"])
                em.dma(lambda q, ca=ca, l=l: q.dma_start(
                    out=stgg[:].rearrange("p (kc n) -> p kc n", n=128),
                    in_=self.in_w1g[l][:, ca * 128:(ca + 1) * 128].rearrange(
                        "(kc p) n -> p kc n", p=128)),
                    writes=["stgg"], tag=f"ldw1g{ca}")
                em.op("G", lambda g, p=p: g.tensor_copy(w1g[p][:], stgg[:]),
                      reads=["stgg"], writes=[f"w1g{p}"])
                for tb in range(2):
                    pak = ["pA", "pB"][tb]
                    pgk = ["pC", "pD"][tb]
                    PA_, PG_ = self.PS[pak], self.PS[pgk]
                    c0 = 1 + H * 1024 + tb * 512
                    for kc in range(8):
                        em.op("T", lambda t, kc=kc, p=p, c0=c0, PA_=PA_: t.matmul(
                            PA_[:], w1a[p][:, kc * 128:(kc + 1) * 128],
                            self.hc(kc, c0, 512), start=(kc == 0), stop=(kc == 7)),
                            reads=[f"w1a{p}", f"h{kc}"], writes=[pak])
                        em.op("T", lambda t, kc=kc, p=p, c0=c0, PG_=PG_: t.matmul(
                            PG_[:], w1g[p][:, kc * 128:(kc + 1) * 128],
                            self.hc(kc, c0, 512), start=(kc == 0), stop=(kc == 7)),
                            reads=[f"w1g{p}", f"h{kc}"], writes=[pgk])
                    em.op("S", lambda s, ca=ca, l=l, PG_=PG_: s.activation(
                        glb[:], PG_[:], AF.Gelu, bias=self.b1gt[l][:, ca:ca + 1]),
                        reads=[pgk, f"b1gt{l}"], writes=["glb"])
                    em.op("S", lambda s, ca=ca, l=l, PA_=PA_: s.activation(
                        abf[:], PA_[:], AF.Identity, bias=self.b1at[l][:, ca:ca + 1]),
                        reads=[pak, f"b1at{l}"], writes=["abf"])
                    em.op("V", lambda v, ca=ca, tb=tb: v.tensor_tensor(
                        hglu[:, ca * 1024 + tb * 512: ca * 1024 + (tb + 1) * 512],
                        abf[:], glb[:], ALU.mult),
                        reads=["abf", "glb"], writes=[f"hg{ca}"])
            for m in range(8):
                p = m % 2
                em.dma(lambda q, m=m, l=l: q.dma_start(
                    out=stg2[:].rearrange("p (kc n) -> p kc n", n=128),
                    in_=self.in_w2[l][:, m * 128:(m + 1) * 128].rearrange(
                        "(kc p) n -> p kc n", p=128)),
                    writes=["stg2"], tag=f"ldw2_{m}")
                em.op("G", lambda g, p=p: g.tensor_copy(w2m[p][:], stg2[:]),
                      reads=["stg2"], writes=[f"w2m{p}"])
                for tb in range(2):
                    pwk = ["pE", "pF"][tb]
                    PW = self.PS[pwk]
                    for ca in range(NCA):
                        em.op("T", lambda t, ca=ca, tb=tb, p=p, PW=PW: t.matmul(
                            PW[:], w2m[p][:, ca * 128:(ca + 1) * 128],
                            hglu[:, ca * 1024 + tb * 512: ca * 1024 + (tb + 1) * 512],
                            start=(ca == 0), stop=(ca == NCA - 1)),
                            reads=[f"w2m{p}", f"hg{ca}"], writes=[pwk])
                    em.op("S", lambda s, m=m, l=l, PW=PW: s.activation(
                        w2s[:], PW[:], AF.Identity, bias=self.b2t[l][:, m:m + 1]),
                        reads=[pwk, f"b2t{l}"], writes=["w2s"])
                    c0 = 1 + H * 1024 + tb * 512
                    em.op("V", lambda v, m=m, c0=c0: v.tensor_tensor(
                        self.xc(m, c0, 512), w2s[:], self.xc(m, c0, 512), ALU.add),
                        reads=["w2s", f"x{m}"], writes=[f"x{m}"])
        if f"l{l}_out" in self.taps:
            self.tap_sbuf(f"l{l}_out", self.x, F32,
                          [f"x{c}" for c in range(8)] + [f"xh{c}" for c in range(8)])
        em.barrier()
        sc.close()

    def final_ln(self):
        nc, em, es = self.nc, self.em, self.es
        sc = ExitStack()
        xr = sc.enter_context(nc.sbuf_tensor("xr_all", [128, 8 * 2048], F32))
        otm = [sc.enter_context(nc.sbuf_tensor(f"otm{i}", [128, 1024], F32)) for i in range(2)]
        for c in range(8):
            em.dma(lambda q, c=c: q.dma_start(out=xr[:, c * 2048:(c + 1) * 2048],
                                              in_=self.d_xres1[c]),
                   reads=["d_xres1"], writes=["xr"], tag=f"xr{c}")
        blocks4 = [(1, 512), (513, 512), (1025, 512), (1537, 512)]

        def apply_lnf(bi, col0, wdt):
            for c in range(8):
                em.op("V", lambda v, c=c, col0=col0, wdt=wdt: v.tensor_tensor(
                    self.lnt1[:, :wdt], self.xc(c, col0, wdt), self.pE[:, :wdt], ALU.mult),
                    reads=[f"x{c}", "pE"], writes=["lnt1"])
                em.op("V", lambda v, wdt=wdt: v.tensor_tensor(
                    self.lnt1[:, :wdt], self.lnt1[:, :wdt], self.pF[:, :wdt], ALU.add),
                    reads=["lnt1", "pF"], writes=["lnt1"])
                em.op("V", lambda v, c=c, col0=col0, wdt=wdt: v.tensor_tensor(
                    self.xc(c, col0, wdt), self.lnt1[:, :wdt],
                    xr[:, c * 2048 + col0 - 1: c * 2048 + col0 - 1 + wdt], ALU.add),
                    reads=["lnt1", "xr"], writes=[f"x{c}"])

        self._ln_stats_blocks(True, blocks4, None, "lnf", apply_fn=apply_lnf)
        for i in range(16):
            p = i % 2
            for half in range(2):
                pk = ["pA", "pB"][half]
                PK = self.PS[pk]
                for j in range(4):
                    c = half * 4 + j
                    em.op("T", lambda t, c=c, j=j, i=i, PK=PK: t.matmul(
                        PK[:, j * 128:(j + 1) * 128], self.xc(c, 1 + i * 128, 128),
                        self.identf[:], start=(j == 0), stop=(j == 3),
                        is_transpose=True, skip_group_check=True),
                        reads=[f"x{c}", "identf"], writes=[pk])
                em.op("V", lambda v, p=p, half=half, PK=PK: v.tensor_copy(
                    otm[p][:, half * 512:(half + 1) * 512], PK[:]),
                    reads=[pk], writes=[f"otm{p}"])
            em.dma(lambda q, i=i, p=p: q.dma_start(
                out=self.out[i * 128:(i + 1) * 128, :], in_=otm[p][:]),
                reads=[f"otm{p}"], writes=["out"], tag=f"st{i}")
        sc.close()

    def build(self):
        self.declare()
        self.load_consts()
        self.step0()
        self.load_biases()
        if self.stop == "step0":
            return self.em.finalize(self.es, list(self.tap_names))
        self.attn_block()
        if self.stop == "attn":
            return self.em.finalize(self.es, list(self.tap_names))
        for l in range(2):
            self.layer(l)
            if self.stop == f"l{l}":
                return self.em.finalize(self.es, list(self.tap_names))
        self.final_ln()
        self.em.finalize(self.es, ["out"] + list(self.tap_names))


QK_SCALE = 8.0


def build_program(taps=(), stop="full"):
    nc = bass.Bass("TRN2", target_bir_lowering=False)
    es = ExitStack()
    prog = Prog(nc, es, taps, stop)
    prog.build()
    return nc, es, prog



def make_core_inputs(inputs, c):
    b, qd = divmod(c, 4)
    t0 = qd * T
    g = lambda k: np.asarray(inputs[k], np.float32)
    x = np.ascontiguousarray(g("L0_attn_x")[b, t0:t0 + T])
    xh = (np.zeros((1, D), np.float32) if t0 == 0
          else np.ascontiguousarray(g("L0_attn_x")[b, t0 - 1].reshape(1, D)))
    gate = (np.ones if t0 > 0 else np.zeros)((128, 1), np.float32)
    maskp = (np.zeros((128, 512), BF16NP) if t0 > 0
             else np.full((128, 512), NEGBIG, BF16NP))
    tpos = np.arange(t0, t0 + T, dtype=np.float64)
    half_idx = np.arange(0, DH, 2, dtype=np.float64)
    inv_freq = 1.0 / 10000.0 ** (half_idx / DH)
    freqs = tpos[:, None] * inv_freq[None, :]
    base = (half_idx + 0.4 * DH) / (1.4 * DH)
    power = (tpos - N // 2) / float(WINDOW // 2)
    scale = base[None, :] ** power[:, None]
    cosf, sinf = np.cos(freqs), np.sin(freqs)
    rotc = np.concatenate([cosf * scale, sinf * scale, cosf / scale, sinf / scale],
                          axis=1).astype(np.float32)
    d = dict(x=x, xh=xh, gate=gate, maskp=maskp, rotc=rotc)
    for w in ("Wq", "Wk", "Wv", "Wo"):
        d[w] = g("L0_attn_" + w)
    for w in ("bq", "bk", "bv", "bo"):
        d[w] = g("L0_attn_" + w).reshape(8, 128)
    for l in range(2):
        d[f"wqkv{l}"] = np.ascontiguousarray(g("L0_ffn_wqkv")[l])
        d[f"qs{l}"] = g("L0_ffn_q_scale")[l].reshape(1, DH)
        d[f"ks{l}"] = g("L0_ffn_k_scale")[l].reshape(1, DH)
        d[f"wof{l}"] = np.ascontiguousarray(g("L0_ffn_wo")[l])
        d[f"lg{l}"] = g("L0_ffn_ln_g")[l].reshape(8, 128)
        d[f"lb{l}"] = g("L0_ffn_ln_b")[l].reshape(8, 128)
        w1 = g("L0_ffn_w1")[l]
        w1a = np.zeros((D, HIDP), np.float32); w1a[:, :HID] = w1[:, :HID]
        w1g = np.zeros((D, HIDP), np.float32); w1g[:, :HID] = w1[:, HID:]
        d[f"w1a{l}"] = w1a
        d[f"w1g{l}"] = w1g
        b1 = g("L0_ffn_b1")[l]
        b1a = np.zeros((NCA * 128,), np.float32); b1a[:HID] = b1[:HID]
        b1g = np.zeros((NCA * 128,), np.float32); b1g[:HID] = b1[HID:]
        d[f"b1a{l}"] = b1a.reshape(NCA, 128)
        d[f"b1g{l}"] = b1g.reshape(NCA, 128)
        w2 = np.zeros((HIDP, D), np.float32); w2[:HID] = g("L0_ffn_w2")[l]
        d[f"w2{l}"] = w2
        d[f"b2{l}"] = g("L0_ffn_b2")[l].reshape(8, 128)
    return {k: np.ascontiguousarray(v if k == "maskp" else np.asarray(v, np.float32))
            for k, v in d.items()}


def kernel(**inputs):
    taps = tuple(t for t in os.environ.get("KTAPS", "").split(",") if t)
    stop = os.environ.get("KSTOP", "full")
    nc, es, prog = build_program(taps, stop)
    with es:
        in_maps = [make_core_inputs(inputs, c) for c in range(NCORES)]
        res = run_bass_kernel_spmd(nc, in_maps, core_ids=list(range(NCORES)))
    if taps or stop != "full":
        kernel.tap_results = res.results
        kernel.tap_names = prog.tap_names
    if stop != "full":
        return np.zeros((B, N, D), np.float32)
    outs = [res.results[c]["out"] for c in range(NCORES)]
    out = np.empty((B, N, D), np.float32)
    for c in range(NCORES):
        b, qd = divmod(c, 4)
        out[b, qd * T:(qd + 1) * T] = outs[c]
    return out
